# version 08
# baseline (speedup 1.0000x reference)
import numpy as np
import ml_dtypes
import concourse.bass as bass
import concourse.mybir as mybir

F32 = mybir.dt.float32
BF16 = mybir.dt.bfloat16
AF = mybir.ActivationFunctionType
ALU = mybir.AluOpType
AX = mybir.AxisListType
BF16NP = ml_dtypes.bfloat16

ENGS = ("pe", "act", "dve", "pool", "sp")
_ENG_ATTR = {"pe": "tensor", "act": "scalar", "dve": "vector", "pool": "gpsimd", "sp": "sync"}


def patch_fast_compile():
    import concourse.bass_utils as bu
    if getattr(bu, "_mk_patched", False):
        return
    orig = bu.run_command
    def run2(argv, **kw):
        argv = ["--enable-birsim=false" if c == "--enable-birsim=true" else c for c in argv]
        return orig(argv, **kw)
    bu.run_command = run2
    bu._mk_patched = True


class Buf:
    __slots__ = ("name", "writers", "readers")
    def __init__(self, name):
        self.name = name
        self.writers = []
        self.readers = []


class PsumBank:
    def __init__(self, nc, i):
        self.t = nc.alloc_psum_tensor(f"psb{i}", [128, 512], F32)
        self.buf = Buf(f"psb{i}")
    @property
    def f32(self):
        return self.t[:, :]
    @property
    def bf16(self):
        return self.t[:, :].bitcast(BF16)


class Prog:
    def __init__(self, nc):
        self.nc = nc
        self.plan = {e: [] for e in ENGS}
        self.sems = {e: nc.alloc_semaphore(f"sem_{e}") for e in ENGS}
        self.carriers = {e: [] for e in ENGS}
        self.waited = {e: {} for e in ENGS}
        self.last_drain = {e: -1 for e in ENGS}
        self.streams = {}
        self.free_dma_sems = []
        self.free_eng_sems = []
        self.sem_base = {e: 0 for e in ENGS}
        self.banks = [PsumBank(nc, i) for i in range(8)]
        self._bank_free = list(range(8))
        self.sbuf_base = 17408
        self.sbuf_limit = 229344
        self._off = self.sbuf_base
        self._uid = 0

    def uid(self):
        self._uid += 1
        return self._uid

    def salloc(self, name, shape, dtype, align=64):
        size = shape[1] * (2 if dtype == BF16 else 4)
        off = (self._off + align - 1) // align * align
        assert off + size <= self.sbuf_limit, f"SBUF overflow: {name} at {off}+{size}"
        t = self.nc.alloc_sbuf_tensor_at(f"{name}_{self.uid()}", shape, dtype, offset=off)
        self._off = off + size
        return t

    def mark(self):
        return self._off

    def release(self, m):
        self._off = m

    def bank(self):
        assert self._bank_free, "no free psum banks"
        return self.banks[self._bank_free.pop(0)]

    def bank_free(self, *bs):
        for b in bs:
            self._bank_free.append(self.banks.index(b))

    def _edge(self, eng, dep):
        kind = dep[0]
        if kind == "E":
            _, e2, idx = dep
            if e2 == eng:
                if eng in ("pe", "sp"):
                    return
                if idx >= self.last_drain[eng]:
                    self.plan[eng].append({"k": "drain"})
                    self.last_drain[eng] = len(self.plan[eng]) - 1
                return
            car = self.carriers[e2]
            lo, hi = 0, len(car)
            while lo < hi:
                mid = (lo + hi) // 2
                if car[mid][0] >= idx:
                    hi = mid
                else:
                    lo = mid + 1
            if lo < len(car):
                val = car[lo][1]
                sem = self.sems[e2]
                key = ("e", e2)
            else:
                ent = self.plan[e2][idx]
                assert ent["k"] == "op" and not ent.get("dma"), f"dep on non-op {ent}"
                if ent.get("car") is not None:
                    sem, val = ent["car"]
                    key = ("sx", sem.num)
                else:
                    val = (car[-1][1] + 1) if car else (self.sem_base[e2] + 1)
                    sem = self.sems[e2]
                    ent["incs"].append((sem, 1))
                    ent["car"] = (sem, val)
                    car.append((idx, val))
                    key = ("e", e2)
            if self.waited[eng].get(key, 0) >= val:
                return
            self.plan[eng].append({"k": "wait", "sem": sem, "val": val})
            self.waited[eng][key] = val
        else:
            _, stream, cnt = dep
            sem, total = self.streams[stream]
            cnt = total
            key = ("d", stream)
            if self.waited[eng].get(key, 0) >= cnt:
                return
            self.plan[eng].append({"k": "wait", "sem": sem, "val": 16 * cnt})
            self.waited[eng][key] = cnt

    def _deps_for(self, reads, writes, wpart):
        deps = []
        seen = set()
        def add(d):
            if d not in seen:
                seen.add(d)
                deps.append(d)
        for b in reads:
            for w in b.writers:
                add(w)
        for b in writes:
            for w in b.writers:
                add(w)
            for r in b.readers:
                add(r)
        for b in wpart:
            for w in b.writers:
                add(w)
            for r in b.readers:
                add(r)
        deps.sort(key=lambda d: (d[1], -(d[2] if d[0] == "E" else 0)) if d[0] == "E" else (d[1], 0))
        return deps

    def op(self, eng, fn, reads=(), writes=(), wpart=()):
        for d in self._deps_for(reads, writes, wpart):
            self._edge(eng, d)
        idx = len(self.plan[eng])
        self.plan[eng].append({"k": "op", "fn": fn, "incs": []})
        myid = ("E", eng, idx)
        for b in reads:
            b.readers.append(myid)
        for b in writes:
            b.writers = [myid]
            b.readers = []
        for b in wpart:
            b.writers.append(myid)
        return myid

    def dma(self, queue, fn, reads=(), writes=(), wpart=(), stream=None):
        assert stream is not None
        if stream not in self.streams:
            if self.free_dma_sems:
                self.streams[stream] = list(self.free_dma_sems.pop(0))
            else:
                self.streams[stream] = [self.nc.alloc_semaphore(f"ds_{stream}"), 0]
        st = self.streams[stream]
        for d in self._deps_for(reads, writes, wpart):
            self._edge(queue, d)
        self.plan[queue].append({"k": "op", "fn": fn, "incs": [(st[0], 16)], "dma": True})
        st[1] += 1
        myid = ("D", stream, st[1])
        for b in reads:
            b.readers.append(myid)
        for b in writes:
            b.writers = [myid]
            b.readers = []
        for b in wpart:
            b.writers.append(myid)
        return myid

    def new_epoch(self):
        self._uid += 1
        keep = {}
        for s, (sem, cnt) in self.streams.items():
            if s == "consts":
                keep[s] = [sem, cnt]
            else:
                self.free_dma_sems.append((sem, cnt))
        self.streams = keep
        for e in ENGS:
            last = self.carriers[e][-1][1] if self.carriers[e] else self.sem_base[e]
            self.free_eng_sems.append((self.sems[e], last))
        old = dict(self.sems)
        for e in ENGS:
            sem, base = self.free_eng_sems.pop(0)
            if sem is old[e]:
                self.free_eng_sems.append((sem, base))
                sem, base = self.free_eng_sems.pop(0)
            self.sems[e] = sem
            self.sem_base[e] = base
            self.carriers[e] = []
            self.last_drain[e] = -1
        self.waited = {e: {} for e in ENGS}
        for b in self.banks:
            b.buf.writers = []
            b.buf.readers = []

    def barrier(self):
        deps = []
        for e in ENGS:
            idx = len(self.plan[e]) - 1
            while idx >= 0 and (self.plan[e][idx]["k"] != "op" or self.plan[e][idx].get("dma")):
                idx -= 1
            if idx >= 0:
                deps.append(("E", e, idx))
        for s, (sem, cnt) in self.streams.items():
            if cnt:
                deps.append(("D", s, cnt))
        for e in ENGS:
            for d in deps:
                if d[0] == "E" and d[1] == e:
                    continue
                self._edge(e, d)

    def finalize(self):
        nc = self.nc
        with nc.Block() as block:
            for eng in ENGS:
                entries = self.plan[eng]
                def make_body(entries):
                    def body(h):
                        for ent in entries:
                            k = ent["k"]
                            if k == "wait":
                                h.wait_ge(ent["sem"], ent["val"])
                            elif k == "drain":
                                h.drain()
                            else:
                                inst = ent["fn"](h)
                                for sem, amt in ent["incs"]:
                                    inst.then_inc(sem, amt)
                    return body
                getattr(block, _ENG_ATTR[eng])(make_body(entries))



def ceil_div(a, b):
    return (a + b - 1) // b


class Consts:
    def __init__(self, nc):
        self.nc = nc
        self.arrs = {}
        self.aps = {}

    def add(self, name, arr, dtype):
        if name in self.aps:
            return self.aps[name]
        t = self.nc.dram_tensor(name, list(arr.shape), dtype, kind="ExternalInput")
        self.arrs[name] = arr
        self.aps[name] = t.ap()
        return self.aps[name]

    def ident_bf16(self):
        return self.add("c_ident_bf16", np.eye(128, dtype=BF16NP), BF16)

    def ident_sb32(self, P):
        if not hasattr(self, "_ident_sb32"):
            ap = self.add("c_ident_f32", np.eye(128, dtype=np.float32), F32)
            t = P.salloc("identsb32", [128, 128], F32)
            b = Buf("identsb32")
            P.dma("sp", lambda h: h.dma_start(out=t[:, :], in_=ap), writes=[b], stream="consts")
            self._ident_sb32 = (t, b)
        return self._ident_sb32

    def ident_sb(self, P):
        if not hasattr(self, "_ident_sb"):
            ap = self.ident_bf16()
            t = P.salloc("identsb", [128, 128], BF16)
            b = Buf("identsb")
            P.dma("sp", lambda h: h.dma_start(out=t[:, :], in_=ap), writes=[b], stream="consts")
            self._ident_sb = (t, b)
        return self._ident_sb


def load_const_tile(P, ap, shape, dtype, name, stream="consts"):
    t = P.salloc(name, shape, dtype)
    b = Buf(name)
    P.dma("sp", lambda h, t=t, ap=ap: h.dma_start(out=t[:, :], in_=ap), writes=[b], stream=stream)
    return t, b


def emit_ln(P, C, X, XN_tm, XN_fm, ntok, eps=1e-5, name="ln", XN_lo_fm=None, XN_tm32=None):
    m0 = P.mark()
    D = 1024
    ident, identb = C.ident_sb(P)
    ntiles = ntok // 128
    xt = [P.salloc(f"{name}_x{i}", [128, D], F32) for i in range(2)]
    xtb = [Buf(f"{name}_x{i}") for i in range(2)]
    xc = P.salloc(f"{name}_xc", [128, D], F32); xcb = Buf("xc")
    sq = P.salloc(f"{name}_sq", [128, D], F32); sqb = Buf("sq")
    stat = P.salloc(f"{name}_st", [128, 8], F32); statb = Buf("stat")
    xnb = P.salloc(f"{name}_xnb", [128, D], BF16); xnbb = Buf("xnb")
    fmc = [P.salloc(f"{name}_fm{c}", [128, ntok], BF16) for c in range(8)]
    fmb = [Buf(f"{name}_fm{c}") for c in range(8)]
    need32 = (XN_lo_fm is not None) or (XN_tm32 is not None)
    if need32:
        xn32 = P.salloc(f"{name}_xn32", [128, D], F32); xn32b = Buf("xn32")
    if XN_lo_fm is not None:
        xlo = P.salloc(f"{name}_xlo", [128, D], BF16); xlob = Buf("xlo")
        loc = [P.salloc(f"{name}_lo{c}", [128, ntok], BF16) for c in range(8)]
        lob = [Buf(f"{name}_lo{c}") for c in range(8)]

    for i in range(ntiles):
        sl = i % 2
        t0 = i * 128
        P.dma("sp", lambda h, i=i, sl=sl: h.dma_start(out=xt[sl][:, :], in_=X[i*128:(i+1)*128, :]),
              reads=[], writes=[xtb[sl]], stream=f"{name}x{sl}")
        P.op("dve", lambda h, sl=sl: h.reduce_sum(out=stat[:, 0:1], in_=xt[sl][:, :], axis=AX.X),
             reads=[xtb[sl]], writes=[statb])
        P.op("dve", lambda h: h.tensor_scalar(out=stat[:, 1:2], in0=stat[:, 0:1], scalar1=1.0/D, scalar2=None, op0=ALU.mult),
             reads=[statb], wpart=[statb])
        P.op("dve", lambda h, sl=sl: h.tensor_scalar(out=xc[:, :], in0=xt[sl][:, :], scalar1=stat[:, 1:2], scalar2=None, op0=ALU.subtract),
             reads=[xtb[sl], statb], writes=[xcb])
        P.op("act", lambda h: h.activation(out=sq[:, :], in_=xc[:, :], func=AF.Square),
             reads=[xcb], writes=[sqb])
        P.op("dve", lambda h: h.reduce_sum(out=stat[:, 2:3], in_=sq[:, :], axis=AX.X),
             reads=[sqb], wpart=[statb])
        P.op("dve", lambda h: h.tensor_scalar(out=stat[:, 3:4], in0=stat[:, 2:3], scalar1=1.0/D, scalar2=eps, op0=ALU.mult, op1=ALU.add),
             reads=[statb], wpart=[statb])
        P.op("act", lambda h: h.activation(out=stat[:, 4:5], in_=stat[:, 3:4], func=AF.Sqrt),
             reads=[statb], wpart=[statb])
        P.op("dve", lambda h: h.reciprocal(out=stat[:, 5:6], in_=stat[:, 4:5]),
             reads=[statb], wpart=[statb])
        if not need32:
            P.op("dve", lambda h: h.tensor_scalar(out=xnb[:, :], in0=xc[:, :], scalar1=stat[:, 5:6], scalar2=None, op0=ALU.mult),
                 reads=[xcb, statb], writes=[xnbb])
        else:
            P.op("dve", lambda h: h.tensor_scalar(out=xn32[:, :], in0=xc[:, :], scalar1=stat[:, 5:6], scalar2=None, op0=ALU.mult),
                 reads=[xcb, statb], writes=[xn32b])
            P.op("dve", lambda h: h.tensor_copy(out=xnb[:, :], in_=xn32[:, :]), reads=[xn32b], writes=[xnbb])
            if XN_lo_fm is not None:
                P.op("dve", lambda h: h.tensor_sub(out=xlo[:, :], in0=xn32[:, :], in1=xnb[:, :]), reads=[xn32b, xnbb], writes=[xlob])
            if XN_tm32 is not None:
                P.dma("act", lambda h, i=i: h.dma_start(out=XN_tm32[i*128:(i+1)*128, :], in_=xn32[:, :]),
                      reads=[xn32b], stream=f"{name}tm32")
        if XN_tm is not None:
            P.dma("act", lambda h, i=i: h.dma_start(out=XN_tm[i*128:(i+1)*128, :], in_=xnb[:, :]),
                  reads=[xnbb], stream=f"{name}tm")
        bank = P.bank()
        for c in range(8):
            P.op("pe", lambda h, c=c, bank=bank: h.transpose(out=bank.bf16[:, c*128:(c+1)*128], in_=xnb[:, c*128:(c+1)*128], identity=ident[:, :]),
                 reads=[xnbb, identb], wpart=[bank.buf])
        for c in range(8):
            P.op("dve", lambda h, c=c, t0=t0, bank=bank: h.tensor_copy(out=fmc[c][:, t0:t0+128], in_=bank.bf16[:, c*128:(c+1)*128]),
                 reads=[bank.buf], wpart=[fmb[c]])
        P.bank_free(bank)
        if XN_lo_fm is not None:
            bank = P.bank()
            for c in range(8):
                P.op("pe", lambda h, c=c, bank=bank: h.transpose(out=bank.bf16[:, c*128:(c+1)*128], in_=xlo[:, c*128:(c+1)*128], identity=ident[:, :]),
                     reads=[xlob, identb], wpart=[bank.buf])
            for c in range(8):
                P.op("dve", lambda h, c=c, t0=t0, bank=bank: h.tensor_copy(out=loc[c][:, t0:t0+128], in_=bank.bf16[:, c*128:(c+1)*128]),
                     reads=[bank.buf], wpart=[lob[c]])
            P.bank_free(bank)
    for c in range(8):
        P.dma("act", lambda h, c=c: h.dma_start(out=XN_fm[c*128:(c+1)*128, :], in_=fmc[c][:, :]),
              reads=[fmb[c]], stream=f"{name}fm")
    if XN_lo_fm is not None:
        for c in range(8):
            P.dma("act", lambda h, c=c: h.dma_start(out=XN_lo_fm[c*128:(c+1)*128, :], in_=loc[c][:, :]),
                  reads=[lob[c]], stream=f"{name}fm")
    P.barrier()
    P.release(m0)


def load_w_chunks(P, Wdram, K, N, name, dtype=BF16, stream="w"):
    nk = ceil_div(K, 128)
    tiles, bufs = [], []
    for kc in range(nk):
        rows = min(128, K - kc * 128)
        t = P.salloc(f"{name}{kc}", [128, N], dtype)
        b = Buf(f"{name}{kc}")
        P.dma("pool", lambda h, t=t, kc=kc, rows=rows: h.dma_start(out=t[0:rows, :], in_=Wdram[kc*128:kc*128+rows, :]),
              writes=[b], stream=stream)
        tiles.append(t); bufs.append(b)
    return tiles, bufs



def emit_attn(P, C, io, nheads, hd, causal, scale, nseq=2, T=2048, name="attn"):
    assert hd in (64, 1024)
    m0 = P.mark()
    D = 1024
    XN_fm = io["XN_fm"]; OUT = io["OUT"]
    ident, identb = C.ident_sb(P)
    nkT = T // 128
    nq4 = T // 512
    vw = nheads * (hd + 1)

    if causal:
        cm = np.zeros((128, 4*512), np.float32)
        for d in range(4):
            for kk in range(128):
                cm[kk, d*512:d*512 + d*128 + kk] = -30000.0
        cmap = C.add("c_causal512", cm, F32)
        cmt = P.salloc(f"{name}_cm", [128, 4*512], F32)
        cmb = Buf("cmt")
        P.dma("sp", lambda h: h.dma_start(out=cmt[:, :], in_=cmap), writes=[cmb], stream="consts")

    wsb = [P.salloc(f"{name}_w{kc}", [128, D], BF16) for kc in range(8)]
    wbufs = [Buf(f"{name}_w{kc}") for kc in range(8)]
    xfm = [P.salloc(f"{name}_xf{kc}", [128, T], BF16) for kc in range(8)]
    xfb = [Buf(f"{name}_xf{kc}") for kc in range(8)]
    qfm = [P.salloc(f"{name}_qf{kc}", [128, T], BF16) for kc in range(8)]
    qfb = [Buf(f"{name}_qf{kc}") for kc in range(8)]
    kfm = [P.salloc(f"{name}_kf{kc}", [128, T], BF16) for kc in range(8)]
    kfb = [Buf(f"{name}_kf{kc}") for kc in range(8)]
    vau = [P.salloc(f"{name}_va{t}", [128, vw], BF16) for t in range(nkT)]
    vab = [Buf(f"{name}_va{t}") for t in range(nkT)]
    ctx = [P.salloc(f"{name}_cx{t}", [128, D], BF16) for t in range(nkT)]
    ctxb = [Buf(f"{name}_cx{t}") for t in range(nkT)]
    if hd == 64:
        pt = [P.salloc(f"{name}_pt{i}", [128, 512], BF16) for i in range(2)]
        ptb = [Buf(f"{name}_pt{i}") for i in range(2)]
    else:
        pta = [P.salloc(f"{name}_pta{i}", [128, 512], BF16) for i in range(nkT)]
        ptab = [Buf(f"{name}_pta{i}") for i in range(nkT)]
    st = P.salloc(f"{name}_stat", [128, 8], F32)
    stb = Buf(f"{name}_stat")
    ob = [P.salloc(f"{name}_ob{i}", [128, 512], F32) for i in range(2)]
    obb = [Buf(f"{name}_ob{i}") for i in range(2)]
    cfm = xfm
    cfb = xfb
    borep = borepb = None
    if io.get("bo_rep") is not None:
        borep = P.salloc(f"{name}_bo", [128, D], F32); borepb = Buf("bo")
        P.dma("sp", lambda h: h.dma_start(out=borep[:, :], in_=io["bo_rep"]), writes=[borepb], stream="consts")
    bvrep = bvrepb = None
    if io.get("bv_rep") is not None:
        bvrep = P.salloc(f"{name}_bv", [128, D], F32); bvrepb = Buf("bv")
        P.dma("sp", lambda h: h.dma_start(out=bvrep[:, :], in_=io["bv_rep"]), writes=[bvrepb], stream="consts")
    bpart = {}
    for nm in ("bq", "bk"):
        if io.get(nm) is not None:
            bt = P.salloc(f"{name}_{nm}", [128, 8], F32); bb = Buf(nm)
            P.dma("sp", lambda h, bt=bt, nm=nm: h.dma_start(out=bt[:, :], in_=io[nm]), writes=[bb], stream="consts")
            bpart[nm] = (bt, bb)

    for s in range(nseq):
        toff = s * T
        for kc in range(8):
            P.dma("sp", lambda h, kc=kc, toff=toff: h.dma_start(out=xfm[kc][:, :], in_=XN_fm[kc*128:(kc+1)*128, toff:toff+T]),
                  writes=[xfb[kc]], stream=f"{name}xf")

        def loadw(Wt):
            for kc in range(8):
                P.dma("pool", lambda h, kc=kc, Wt=Wt: h.dma_start(out=wsb[kc][:, :], in_=Wt[kc*128:(kc+1)*128, :]),
                      writes=[wbufs[kc]], stream=f"{name}w")

        def proj_fm(bias, dst, dstb):
            for oc in range(8):
                for t4 in range(T // 512):
                    bank = P.bank()
                    for kc in range(8):
                        P.op("pe", lambda h, kc=kc, oc=oc, t4=t4, bank=bank: h.matmul(
                            out=bank.f32, lhsT=wsb[kc][:, oc*128:(oc+1)*128], rhs=xfm[kc][:, t4*512:(t4+1)*512],
                            start=(kc == 0), stop=(kc == 7)),
                            reads=[wbufs[kc], xfb[kc]], writes=[bank.buf] if kc == 0 else (), wpart=[bank.buf] if kc > 0 else ())
                    if bias is not None:
                        bt, bb = bpart[bias]
                        P.op("act", lambda h, oc=oc, t4=t4, bank=bank, bt=bt: h.activation(
                            out=dst[oc][:, t4*512:(t4+1)*512], in_=bank.f32, func=AF.Identity, bias=bt[:, oc:oc+1]),
                            reads=[bank.buf, bb], wpart=[dstb[oc]])
                    else:
                        P.op("act", lambda h, oc=oc, t4=t4, bank=bank: h.copy(
                            out=dst[oc][:, t4*512:(t4+1)*512], in_=bank.f32),
                            reads=[bank.buf], wpart=[dstb[oc]])
                    P.bank_free(bank)

        loadw(io["WqT"]); proj_fm("bq" if "bq" in bpart else None, qfm, qfb)
        loadw(io["WkT"]); proj_fm("bk" if "bk" in bpart else None, kfm, kfb)
        loadw(io["WvT"])
        for t in range(nkT):
            if nheads == 1:
                P.op("dve", lambda h, t=t: h.memset(vau[t][:, 1024:1025], 1.0), writes=[vab[t]])
            else:
                P.op("dve", lambda h, t=t: h.memset(vau[t][:, :], 1.0), writes=[vab[t]])
            for dc in range(2):
                bank = P.bank()
                for kc in range(8):
                    P.op("pe", lambda h, kc=kc, t=t, dc=dc, bank=bank: h.matmul(
                        out=bank.f32, lhsT=xfm[kc][:, t*128:(t+1)*128], rhs=wsb[kc][:, dc*512:(dc+1)*512],
                        start=(kc == 0), stop=(kc == 7)),
                        reads=[wbufs[kc], xfb[kc]], writes=[bank.buf] if kc == 0 else (), wpart=[bank.buf] if kc > 0 else ())
                def vevac(h, t=t, dc=dc, bank=bank):
                    if nheads == 1:
                        dst = vau[t][:, dc*512:(dc+1)*512]
                        if bvrep is not None:
                            return h.tensor_add(out=dst, in0=bank.f32, in1=bvrep[:, dc*512:(dc+1)*512])
                        return h.tensor_copy(out=dst, in_=bank.f32)
                    dstv = vau[t][:, :].rearrange("p (h e) -> p h e", e=hd+1)
                    src3 = bank.t[0:128, 0:512].rearrange("p (h e) -> p h e", e=hd)
                    nh0 = (dc * 512) // hd
                    nh1 = ((dc + 1) * 512) // hd
                    dst = dstv[:, nh0:nh1, 0:hd]
                    if bvrep is not None:
                        b3 = bvrep[:, dc*512:(dc+1)*512].rearrange("p (h e) -> p h e", e=hd)
                        return h.tensor_add(out=dst, in0=src3, in1=b3)
                    return h.tensor_copy(out=dst, in_=src3)
                P.op("dve", vevac, reads=[bank.buf] + ([bvrepb] if bvrep is not None else []), wpart=[vab[t]])
                P.bank_free(bank)

        if hd == 1024:
            for q4 in range(nq4):
                nk0 = nkT
                sbk = [P.bank(), P.bank()]
                for k0 in range(nk0):
                    sb = sbk[k0 % 2]
                    for kc in range(8):
                        P.op("pe", lambda h_, kc=kc, k0=k0, q4=q4, sb=sb: h_.matmul(
                            out=sb.f32, lhsT=kfm[kc][:, k0*128:(k0+1)*128], rhs=qfm[kc][:, q4*512:(q4+1)*512],
                            start=(kc == 0), stop=(kc == 7)),
                            reads=[kfb[kc], qfb[kc]], writes=[sb.buf] if kc == 0 else (), wpart=[sb.buf] if kc > 0 else ())
                    P.op("act", lambda h_, sb=sb, k0=k0: h_.activation(out=pta[k0][:, :], in_=sb.f32, func=AF.Exp, scale=scale),
                         reads=[sb.buf], writes=[ptab[k0]])
                P.bank_free(*sbk)
                for qq in range(4):
                    t = q4 * 4 + qq
                    lb = P.bank()
                    for k0 in range(nk0):
                        P.op("pe", lambda h_, qq=qq, k0=k0, lb=lb, first=(k0 == 0), last=(k0 == nk0-1): h_.matmul(
                            out=lb.t[0:128, 0:1], lhsT=pta[k0][:, qq*128:(qq+1)*128], rhs=vau[k0][:, 1024:1025],
                            start=first, stop=last),
                            reads=[ptab[k0], vab[k0]], writes=[lb.buf] if k0 == 0 else (), wpart=[lb.buf] if k0 > 0 else ())
                    P.op("dve", lambda h_, lb=lb: h_.tensor_copy(out=st[:, 0:1], in_=lb.t[0:128, 0:1]),
                         reads=[lb.buf], writes=[stb])
                    P.op("dve", lambda h_: h_.reciprocal(out=st[:, 1:2], in_=st[:, 0:1]), reads=[stb], wpart=[stb])
                    P.bank_free(lb)
                    for piece in range(2):
                        pb = P.bank()
                        for k0 in range(nk0):
                            P.op("pe", lambda h_, qq=qq, k0=k0, piece=piece, pb=pb, first=(k0 == 0), last=(k0 == nk0-1): h_.matmul(
                                out=pb.f32, lhsT=pta[k0][:, qq*128:(qq+1)*128], rhs=vau[k0][:, piece*512:(piece+1)*512],
                                start=first, stop=last),
                                reads=[ptab[k0], vab[k0]], writes=[pb.buf] if k0 == 0 else (), wpart=[pb.buf] if k0 > 0 else ())
                        P.op("dve", lambda h_, pb=pb, t=t, piece=piece: h_.tensor_scalar(
                            out=ctx[t][:, piece*512:(piece+1)*512], in0=pb.f32, scalar1=st[:, 1:2], scalar2=None, op0=ALU.mult),
                            reads=[pb.buf, stb], wpart=[ctxb[t]])
                        P.bank_free(pb)
        for h in (range(nheads) if hd == 64 else []):
            kc8 = (h * hd) // 128
            rb = (h * hd) % 128
            for q4 in range(nq4):
                nk0 = (q4 + 1) * 4 if causal else nkT
                pvb = [P.bank() for _ in range(4)]
                sbk = [P.bank(), P.bank()]
                def emit_S(k0, kc8=kc8, rb=rb, q4=q4):
                    sb = sbk[k0 % 2]
                    sl = k0 % 2
                    P.op("pe", lambda h_, kc8=kc8, rb=rb, k0=k0, q4=q4, sb=sb: h_.matmul(
                        out=sb.f32, lhsT=kfm[kc8][rb:rb+hd, k0*128:(k0+1)*128], rhs=qfm[kc8][rb:rb+hd, q4*512:(q4+1)*512],
                        start=True, stop=True),
                        reads=[kfb[kc8], qfb[kc8]], writes=[sb.buf])
                    if causal and k0 >= q4 * 4:
                        dl = k0 - q4 * 4
                        P.op("dve", lambda h_, dl=dl, sb=sb: h_.tensor_add(out=sb.f32, in0=sb.f32, in1=cmt[:, dl*512:(dl+1)*512]),
                             reads=[cmb], wpart=[sb.buf])
                    P.op("act", lambda h_, sb=sb, sl=sl: h_.activation(out=pt[sl][:, :], in_=sb.f32, func=AF.Exp, scale=scale),
                         reads=[sb.buf], writes=[ptb[sl]])
                emit_S(0)
                for k0 in range(nk0):
                    if k0 + 1 < nk0:
                        emit_S(k0 + 1)
                    sl = k0 % 2
                    first = (k0 == 0)
                    last = (k0 == nk0 - 1)
                    for qq in range(4):
                        bqq = pvb[qq]
                        P.op("pe", lambda h_, sl=sl, qq=qq, k0=k0, hh=h, first=first, last=last, b=bqq: h_.matmul(
                            out=b.t[0:128, 0:hd+1], lhsT=pt[sl][:, qq*128:(qq+1)*128], rhs=vau[k0][:, hh*(hd+1):(hh+1)*(hd+1)],
                            start=first, stop=last),
                            reads=[ptb[sl], vab[k0]], writes=[bqq.buf] if first else (), wpart=[bqq.buf] if not first else ())
                for qq in range(4):
                    t = q4 * 4 + qq
                    b = pvb[qq]
                    P.op("dve", lambda h_, b=b: h_.tensor_copy(out=st[:, 0:1], in_=b.t[0:128, hd:hd+1]),
                         reads=[b.buf], writes=[stb])
                    P.op("dve", lambda h_: h_.reciprocal(out=st[:, 1:2], in_=st[:, 0:1]),
                         reads=[stb], wpart=[stb])
                    P.op("dve", lambda h_, b=b, t=t, hh=h: h_.tensor_scalar(
                        out=ctx[t][:, hh*hd:(hh+1)*hd], in0=b.t[0:128, 0:hd], scalar1=st[:, 1:2], scalar2=None, op0=ALU.mult),
                        reads=[b.buf, stb], wpart=[ctxb[t]])
                P.bank_free(*pvb, *sbk)

        if io.get("WoT") is None:
            for t in range(nkT):
                for half in range(2):
                    sl = (t * 2 + half) % 2
                    P.op("dve", lambda h, t=t, sl=sl, half=half: h.tensor_copy(out=ob[sl][:, :], in_=ctx[t][:, half*512:(half+1)*512]),
                         reads=[ctxb[t]], writes=[obb[sl]])
                    P.dma("act", lambda h, t=t, sl=sl, half=half, toff=toff: h.dma_start(
                        out=OUT[toff+t*128:toff+(t+1)*128, half*512:(half+1)*512], in_=ob[sl][:, :]),
                        reads=[obb[sl]], stream=f"{name}o")
        else:
            for t in range(nkT):
                bank = P.bank()
                for c in range(8):
                    P.op("pe", lambda h, c=c, t=t, bank=bank: h.transpose(
                        out=bank.bf16[:, c*128:(c+1)*128], in_=ctx[t][:, c*128:(c+1)*128], identity=ident[:, :]),
                        reads=[ctxb[t], identb], writes=[bank.buf] if c == 0 else (), wpart=[bank.buf] if c > 0 else ())
                for c in range(8):
                    P.op("dve", lambda h, c=c, t=t, bank=bank: h.tensor_copy(out=cfm[c][:, t*128:(t+1)*128], in_=bank.bf16[:, c*128:(c+1)*128]),
                         reads=[bank.buf], wpart=[cfb[c]])
                P.bank_free(bank)
            loadw(io["WoT"])
            for t in range(nkT):
                for dc in range(2):
                    bank = P.bank()
                    for kc in range(8):
                        P.op("pe", lambda h, kc=kc, t=t, dc=dc, bank=bank: h.matmul(
                            out=bank.f32, lhsT=cfm[kc][:, t*128:(t+1)*128], rhs=wsb[kc][:, dc*512:(dc+1)*512],
                            start=(kc == 0), stop=(kc == 7)),
                            reads=[wbufs[kc], cfb[kc]], writes=[bank.buf] if kc == 0 else (), wpart=[bank.buf] if kc > 0 else ())
                    sl = (t * 2 + dc) % 2
                    if borep is not None:
                        P.op("dve", lambda h, bank=bank, dc=dc, sl=sl: h.tensor_add(out=ob[sl][:, :], in0=bank.f32, in1=borep[:, dc*512:(dc+1)*512]),
                             reads=[bank.buf, borepb], writes=[obb[sl]])
                    else:
                        P.op("dve", lambda h, bank=bank, sl=sl: h.tensor_copy(out=ob[sl][:, :], in_=bank.f32),
                             reads=[bank.buf], writes=[obb[sl]])
                    P.dma("act", lambda h, t=t, dc=dc, sl=sl, toff=toff: h.dma_start(
                        out=OUT[toff+t*128:toff+(t+1)*128, dc*512:(dc+1)*512], in_=ob[sl][:, :]),
                        reads=[obb[sl]], stream=f"{name}o")
                    P.bank_free(bank)
    P.barrier()
    P.release(m0)


def emit_moe(P, C, io, ntok, func=AF.Relu, name="moe"):
    m0 = P.mark()
    D, F, E = 1024, 4096, 8
    Xfm = io["X_fm"]; OUT = io["OUT"]
    ident, identb = C.ident_sb(P)

    gwf = P.salloc(f"{name}_gwf", [128, 8*8], F32)
    gwfb = Buf(f"{name}_gwf")
    gw = P.salloc(f"{name}_gw", [128, 8*8], BF16)
    gwb = Buf(f"{name}_gw")
    gwl = P.salloc(f"{name}_gwl", [128, 8*8], BF16)
    gwlb = Buf(f"{name}_gwl")
    for kc in range(8):
        P.dma("sp", lambda h, kc=kc: h.dma_start(out=gwf[:, kc*8:(kc+1)*8], in_=io["gate_wT"][kc*128:(kc+1)*128, :]),
              wpart=[gwfb], stream=f"{name}gwf")
    P.op("dve", lambda h: h.tensor_copy(out=gw[:, :], in_=gwf[:, :]), reads=[gwfb], writes=[gwb])
    P.op("dve", lambda h: h.tensor_sub(out=gwl[:, :], in0=gwf[:, :], in1=gw[:, :]), reads=[gwfb, gwb], writes=[gwlb])
    gbr = gbrb = None
    if io.get("gate_b_rep") is not None:
        gbr = P.salloc(f"{name}_gb", [128, 8], F32); gbrb = Buf("gb")
        P.dma("sp", lambda h: h.dma_start(out=gbr[:, :], in_=io["gate_b_rep"]), writes=[gbrb], stream="consts")
    b2sb = P.salloc(f"{name}_b2", [128, D], BF16)
    b2b = Buf(f"{name}_b2")
    P.dma("pool", lambda h: h.dma_start(out=b2sb[0:8, :], in_=io["B2"]), writes=[b2b], stream=f"{name}b2")

    nhalf = 2
    TH = ntok // nhalf
    nt = TH // 128
    xfw = P.salloc(f"{name}_xfw", [128, 8*TH], BF16)
    xfwb = Buf(f"{name}_xfw")
    xf = [xfw[:, kc*TH:(kc+1)*TH] for kc in range(8)]
    xfb = [xfwb for kc in range(8)]
    acc = [P.salloc(f"{name}_ac{t}", [128, D], F32) for t in range(nt)]
    accb = [Buf(f"{name}_ac{t}") for t in range(nt)]
    w1w = [P.salloc(f"{name}_w1w{s}", [128, 8192], BF16) for s in range(2)]
    w1wb = [Buf(f"{name}_w1w{s}") for s in range(2)]
    w2w = [P.salloc(f"{name}_w2w{s}", [128, 8192], BF16) for s in range(2)]
    w2wb = [Buf(f"{name}_w2w{s}") for s in range(2)]
    w1s = [[w1w[s][:, kc*1024:(kc+1)*1024] for kc in range(8)] for s in range(2)]
    w1sb = [[w1wb[s] for kc in range(8)] for s in range(2)]
    w2s = [[w2w[s][:, kc*1024:(kc+1)*1024] for kc in range(8)] for s in range(2)]
    w2sb = [[w2wb[s] for kc in range(8)] for s in range(2)]
    b1s = [P.salloc(f"{name}_b1_{s}", [128, 8], F32) for s in range(2)]
    b1sb = [Buf(f"{name}_b1_{s}") for s in range(2)]
    hid = [P.salloc(f"{name}_hid{fc}", [128, 512], BF16) for fc in range(8)]
    hidb = [Buf(f"{name}_hid{fc}") for fc in range(8)]
    comb = P.salloc(f"{name}_comb", [128, 8*nt], F32)
    combb = [Buf(f"{name}_cb{t}") for t in range(nt)]
    combf = P.salloc(f"{name}_cfm", [128, TH], BF16)
    combfb = Buf(f"{name}_cfm")
    gs = P.salloc(f"{name}_gs", [128, 16], F32)
    gsb = Buf(f"{name}_gs")
    cbf = P.salloc(f"{name}_cbf", [128, 8], BF16)
    cbfb = Buf(f"{name}_cbf")
    ostage = [P.salloc(f"{name}_os{i}", [128, 512], F32) for i in range(4)]
    ostb = [Buf(f"{name}_os{i}") for i in range(4)]

    def load_quarter(slot, e, fq):
        def l1(h, e=e, fq=fq, slot=slot):
            srcv = io["W1T"][e*1024:(e+1)*1024, fq*1024:(fq+1)*1024].rearrange("(kc p) j -> p kc j", p=128)
            return h.dma_start(out=w1w[slot][:, :].rearrange("p (kc j) -> p kc j", kc=8), in_=srcv)
        P.dma("pool", l1, writes=[w1wb[slot]], stream=f"{name}w1")
        def l2(h, e=e, fq=fq, slot=slot):
            srcv = io["W2T"][e*4096+fq*1024:e*4096+(fq+1)*1024, :].rearrange("(kc p) j -> p kc j", p=128)
            return h.dma_start(out=w2w[slot][:, :].rearrange("p (kc j) -> p kc j", kc=8), in_=srcv)
        P.dma("pool", l2, writes=[w2wb[slot]], stream=f"{name}w2")
        P.dma("pool", lambda h, e=e, fq=fq, slot=slot: h.dma_start(
            out=b1s[slot][:, :], in_=io["B1"][e*128:(e+1)*128, fq*8:(fq+1)*8]),
            writes=[b1sb[slot]], stream=f"{name}w1")

    Xlo = io["X_lo_fm"]
    xl = [P.salloc(f"{name}_xl{kc}", [128, 512], BF16) for kc in range(8)]
    xlb = [Buf(f"{name}_xl{kc}") for kc in range(8)]
    for half in range(nhalf):
        hoff = half * TH
        def lx(h, hoff=hoff):
            srcv = Xfm.rearrange("(kc p) n -> p kc n", p=128)[:, :, hoff:hoff+TH]
            return h.dma_start(out=xfw[:, :].rearrange("p (kc n) -> p kc n", kc=8), in_=srcv)
        P.dma("sp", lx, writes=[xfwb], stream=f"{name}xf")
        for t in range(nt):
            if t % 4 == 0:
                for kc in range(8):
                    P.dma("sp", lambda h, kc=kc, hoff=hoff, t=t: h.dma_start(
                        out=xl[kc][:, :], in_=Xlo[kc*128:(kc+1)*128, hoff+t*128:hoff+t*128+512]),
                        writes=[xlb[kc]], stream=f"{name}xl")
            xo = (t % 4) * 128
            bank = P.bank()
            for kc in range(8):
                P.op("pe", lambda h, kc=kc, t=t, bank=bank: h.matmul(
                    out=bank.t[0:128, 0:8], lhsT=xf[kc][:, t*128:(t+1)*128], rhs=gw[:, kc*8:(kc+1)*8],
                    start=(kc == 0), stop=False),
                    reads=[xfb[kc], gwb], writes=[bank.buf] if kc == 0 else (), wpart=[bank.buf] if kc > 0 else ())
            for kc in range(8):
                P.op("pe", lambda h, kc=kc, t=t, xo=xo, bank=bank: h.matmul(
                    out=bank.t[0:128, 0:8], lhsT=xl[kc][:, xo:xo+128], rhs=gw[:, kc*8:(kc+1)*8],
                    start=False, stop=False),
                    reads=[xlb[kc], gwb], wpart=[bank.buf])
            for kc in range(8):
                P.op("pe", lambda h, kc=kc, t=t, bank=bank: h.matmul(
                    out=bank.t[0:128, 0:8], lhsT=xf[kc][:, t*128:(t+1)*128], rhs=gwl[:, kc*8:(kc+1)*8],
                    start=False, stop=(kc == 7)),
                    reads=[xfb[kc], gwlb], wpart=[bank.buf])
            gl = comb[:, t*8:(t+1)*8]
            if gbr is not None:
                P.op("dve", lambda h, bank=bank, gl=gl: h.tensor_add(out=gl, in0=bank.t[0:128, 0:8], in1=gbr[:, :]),
                     reads=[bank.buf, gbrb], writes=[combb[t]])
            else:
                P.op("dve", lambda h, bank=bank, gl=gl: h.tensor_copy(out=gl, in_=bank.t[0:128, 0:8]),
                     reads=[bank.buf], writes=[combb[t]])
            P.bank_free(bank)
            P.op("dve", lambda h, gl=gl: h.reduce_max(out=gs[:, 0:1], in_=gl, axis=AX.X), reads=[combb[t]], writes=[gsb])
            P.op("dve", lambda h, gl=gl: h.tensor_scalar(out=gs[:, 2:10], in0=gl, scalar1=gs[:, 0:1], scalar2=None, op0=ALU.is_ge),
                 reads=[combb[t], gsb], wpart=[gsb])
            P.op("dve", lambda h, gl=gl: h.scalar_tensor_tensor(out=gs[:, 2:10], in0=gs[:, 2:10], scalar=-1e30, in1=gl, op0=ALU.mult, op1=ALU.add),
                 reads=[combb[t], gsb], wpart=[gsb])
            P.op("dve", lambda h: h.reduce_max(out=gs[:, 1:2], in_=gs[:, 2:10], axis=AX.X), reads=[gsb], wpart=[gsb])
            P.op("dve", lambda h: h.tensor_scalar(out=gs[:, 10:11], in0=gs[:, 0:1], scalar1=gs[:, 1:2], scalar2=None, op0=ALU.subtract),
                 reads=[gsb], wpart=[gsb])
            P.op("act", lambda h: h.activation(out=gs[:, 11:12], in_=gs[:, 10:11], func=AF.Sigmoid), reads=[gsb], wpart=[gsb])
            P.op("dve", lambda h: h.tensor_scalar(out=gs[:, 12:13], in0=gs[:, 11:12], scalar1=-1.0, scalar2=1.0, op0=ALU.mult, op1=ALU.add),
                 reads=[gsb], wpart=[gsb])
            P.op("dve", lambda h: h.tensor_scalar(out=gs[:, 2:10], in0=gs[:, 2:10], scalar1=gs[:, 1:2], scalar2=None, op0=ALU.is_ge),
                 reads=[gsb], wpart=[gsb])
            P.op("dve", lambda h, gl=gl: h.tensor_scalar(out=gl, in0=gl, scalar1=gs[:, 0:1], scalar2=None, op0=ALU.is_ge),
                 reads=[gsb], wpart=[combb[t]])
            P.op("dve", lambda h, gl=gl: h.tensor_scalar(out=gl, in0=gl, scalar1=gs[:, 11:12], scalar2=None, op0=ALU.mult),
                 reads=[gsb], wpart=[combb[t]])
            P.op("dve", lambda h, gl=gl: h.scalar_tensor_tensor(out=gl, in0=gs[:, 2:10], scalar=gs[:, 12:13], in1=gl, op0=ALU.mult, op1=ALU.add),
                 reads=[gsb], wpart=[combb[t]])
            P.op("dve", lambda h, gl=gl: h.tensor_copy(out=cbf[:, :], in_=gl), reads=[combb[t]], writes=[cbfb])
            bank = P.bank()
            P.op("pe", lambda h, bank=bank: h.transpose(out=bank.bf16[0:8, 0:128], in_=cbf[:, 0:8], identity=ident[:, :]),
                 reads=[cbfb, identb], writes=[bank.buf])
            P.op("dve", lambda h, t=t, bank=bank: h.tensor_copy(out=combf[0:8, t*128:(t+1)*128], in_=bank.bf16[0:8, 0:128]),
                 reads=[bank.buf], wpart=[combfb])
            P.bank_free(bank)
        for t in range(nt):
            for dc in range(2):
                bank = P.bank()
                P.op("pe", lambda h, t=t, dc=dc, bank=bank: h.matmul(
                    out=bank.f32, lhsT=combf[0:8, t*128:(t+1)*128], rhs=b2sb[0:8, dc*512:(dc+1)*512],
                    start=True, stop=True),
                    reads=[combfb, b2b], writes=[bank.buf])
                P.op("dve", lambda h, t=t, dc=dc, bank=bank: h.tensor_copy(out=acc[t][:, dc*512:(dc+1)*512], in_=bank.f32),
                     reads=[bank.buf], writes=[accb[t]] if dc == 0 else (), wpart=[accb[t]] if dc else ())
                P.bank_free(bank)
        if half == 0:
            load_quarter(0, 0, 0)
        for e in range(E):
            for fq in range(4):
                slot = (e * 4 + fq) % 2
                nslot = 1 - slot
                ne, nfq = (e, fq + 1) if fq < 3 else (e + 1, 0)
                if ne < E:
                    load_quarter(nslot, ne, nfq)
                elif half == 0:
                    load_quarter(nslot, 0, 0)
                for tc in range(TH // 512):
                    for fc in range(8):
                        bank = P.bank()
                        for kc in range(8):
                            P.op("pe", lambda h, kc=kc, fc=fc, tc=tc, slot=slot, bank=bank: h.matmul(
                                out=bank.f32, lhsT=w1s[slot][kc][:, fc*128:(fc+1)*128], rhs=xf[kc][:, tc*512:(tc+1)*512],
                                start=(kc == 0), stop=(kc == 7)),
                                reads=[w1sb[slot][kc], xfb[kc]], writes=[bank.buf] if kc == 0 else (), wpart=[bank.buf] if kc > 0 else ())
                        P.op("act", lambda h, fc=fc, slot=slot, bank=bank: h.activation(
                            out=hid[fc][:, :], in_=bank.f32, func=func, bias=b1s[slot][:, fc:fc+1]),
                            reads=[bank.buf, b1sb[slot]], writes=[hidb[fc]])
                        P.bank_free(bank)
                    for tt in range(4):
                        t = tc * 4 + tt
                        for dc in range(2):
                            bank = P.bank()
                            for fc in range(8):
                                P.op("pe", lambda h, fc=fc, tt=tt, dc=dc, slot=slot, bank=bank: h.matmul(
                                    out=bank.f32, lhsT=hid[fc][:, tt*128:(tt+1)*128], rhs=w2s[slot][fc][:, dc*512:(dc+1)*512],
                                    start=(fc == 0), stop=(fc == 7)),
                                    reads=[hidb[fc], w2sb[slot][fc]], writes=[bank.buf] if fc == 0 else (), wpart=[bank.buf] if fc > 0 else ())
                            P.op("dve", lambda h, t=t, dc=dc, e=e, bank=bank: h.scalar_tensor_tensor(
                                out=acc[t][:, dc*512:(dc+1)*512], in0=bank.f32, scalar=comb[:, t*8+e:t*8+e+1],
                                in1=acc[t][:, dc*512:(dc+1)*512], op0=ALU.mult, op1=ALU.add),
                                reads=[bank.buf, combb[t]], wpart=[accb[t]])
                            P.bank_free(bank)
        for t in range(nt):
            for dc in range(2):
                sl = (t * 2 + dc) % 4
                P.op("dve", lambda h, t=t, dc=dc, sl=sl: h.tensor_copy(out=ostage[sl][:, :], in_=acc[t][:, dc*512:(dc+1)*512]),
                     reads=[accb[t]], writes=[ostb[sl]])
                P.dma("act", lambda h, t=t, dc=dc, sl=sl, hoff=hoff: h.dma_start(
                    out=OUT[hoff+t*128:hoff+(t+1)*128, dc*512:(dc+1)*512], in_=ostage[sl][:, :]),
                    reads=[ostb[sl]], stream=f"{name}o")
    P.barrier()
    P.release(m0)


def emit_tree(P, C, io, ntok, name="l1f"):
    m0 = P.mark()
    D = 1024
    ident, identb = C.ident_sb(P)
    nt = ntok // 128
    Xfm = io["XN_fm"]; OUT = io["OUT"]

    gate = [P.salloc(f"{name}_g{t}", [128, 256], BF16) for t in range(nt)]
    gb = [Buf(f"{name}_g{t}") for t in range(nt)]
    lpf = [P.salloc(f"{name}_lpf{c}", [128, ntok], BF16) for c in range(2)]
    lpfb = [Buf(f"{name}_lpf{c}") for c in range(2)]
    xfm = [P.salloc(f"{name}_xf{kc}", [128, ntok], BF16) for kc in range(8)]
    xfb = [Buf(f"{name}_xf{kc}") for kc in range(8)]
    wsb = [P.salloc(f"{name}_w{kc}", [128, D], BF16) for kc in range(8)]
    wbufs = [Buf(f"{name}_w{kc}") for kc in range(8)]
    qfm = [P.salloc(f"{name}_qf{kc}", [128, ntok], BF16) for kc in range(8)]
    qfb = [Buf(f"{name}_qf{kc}") for kc in range(8)]
    rsb = [P.salloc(f"{name}_r{kc}", [128, 256], BF16) for kc in range(8)]
    rbufs = [Buf(f"{name}_r{kc}") for kc in range(8)]
    lv = [P.salloc(f"{name}_lv{c}", [128, D], BF16) for c in range(2)]
    lvb = [Buf(f"{name}_lv{c}") for c in range(2)]
    lpA2 = [P.salloc(f"{name}_lpA{i}", [128, 256], F32) for i in range(2)]
    lpA2b = [Buf(f"{name}_lpA{i}") for i in range(2)]
    lpB2 = [P.salloc(f"{name}_lpB{i}", [128, 256], F32) for i in range(2)]
    lpB2b = [Buf(f"{name}_lpB{i}") for i in range(2)]
    gc2 = [P.salloc(f"{name}_gc{i}", [128, 128], F32) for i in range(2)]
    gc2b = [Buf(f"{name}_gc{i}") for i in range(2)]
    lpbf2 = [P.salloc(f"{name}_lpbf{i}", [128, 256], BF16) for i in range(2)]
    lpbf2b = [Buf(f"{name}_lpbf{i}") for i in range(2)]
    retf = xfm
    retfb = xfb
    borep = P.salloc(f"{name}_bo", [128, D], F32); borepb = Buf("bo")
    P.dma("sp", lambda h: h.dma_start(out=borep[:, :], in_=io["bo_rep"]), writes=[borepb], stream="consts")
    ostage = [P.salloc(f"{name}_os{i}", [128, 512], F32) for i in range(4)]
    ostb = [Buf(f"{name}_os{i}") for i in range(4)]

    for kc in range(8):
        P.dma("sp", lambda h, kc=kc: h.dma_start(out=xfm[kc][:, :], in_=Xfm[kc*128:(kc+1)*128, :]),
              writes=[xfb[kc]], stream=f"{name}xf")
        P.dma("pool", lambda h, kc=kc: h.dma_start(out=wsb[kc][:, :], in_=io["WqT"][kc*128:(kc+1)*128, :]),
              writes=[wbufs[kc]], stream=f"{name}w")
        P.dma("pool", lambda h, kc=kc: h.dma_start(out=rsb[kc][:, :], in_=io["RoutT"][kc*128:(kc+1)*128, :]),
              writes=[rbufs[kc]], stream=f"{name}rt")
    for c in range(2):
        P.dma("pool", lambda h, c=c: h.dma_start(out=lv[c][:, :], in_=io["LV"][c*128:(c+1)*128, :]),
              writes=[lvb[c]], stream=f"{name}rt")

    for oc in range(8):
        for t4 in range(ntok // 512):
            bank = P.bank()
            for kc in range(8):
                P.op("pe", lambda h, kc=kc, oc=oc, t4=t4, bank=bank: h.matmul(
                    out=bank.f32, lhsT=wsb[kc][:, oc*128:(oc+1)*128], rhs=xfm[kc][:, t4*512:(t4+1)*512],
                    start=(kc == 0), stop=(kc == 7)),
                    reads=[wbufs[kc], xfb[kc]], writes=[bank.buf] if kc == 0 else (), wpart=[bank.buf] if kc > 0 else ())
            P.op("act", lambda h, oc=oc, t4=t4, bank=bank: h.copy(out=qfm[oc][:, t4*512:(t4+1)*512], in_=bank.f32),
                 reads=[bank.buf], wpart=[qfb[oc]])
            P.bank_free(bank)
    for t in range(nt):
        bank = P.bank()
        for kc in range(8):
            P.op("pe", lambda h, kc=kc, t=t, bank=bank: h.matmul(
                out=bank.t[0:128, 0:256], lhsT=qfm[kc][:, t*128:(t+1)*128], rhs=rsb[kc][:, :],
                start=(kc == 0), stop=(kc == 7)),
                reads=[qfb[kc], rbufs[kc]], writes=[bank.buf] if kc == 0 else (), wpart=[bank.buf] if kc > 0 else ())
        P.op("act", lambda h, t=t, bank=bank: h.activation(out=gate[t][:, :], in_=bank.t[0:128, 0:256], func=AF.Sigmoid),
             reads=[bank.buf], writes=[gb[t]])
        P.bank_free(bank)
        par = t % 2
        lpA, lpAb, lpB, lpBb = lpA2[par], lpA2b[par], lpB2[par], lpB2b[par]
        gc, gcb = gc2[par], gc2b[par]
        lpbf, lpbfb = lpbf2[par], lpbf2b[par]
        eng = "dve" if par == 0 else "pool"
        P.op(eng, lambda h, t=t, lpA=lpA: h.tensor_scalar(out=lpA[:, 0:1], in0=gate[t][:, 0:1], scalar1=-1.0, scalar2=1.0, op0=ALU.mult, op1=ALU.add),
             reads=[gb[t]], writes=[lpAb])
        P.op(eng, lambda h, t=t, lpA=lpA: h.tensor_copy(out=lpA[:, 1:2], in_=gate[t][:, 0:1]), reads=[gb[t]], wpart=[lpAb])
        cur, curb, oth, othb = lpA, lpAb, lpB, lpBb
        for lvl in range(1, 8):
            n = 1 << lvl
            P.op(eng, lambda h, t=t, n=n, gc=gc: h.tensor_scalar(out=gc[:, 0:n], in0=gate[t][:, n-1:2*n-1], scalar1=-1.0, scalar2=1.0, op0=ALU.mult, op1=ALU.add),
                 reads=[gb[t]], writes=[gcb])
            P.op(eng, lambda h, cur=cur, oth=oth, n=n, gc=gc: h.tensor_mul(out=oth[:, 0:n], in0=cur[:, 0:n], in1=gc[:, 0:n]),
                 reads=[curb, gcb], writes=[othb])
            P.op(eng, lambda h, cur=cur, oth=oth, t=t, n=n: h.tensor_mul(out=oth[:, n:2*n], in0=cur[:, 0:n], in1=gate[t][:, n-1:2*n-1]),
                 reads=[curb, gb[t]], wpart=[othb])
            cur, curb, oth, othb = oth, othb, cur, curb
        P.op(eng, lambda h, cur=cur, lpbf=lpbf: h.tensor_copy(out=lpbf[:, :], in_=cur[:, :]), reads=[curb], writes=[lpbfb])
        bank = P.bank()
        for c in range(2):
            P.op("pe", lambda h, c=c, bank=bank, lpbf=lpbf: h.transpose(out=bank.bf16[:, c*128:(c+1)*128], in_=lpbf[:, c*128:(c+1)*128], identity=ident[:, :]),
                 reads=[lpbfb, identb], writes=[bank.buf] if c == 0 else (), wpart=[bank.buf] if c else ())
        for c in range(2):
            P.op("dve", lambda h, c=c, t=t, bank=bank: h.tensor_copy(out=lpf[c][:, t*128:(t+1)*128], in_=bank.bf16[:, c*128:(c+1)*128]),
                 reads=[bank.buf], wpart=[lpfb[c]])
        P.bank_free(bank)
    for dc in range(8):
        for t4 in range(ntok // 512):
            bank = P.bank()
            for c in range(2):
                P.op("pe", lambda h, c=c, dc=dc, t4=t4, bank=bank: h.matmul(
                    out=bank.f32, lhsT=lv[c][:, dc*128:(dc+1)*128], rhs=lpf[c][:, t4*512:(t4+1)*512],
                    start=(c == 0), stop=(c == 1)),
                    reads=[lvb[c], lpfb[c]], writes=[bank.buf] if c == 0 else (), wpart=[bank.buf] if c else ())
            P.op("act", lambda h, dc=dc, t4=t4, bank=bank: h.copy(out=retf[dc][:, t4*512:(t4+1)*512], in_=bank.f32),
                 reads=[bank.buf], wpart=[retfb[dc]])
            P.bank_free(bank)
    for kc in range(8):
        P.dma("pool", lambda h, kc=kc: h.dma_start(out=wsb[kc][:, :], in_=io["WoutT"][kc*128:(kc+1)*128, :]),
              writes=[wbufs[kc]], stream=f"{name}w")
    for t in range(nt):
        for dc in range(2):
            bank = P.bank()
            for kc in range(8):
                P.op("pe", lambda h, kc=kc, t=t, dc=dc, bank=bank: h.matmul(
                    out=bank.f32, lhsT=retf[kc][:, t*128:(t+1)*128], rhs=wsb[kc][:, dc*512:(dc+1)*512],
                    start=(kc == 0), stop=(kc == 7)),
                    reads=[wbufs[kc], retfb[kc]], writes=[bank.buf] if kc == 0 else (), wpart=[bank.buf] if kc > 0 else ())
            sl = (t * 2 + dc) % 4
            P.op("dve", lambda h, bank=bank, dc=dc, sl=sl: h.tensor_add(out=ostage[sl][:, :], in0=bank.f32, in1=borep[:, dc*512:(dc+1)*512]),
                 reads=[bank.buf, borepb], writes=[ostb[sl]])
            P.dma("act", lambda h, t=t, dc=dc, sl=sl: h.dma_start(
                out=OUT[t*128:(t+1)*128, dc*512:(dc+1)*512], in_=ostage[sl][:, :]),
                reads=[ostb[sl]], stream=f"{name}o")
            P.bank_free(bank)
    P.barrier()
    P.release(m0)


def prep_tree(Wq, routers, leaf_values, Wout_w):
    D = 1024
    M = 256
    def bitrev(x, bits):
        r = 0
        for i in range(bits):
            r = (r << 1) | ((x >> i) & 1)
        return r
    routP = np.zeros((M, D), routers.dtype)
    for lvl in range(8):
        n = 1 << lvl
        for j in range(n):
            routP[n-1+j] = routers[n-1+bitrev(j, lvl)] if lvl > 0 else routers[0]
    lvP = leaf_values[[bitrev(j, 8) for j in range(M)]]
    return {
        "WqT": np.ascontiguousarray(Wq.T),
        "RoutT": np.ascontiguousarray(routP.T),
        "LV": np.ascontiguousarray(lvP),
        "WoutT": np.ascontiguousarray(Wout_w.T),
    }


def emit_perf(P, C, io, nseq=2, T=2048, name="l2a", cc_tag="l2amax"):
    m0 = P.mark()
    D, H, hd, M = 1024, 16, 64, 256
    c_ = 64.0 ** -0.25
    ratio = M ** -0.5
    EPS = 1e-4
    nkT = T // 128
    Xfm = io["XN_fm"]; OUT = io["OUT"]
    ident, identb = C.ident_sb(P)
    id32, id32b = C.ident_sb32(P)
    nc = P.nc
    ones_row = C.add("c_ones_row32", np.ones((1, 128), np.float32), F32)

    cci = nc.dram_tensor(f"{cc_tag}_in", [128, 8], F32, kind="Internal").ap().opt()
    cco = nc.dram_tensor(f"{cc_tag}_out", [128, 8], F32, kind="Internal").ap().opt()
    ccib, ccob = Buf("cci"), Buf("cco")

    wsb = [P.salloc(f"{name}_w{kc}", [128, D], BF16) for kc in range(8)]
    wbufs = [Buf(f"{name}_w{kc}") for kc in range(8)]
    xfm = [P.salloc(f"{name}_xf{kc}", [128, T], BF16) for kc in range(8)]
    xfb = [Buf(f"{name}_xf{kc}") for kc in range(8)]
    kall = [P.salloc(f"{name}_ka{kc}", [128, T], BF16) for kc in range(8)]
    kab = [Buf(f"{name}_ka{kc}") for kc in range(8)]
    qfm = [P.salloc(f"{name}_qf{kc}", [128, T], BF16) for kc in range(8)]
    qfb = [Buf(f"{name}_qf{kc}") for kc in range(8)]
    vtm = [P.salloc(f"{name}_vt{t}", [128, D], BF16) for t in range(nkT)]
    vtb = [Buf(f"{name}_vt{t}") for t in range(nkT)]
    prj = P.salloc(f"{name}_prj", [128, M], BF16)
    prjb = Buf(f"{name}_prj")
    P.dma("pool", lambda h: h.dma_start(out=prj[:, :], in_=io["ProjPadT"]), writes=[prjb], stream=f"{name}prj")
    prjB = P.salloc(f"{name}_prjB", [128, M], BF16)
    prjBb = Buf(f"{name}_prjB")
    P.dma("pool", lambda h: h.dma_start(out=prjB[:, :], in_=io["ProjPadB"]), writes=[prjBb], stream=f"{name}prj")
    onesr = P.salloc(f"{name}_onr", [128, 128], F32)
    onesrb = Buf(f"{name}_onr")
    P.dma("sp", lambda h: h.dma_start(out=onesr[0:1, :], in_=ones_row), writes=[onesrb], stream="consts")
    onec = P.salloc(f"{name}_one", [128, 1], BF16)
    onecb = Buf(f"{name}_one")
    P.op("dve", lambda h: h.memset(onec[:, :], 1.0), writes=[onecb])
    gmx = P.salloc(f"{name}_gmx", [128, 16], F32)
    gmxb = Buf(f"{name}_gmx")
    stab = P.salloc(f"{name}_stab", [128, 1], F32)
    stabb = Buf(f"{name}_stab")
    sqt = P.salloc(f"{name}_sq", [128, T], BF16)
    sqb = Buf(f"{name}_sq")
    diag = P.salloc(f"{name}_diag", [128, 2*nkT], F32)
    diagb = Buf(f"{name}_diag")
    brx = P.salloc(f"{name}_brx", [128, T], F32)
    brf = brx; brfb = Buf(f"{name}_brf")
    brgb = Buf(f"{name}_brg")
    brep = P.salloc(f"{name}_brep", [128, T], F32)
    brepb = Buf(f"{name}_brep")
    qpad = P.salloc(f"{name}_qpad", [128, T], BF16)
    qpadb = Buf(f"{name}_qpad")
    qpad2 = P.salloc(f"{name}_qpad2", [128, T], BF16)
    qpad2b = Buf(f"{name}_qpad2")
    qpt = [P.salloc(f"{name}_qpt{c}", [128, T], BF16) for c in range(2)]
    qptb = [Buf(f"{name}_qpt{c}") for c in range(2)]
    kpt = [P.salloc(f"{name}_kp{t}", [128, M], BF16) for t in range(nkT)]
    kpb = [Buf(f"{name}_kp{t}") for t in range(nkT)]
    ctxm = P.salloc(f"{name}_cxm", [128, 2*hd], BF16)
    ctxmb = Buf(f"{name}_cxm")
    kcs = P.salloc(f"{name}_kcs", [128, 2], BF16)
    kcsb = Buf(f"{name}_kcs")
    dinv = P.salloc(f"{name}_dinv", [128, nkT], F32)
    dinvb = Buf(f"{name}_dinv")
    aofm = xfm
    aofb = xfb
    borep = P.salloc(f"{name}_bo", [128, D], F32); borepb = Buf("bo")
    P.dma("sp", lambda h: h.dma_start(out=borep[:, :], in_=io["bo_rep"]), writes=[borepb], stream="consts")
    ostage = [P.salloc(f"{name}_os{i}", [128, 512], F32) for i in range(2)]
    ostb = [Buf(f"{name}_os{i}") for i in range(2)]
    st = P.salloc(f"{name}_stt", [128, 8], F32)
    stb = Buf(f"{name}_stt")

    def loadw(Wt):
        for kc in range(8):
            P.dma("pool", lambda h, kc=kc, Wt=Wt: h.dma_start(out=wsb[kc][:, :], in_=Wt[kc*128:(kc+1)*128, :]),
                  writes=[wbufs[kc]], stream=f"{name}w")

    def proj_fm(dst, dstb, Toff):
        for oc in range(8):
            for t4 in range(T // 512):
                bank = P.bank()
                for kc in range(8):
                    P.op("pe", lambda h, kc=kc, oc=oc, t4=t4, bank=bank: h.matmul(
                        out=bank.f32, lhsT=wsb[kc][:, oc*128:(oc+1)*128], rhs=xfm[kc][:, t4*512:(t4+1)*512],
                        start=(kc == 0), stop=(kc == 7)),
                        reads=[wbufs[kc], xfb[kc]], writes=[bank.buf] if kc == 0 else (), wpart=[bank.buf] if kc > 0 else ())
                P.op("act", lambda h, oc=oc, t4=t4, bank=bank, Toff=Toff: h.copy(out=dst[oc][:, Toff+t4*512:Toff+(t4+1)*512], in_=bank.f32),
                     reads=[bank.buf], wpart=[dstb[oc]])
                P.bank_free(bank)

    def loadx(s):
        for kc in range(8):
            P.dma("sp", lambda h, kc=kc, s=s: h.dma_start(out=xfm[kc][:, :], in_=Xfm[kc*128:(kc+1)*128, s*T:(s+1)*T]),
                  writes=[xfb[kc]], stream=f"{name}xf")

    P.op("dve", lambda h: h.memset(gmx[:, :], -1e30), writes=[gmxb])
    for s in range(nseq):
        loadx(s)
        loadw(io["Wk"])
        proj_fm(kall, kab, 0)
        for h in range(H):
            kc8, rb = (h * hd) // 128, (h * hd) % 128
            for nt2 in range(nkT // 2):
                bank = P.bank()
                P.op("pe", lambda h_, kc8=kc8, rb=rb, nt2=nt2, s=s, bank=bank: h_.matmul(
                    out=bank.f32[0:128, 0:256], lhsT=kall[kc8][rb:rb+hd, nt2*256:nt2*256+128], rhs=(prj if rb == 0 else prjB)[rb:rb+hd, :],
                    start=True, stop=True),
                    reads=[kab[kc8], prjb, prjBb], writes=[bank.buf])
                P.op("pe", lambda h_, kc8=kc8, rb=rb, nt2=nt2, s=s, bank=bank: h_.matmul(
                    out=bank.f32[0:128, 256:512], lhsT=kall[kc8][rb:rb+hd, nt2*256+128:nt2*256+256], rhs=(prj if rb == 0 else prjB)[rb:rb+hd, :],
                    start=True, stop=True),
                    reads=[kab[kc8], prjb, prjBb], wpart=[bank.buf])
                P.op("dve", lambda h_, bank=bank: h_.reduce_max(out=st[:, 0:1], in_=bank.f32, axis=AX.X),
                     reads=[bank.buf], writes=[stb])
                P.op("dve", lambda h_: h_.tensor_max(out=gmx[:, 0:1], in0=gmx[:, 0:1], in1=st[:, 0:1]),
                     reads=[stb], wpart=[gmxb])
                P.bank_free(bank)
    bank = P.bank()
    P.op("pe", lambda h, bank=bank: h.transpose(out=bank.f32[0:1, 0:128], in_=gmx[:, 0:1], identity=id32[:, :]),
         reads=[gmxb, id32b], writes=[bank.buf])
    P.op("dve", lambda h, bank=bank: h.reduce_max(out=gmx[0:1, 1:2], in_=bank.f32[0:1, 0:128], axis=AX.X),
         reads=[bank.buf], wpart=[gmxb])
    P.bank_free(bank)
    P.op("dve", lambda h: h.tensor_scalar(out=gmx[:, 2:3], in0=gmx[:, 1:2], scalar1=c_, scalar2=None, op0=ALU.mult),
         reads=[gmxb], wpart=[gmxb])
    P.op("dve", lambda h: h.memset(gmx[:, 8:16], -1e30), wpart=[gmxb])
    P.op("dve", lambda h: h.tensor_copy(out=gmx[0:1, 8:9], in_=gmx[0:1, 2:3]), reads=[gmxb], wpart=[gmxb])
    P.dma("sp", lambda h: h.dma_start(out=cci, in_=gmx[:, 8:16]), reads=[gmxb], writes=[ccib], stream=f"{name}cc")
    P.op("pool", lambda h: h.collective_compute("AllReduce", ALU.max, replica_groups=[list(range(8))], ins=[cci], outs=[cco]),
         reads=[ccib], writes=[ccob])
    P.dma("sp", lambda h: h.dma_start(out=gmx[:, 8:16], in_=cco), reads=[ccob], writes=[gmxb], stream=f"{name}cc")
    bank = P.bank()
    P.op("pe", lambda h, bank=bank: h.matmul(out=bank.f32[0:128, 0:1], lhsT=onesr[0:1, 0:128], rhs=gmx[0:1, 8:9],
                                             start=True, stop=True),
         reads=[onesrb, gmxb], writes=[bank.buf])
    P.op("dve", lambda h, bank=bank: h.tensor_copy(out=stab[:, :], in_=bank.f32[0:128, 0:1]), reads=[bank.buf], writes=[stabb])
    P.bank_free(bank)

    for s in range(nseq):
        loadx(s)
        loadw(io["Wk"]); proj_fm(kall, kab, 0)
        loadw(io["Wq"]); proj_fm(qfm, qfb, 0)
        loadw(io["Wv"])
        for t in range(nkT):
            for dc in range(2):
                bank = P.bank()
                for kc in range(8):
                    P.op("pe", lambda h, kc=kc, t=t, dc=dc, bank=bank: h.matmul(
                        out=bank.f32, lhsT=xfm[kc][:, t*128:(t+1)*128], rhs=wsb[kc][:, dc*512:(dc+1)*512],
                        start=(kc == 0), stop=(kc == 7)),
                        reads=[wbufs[kc], xfb[kc]], writes=[bank.buf] if kc == 0 else (), wpart=[bank.buf] if kc > 0 else ())
                P.op("dve", lambda h, t=t, dc=dc, bank=bank: h.tensor_copy(out=vtm[t][:, dc*512:(dc+1)*512], in_=bank.f32),
                     reads=[bank.buf], writes=[vtb[t]] if dc == 0 else (), wpart=[vtb[t]] if dc else ())
                P.bank_free(bank)
        for h in range(H):
            kc8, rb = (h * hd) // 128, (h * hd) % 128
            for which, src_t, srcb, off in ((0, qfm[kc8], qfb[kc8], 0), (1, kall[kc8], kab[kc8], 0)):
                P.op("act", lambda h_, src_t=src_t, srcb=srcb, off=off, rb=rb: h_.activation(
                    out=sqt[0:hd, :], in_=src_t[rb:rb+hd, off:off+T], func=AF.Square),
                    reads=[srcb], writes=[sqb])
                bks = [P.bank(), P.bank()]
                for nt in range(nkT):
                    bank = bks[nt % 2]
                    P.op("pe", lambda h_, nt=nt, bank=bank: h_.matmul(
                        out=bank.f32[0:128, 0:1], lhsT=sqt[0:hd, nt*128:(nt+1)*128], rhs=onec[0:hd, :],
                        start=True, stop=True),
                        reads=[sqb, onecb], writes=[bank.buf])
                    P.op("dve", lambda h_, which=which, nt=nt, bank=bank: h_.tensor_scalar(
                        out=diag[:, which*nkT+nt:which*nkT+nt+1], in0=bank.f32[0:128, 0:1], scalar1=0.5*c_*c_, scalar2=None, op0=ALU.mult),
                        reads=[bank.buf], writes=[diagb] if (which == 0 and nt == 0) else (), wpart=[diagb] if (which or nt) else ())
                P.bank_free(*bks)
            P.op("dve", lambda h_: h_.tensor_scalar(out=diag[:, nkT:2*nkT], in0=diag[:, nkT:2*nkT], scalar1=stab[:, 0:1], scalar2=None, op0=ALU.add),
                 reads=[stabb, diagb], wpart=[diagb])
            if s == 0 and h == 0 and io.get("DBG_DIAG") is not None:
                P.dma("act", lambda h_: h_.dma_start(out=io["DBG_DIAG"], in_=diag[:, :]), reads=[diagb], stream=f"{name}dbg")
                P.dma("act", lambda h_: h_.dma_start(out=io["DBG_QF"], in_=qfm[0][:, :]), reads=[qfb[0]], stream=f"{name}dbg")
                P.dma("act", lambda h_: h_.dma_start(out=io["DBG_SQ"], in_=sqt[0:64, :]), reads=[sqb], stream=f"{name}dbg")
            for nt in range(nkT):
                par = nt % 2
                bank = P.bank()
                P.op("pe", lambda h_, nt=nt, bank=bank, kc8=kc8, rb=rb: h_.matmul(
                    out=bank.f32[0:128, 0:M], lhsT=kall[kc8][rb:rb+hd, nt*128:(nt+1)*128], rhs=(prj if rb == 0 else prjB)[rb:rb+hd, :],
                    start=True, stop=True),
                    reads=[kab[kc8], prjb, prjBb], writes=[bank.buf])
                P.op("pool", lambda h_, nt=nt, par=par: h_.tensor_scalar(out=st[:, 1+par:2+par], in0=diag[:, nkT+nt:nkT+nt+1], scalar1=-1.0, scalar2=None, op0=ALU.mult),
                     reads=[diagb], wpart=[stb])
                P.op("act", lambda h_, nt=nt, par=par, bank=bank: h_.activation(out=kpt[nt][:, :], in_=bank.f32[0:128, 0:M], func=AF.Exp, scale=c_, bias=st[:, 1+par:2+par]),
                     reads=[bank.buf, stb], writes=[kpb[nt]])
                P.op("pool", lambda h_, nt=nt: h_.tensor_scalar(out=kpt[nt][:, :], in0=kpt[nt][:, :], scalar1=ratio, scalar2=ratio*EPS, op0=ALU.mult, op1=ALU.add),
                     reads=[kpb[nt]], wpart=[kpb[nt]])
                P.bank_free(bank)
            for ng in range(ceil_div(nkT, 4)):
                bank2 = P.bank()
                for nt in range(ng*4, min((ng+1)*4, nkT)):
                    bank = P.bank()
                    P.op("pe", lambda h_, nt=nt, bank=bank, kc8=kc8, rb=rb: h_.matmul(
                        out=bank.f32[0:128, 0:M], lhsT=qfm[kc8][rb:rb+hd, nt*128:(nt+1)*128], rhs=(prj if rb == 0 else prjB)[rb:rb+hd, :],
                        start=True, stop=True),
                        reads=[qfb[kc8], prjb, prjBb], writes=[bank.buf])
                    P.op("dve", lambda h_, bank=bank: h_.reduce_max(out=st[:, 0:1], in_=bank.f32[0:128, 0:M], axis=AX.X),
                         reads=[bank.buf], writes=[stb])
                    P.op("dve", lambda h_, nt=nt: h_.scalar_tensor_tensor(out=st[:, 2:3], in0=diag[:, nt:nt+1], scalar=1.0/c_, in1=st[:, 0:1], op0=ALU.mult, op1=ALU.add),
                         reads=[stb, diagb], wpart=[stb])
                    P.op("pe", lambda h_, nt=nt, bank2=bank2: h_.transpose(out=bank2.f32[0:1, (nt % 4)*128:(nt % 4)*128+128], in_=st[:, 2:3], identity=id32[:, :]),
                         reads=[stb, id32b], wpart=[bank2.buf])
                    P.bank_free(bank)
                P.op("dve", lambda h_, ng=ng, bank2=bank2: h_.tensor_copy(out=brf[0:1, ng*512:(ng+1)*512], in_=bank2.f32[0:1, 0:512]),
                     reads=[bank2.buf], writes=[brfb] if ng == 0 else (), wpart=[brfb] if ng else ())
                P.bank_free(bank2)
            P.op("dve", lambda h_, kc8=kc8, rb=rb: h_.tensor_copy(out=qpad[0:hd, :], in_=qfm[kc8][rb:rb+hd, :]),
                 reads=[qfb[kc8]], writes=[qpadb])
            P.op("dve", lambda h_: h_.tensor_scalar(out=qpad2[0:1, :], in0=brf[0:1, :], scalar1=-1.0, scalar2=None, op0=ALU.mult),
                 reads=[brfb], writes=[qpad2b])
            P.op("dve", lambda h_: h_.tensor_copy(out=qpad[64:65, :], in_=qpad2[0:1, :]),
                 reads=[qpad2b], wpart=[qpadb])
            P.op("dve", lambda h_: h_.tensor_add(out=brx[64:65, :], in0=brf[0:1, :], in1=qpad2[0:1, :]),
                 reads=[brfb, qpad2b], writes=[brgb])
            P.op("dve", lambda h_: h_.tensor_scalar(out=qpad2[64:65, :], in0=brx[64:65, :], scalar1=-1.0, scalar2=None, op0=ALU.mult),
                 reads=[brgb], wpart=[qpad2b])
            for mc in range(2):
                for j in range(T // 512):
                    bank = P.bank()
                    P.op("pe", lambda h_, mc=mc, j=j, bank=bank: h_.matmul(
                        out=bank.f32, lhsT=prj[0:65, mc*128:(mc+1)*128], rhs=qpad[0:65, j*512:(j+1)*512],
                        start=True, stop=False),
                        reads=[prjb, qpadb], writes=[bank.buf])
                    P.op("pe", lambda h_, mc=mc, j=j, bank=bank: h_.matmul(
                        out=bank.f32, lhsT=prj[64:65, mc*128:(mc+1)*128], rhs=qpad2[64:65, j*512:(j+1)*512],
                        start=False, stop=True),
                        reads=[prjb, qpad2b], wpart=[bank.buf])
                    P.op("act", lambda h_, mc=mc, j=j, bank=bank: h_.activation(out=qpt[mc][:, j*512:(j+1)*512], in_=bank.f32, func=AF.Exp, scale=c_),
                         reads=[bank.buf], writes=[qptb[mc]] if j == 0 else (), wpart=[qptb[mc]] if j else ())
                    P.bank_free(bank)
                P.op("pool", lambda h_, mc=mc: h_.tensor_scalar(out=qpt[mc][:, :], in0=qpt[mc][:, :], scalar1=ratio, scalar2=ratio*EPS, op0=ALU.mult, op1=ALU.add),
                     reads=[qptb[mc]], wpart=[qptb[mc]])
            for mc in range(2):
                bank = P.bank()
                for nt in range(nkT):
                    P.op("pe", lambda h_, mc=mc, nt=nt, bank=bank, first=(nt == 0), last=(nt == nkT-1): h_.matmul(
                        out=bank.f32[0:128, 0:1], lhsT=kpt[nt][:, mc*128:(mc+1)*128], rhs=onec[0:128, :],
                        start=first, stop=last),
                        reads=[kpb[nt], onecb], writes=[bank.buf] if nt == 0 else (), wpart=[bank.buf] if nt else ())
                P.op("dve", lambda h_, mc=mc, bank=bank: h_.tensor_copy(out=kcs[:, mc:mc+1], in_=bank.f32[0:128, 0:1]),
                     reads=[bank.buf], writes=[kcsb] if mc == 0 else (), wpart=[kcsb] if mc else ())
                P.bank_free(bank)
            for j in range(T // 512):
                bank = P.bank()
                for mc in range(2):
                    P.op("pe", lambda h_, mc=mc, j=j, bank=bank: h_.matmul(
                        out=bank.f32[0:1, 0:512], lhsT=kcs[:, mc:mc+1], rhs=qpt[mc][:, j*512:(j+1)*512],
                        start=(mc == 0), stop=(mc == 1)),
                        reads=[kcsb, qptb[mc]], writes=[bank.buf] if mc == 0 else (), wpart=[bank.buf] if mc else ())
                P.op("dve", lambda h_, j=j, bank=bank: h_.reciprocal(out=brx[0:1, j*512:(j+1)*512], in_=bank.f32[0:1, 0:512]),
                     reads=[bank.buf], writes=[brfb] if j == 0 else (), wpart=[brfb] if j else ())
                P.bank_free(bank)
            for j in range(T // 512):
                bank = P.bank()
                P.op("pe", lambda h_, j=j, bank=bank: h_.matmul(
                    out=bank.f32, lhsT=onesr[0:1, 0:128], rhs=brx[0:1, j*512:(j+1)*512], start=True, stop=True),
                    reads=[onesrb, brfb], writes=[bank.buf])
                P.op("dve", lambda h_, j=j, bank=bank: h_.tensor_copy(out=brep[:, j*512:(j+1)*512], in_=bank.f32),
                     reads=[bank.buf], writes=[brepb] if j == 0 else (), wpart=[brepb] if j else ())
                P.bank_free(bank)
            for mc in range(2):
                P.op("dve", lambda h_, mc=mc: h_.tensor_mul(out=qpt[mc][:, :], in0=qpt[mc][:, :], in1=brep[:, :]),
                     reads=[brepb, qptb[mc]], wpart=[qptb[mc]])
            for mc in range(2):
                bank = P.bank()
                for nt in range(nkT):
                    P.op("pe", lambda h_, mc=mc, nt=nt, bank=bank, h=h, first=(nt == 0), last=(nt == nkT-1): h_.matmul(
                        out=bank.f32[0:128, 0:hd], lhsT=kpt[nt][:, mc*128:(mc+1)*128], rhs=vtm[nt][:, h*hd:(h+1)*hd],
                        start=first, stop=last),
                        reads=[kpb[nt], vtb[nt]], writes=[bank.buf] if nt == 0 else (), wpart=[bank.buf] if nt else ())
                P.op("dve", lambda h_, mc=mc, bank=bank: h_.tensor_copy(out=ctxm[:, mc*hd:(mc+1)*hd], in_=bank.f32[0:128, 0:hd]),
                     reads=[bank.buf], writes=[ctxmb] if mc == 0 else (), wpart=[ctxmb] if mc else ())
                P.bank_free(bank)
            if s == 0 and h == 0 and io.get("DBG_KP0") is not None:
                P.dma("act", lambda h_: h_.dma_start(out=io["DBG_KP0"], in_=kpt[0][:, :]), reads=[kpb[0]], stream=f"{name}dbg")
                P.dma("act", lambda h_: h_.dma_start(out=io["DBG_QP0"], in_=qpt[0][:, :]), reads=[qptb[0]], stream=f"{name}dbg")
                P.dma("act", lambda h_: h_.dma_start(out=io["DBG_KCS"], in_=kcs[:, :]), reads=[kcsb], stream=f"{name}dbg")
                P.dma("act", lambda h_: h_.dma_start(out=io["DBG_CTX"], in_=ctxm[:, :]), reads=[ctxmb], stream=f"{name}dbg")
                P.dma("act", lambda h_: h_.dma_start(out=io["DBG_STAB"], in_=stab[:, :]), reads=[stabb], stream=f"{name}dbg")
                P.dma("act", lambda h_: h_.dma_start(out=io["DBG_BRF"], in_=brx[0:1, :]), reads=[brfb, qpadb], stream=f"{name}dbg")
            for j in range(T // 512):
                bank = P.bank()
                for mc in range(2):
                    P.op("pe", lambda h_, mc=mc, j=j, bank=bank: h_.matmul(
                        out=bank.f32[0:hd, 0:512], lhsT=ctxm[:, mc*hd:(mc+1)*hd], rhs=qpt[mc][:, j*512:(j+1)*512],
                        start=(mc == 0), stop=(mc == 1)),
                        reads=[ctxmb, qptb[mc]], writes=[bank.buf] if mc == 0 else (), wpart=[bank.buf] if mc else ())
                P.op("dve", lambda h_, j=j, bank=bank, kc8=kc8, rb=rb: h_.tensor_copy(out=aofm[kc8][rb:rb+hd, j*512:(j+1)*512], in_=bank.f32[0:hd, 0:512]),
                     reads=[bank.buf], wpart=[aofb[kc8]])
                P.bank_free(bank)
        loadw(io["Wo"])
        for t in range(nkT):
            for dc in range(2):
                bank = P.bank()
                for kc in range(8):
                    P.op("pe", lambda h, kc=kc, t=t, dc=dc, bank=bank: h.matmul(
                        out=bank.f32, lhsT=aofm[kc][:, t*128:(t+1)*128], rhs=wsb[kc][:, dc*512:(dc+1)*512],
                        start=(kc == 0), stop=(kc == 7)),
                        reads=[wbufs[kc], aofb[kc]], writes=[bank.buf] if kc == 0 else (), wpart=[bank.buf] if kc > 0 else ())
                sl = (t * 2 + dc) % 2
                P.op("dve", lambda h, bank=bank, dc=dc, sl=sl: h.tensor_add(out=ostage[sl][:, :], in0=bank.f32, in1=borep[:, dc*512:(dc+1)*512]),
                     reads=[bank.buf, borepb], writes=[ostb[sl]])
                P.dma("act", lambda h, t=t, dc=dc, sl=sl, s=s: h.dma_start(
                    out=OUT[s*T+t*128:s*T+(t+1)*128, dc*512:(dc+1)*512], in_=ostage[sl][:, :]),
                    reads=[ostb[sl]], stream=f"{name}o")
                P.bank_free(bank)
    P.barrier()
    P.release(m0)


def emit_glu(P, C, io, ntok, name="l2f"):
    m0 = P.mark()
    D, Hf, E = 1024, 2048, 8
    Xfm = io["X_fm"]; OUT = io["OUT"]
    ident, identb = C.ident_sb(P)

    gwf = P.salloc(f"{name}_gwf", [128, 8*8], F32)
    gwfb = Buf(f"{name}_gwf")
    gw = P.salloc(f"{name}_gw", [128, 8*8], BF16)
    gwb = Buf(f"{name}_gw")
    gwl = P.salloc(f"{name}_gwl", [128, 8*8], BF16)
    gwlb = Buf(f"{name}_gwl")
    for kc in range(8):
        P.dma("sp", lambda h, kc=kc: h.dma_start(out=gwf[:, kc*8:(kc+1)*8], in_=io["Wg"][kc*128:(kc+1)*128, :]),
              wpart=[gwfb], stream=f"{name}gwf")
    P.op("dve", lambda h: h.tensor_copy(out=gw[:, :], in_=gwf[:, :]), reads=[gwfb], writes=[gwb])
    P.op("dve", lambda h: h.tensor_sub(out=gwl[:, :], in0=gwf[:, :], in1=gw[:, :]), reads=[gwfb, gwb], writes=[gwlb])

    nhalf = 2
    TH = ntok // nhalf
    nt = TH // 128
    xfw = P.salloc(f"{name}_xfw", [128, 8*TH], BF16)
    xfwb = Buf(f"{name}_xfw")
    xf = [xfw[:, kc*TH:(kc+1)*TH] for kc in range(8)]
    xfb = [xfwb for kc in range(8)]
    xl = [P.salloc(f"{name}_xl{kc}", [128, 512], BF16) for kc in range(8)]
    xlb = [Buf(f"{name}_xl{kc}") for kc in range(8)]
    acc = [P.salloc(f"{name}_ac{t}", [128, D], F32) for t in range(nt)]
    accb = [Buf(f"{name}_ac{t}") for t in range(nt)]
    w1w = [P.salloc(f"{name}_w1w{s}", [128, 8192], BF16) for s in range(2)]
    w1wb = [Buf(f"{name}_w1w{s}") for s in range(2)]
    w2w = [P.salloc(f"{name}_w2w{s}", [128, 8192], BF16) for s in range(1)]
    w2wb = [Buf(f"{name}_w2w{s}") for s in range(1)]
    w3w = [P.salloc(f"{name}_w3w{s}", [128, 8192], BF16) for s in range(1)]
    w3wb = [Buf(f"{name}_w3w{s}") for s in range(1)]
    w1s = [[w1w[s][:, kc*1024:(kc+1)*1024] for kc in range(8)] for s in range(2)]
    w1sb = [[w1wb[s] for kc in range(8)] for s in range(2)]
    w2s = [[w2w[0][:, kc*1024:(kc+1)*1024] for kc in range(8)]]
    w2sb = [[w2wb[0] for kc in range(8)]]
    w3s = [[w3w[0][:, kc*1024:(kc+1)*1024] for kc in range(8)]]
    w3sb = [[w3wb[0] for kc in range(8)]]
    usil = P.salloc(f"{name}_usil", [128, 512], BF16)
    usilb = Buf(f"{name}_usil")
    hid = [P.salloc(f"{name}_hid{fc}", [128, 512], BF16) for fc in range(8)]
    hidb = [Buf(f"{name}_hid{fc}") for fc in range(8)]
    comb = P.salloc(f"{name}_comb", [128, 8*nt], F32)
    combb = [Buf(f"{name}_cb{t}") for t in range(nt)]
    gs = P.salloc(f"{name}_gs", [128, 16], F32)
    gsb = Buf(f"{name}_gs")
    ostage = [P.salloc(f"{name}_os{i}", [128, 512], F32) for i in range(4)]
    ostb = [Buf(f"{name}_os{i}") for i in range(4)]

    def load_half(slot, e, fh):
        def l1(h, e=e, fh=fh, slot=slot):
            srcv = io["W1"][e*1024:(e+1)*1024, fh*1024:(fh+1)*1024].rearrange("(kc p) j -> p kc j", p=128)
            return h.dma_start(out=w1w[slot][:, :].rearrange("p (kc j) -> p kc j", kc=8), in_=srcv)
        P.dma("pool", l1, writes=[w1wb[slot]], stream=f"{name}w1")
        def l2(h, e=e, fh=fh):
            srcv = io["W2"][e*1024:(e+1)*1024, fh*1024:(fh+1)*1024].rearrange("(kc p) j -> p kc j", p=128)
            return h.dma_start(out=w2w[0][:, :].rearrange("p (kc j) -> p kc j", kc=8), in_=srcv)
        P.dma("pool", l2, writes=[w2wb[0]], stream=f"{name}w2")
        def l3(h, e=e, fh=fh):
            srcv = io["W3"][e*2048+fh*1024:e*2048+(fh+1)*1024, :].rearrange("(kc p) j -> p kc j", p=128)
            return h.dma_start(out=w3w[0][:, :].rearrange("p (kc j) -> p kc j", kc=8), in_=srcv)
        P.dma("pool", l3, writes=[w3wb[0]], stream=f"{name}w3")

    for half in range(nhalf):
        hoff = half * TH
        def lx(h, hoff=hoff):
            srcv = Xfm.rearrange("(kc p) n -> p kc n", p=128)[:, :, hoff:hoff+TH]
            return h.dma_start(out=xfw[:, :].rearrange("p (kc n) -> p kc n", kc=8), in_=srcv)
        P.dma("sp", lx, writes=[xfwb], stream=f"{name}xf")
        for t in range(nt):
            if t % 4 == 0:
                for kc in range(8):
                    P.dma("sp", lambda h, kc=kc, hoff=hoff, t=t: h.dma_start(
                        out=xl[kc][:, :], in_=io["X_lo_fm"][kc*128:(kc+1)*128, hoff+t*128:hoff+t*128+512]),
                        writes=[xlb[kc]], stream=f"{name}xl")
            xo = (t % 4) * 128
            bank = P.bank()
            for kc in range(8):
                P.op("pe", lambda h, kc=kc, t=t, bank=bank: h.matmul(
                    out=bank.t[0:128, 0:8], lhsT=xf[kc][:, t*128:(t+1)*128], rhs=gw[:, kc*8:(kc+1)*8],
                    start=(kc == 0), stop=False),
                    reads=[xfb[kc], gwb], writes=[bank.buf] if kc == 0 else (), wpart=[bank.buf] if kc > 0 else ())
            for kc in range(8):
                P.op("pe", lambda h, kc=kc, t=t, xo=xo, bank=bank: h.matmul(
                    out=bank.t[0:128, 0:8], lhsT=xl[kc][:, xo:xo+128], rhs=gw[:, kc*8:(kc+1)*8],
                    start=False, stop=False),
                    reads=[xlb[kc], gwb], wpart=[bank.buf])
            for kc in range(8):
                P.op("pe", lambda h, kc=kc, t=t, bank=bank: h.matmul(
                    out=bank.t[0:128, 0:8], lhsT=xf[kc][:, t*128:(t+1)*128], rhs=gwl[:, kc*8:(kc+1)*8],
                    start=False, stop=(kc == 7)),
                    reads=[xfb[kc], gwlb], wpart=[bank.buf])
            gl = comb[:, t*8:(t+1)*8]
            P.op("dve", lambda h, bank=bank, gl=gl: h.tensor_copy(out=gl, in_=bank.t[0:128, 0:8]),
                 reads=[bank.buf], writes=[combb[t]])
            P.bank_free(bank)
            P.op("dve", lambda h, gl=gl: h.reduce_max(out=gs[:, 0:1], in_=gl, axis=AX.X), reads=[combb[t]], writes=[gsb])
            P.op("dve", lambda h, gl=gl: h.tensor_scalar(out=gs[:, 2:10], in0=gl, scalar1=gs[:, 0:1], scalar2=None, op0=ALU.is_ge),
                 reads=[combb[t], gsb], wpart=[gsb])
            P.op("dve", lambda h, gl=gl: h.scalar_tensor_tensor(out=gs[:, 2:10], in0=gs[:, 2:10], scalar=-1e30, in1=gl, op0=ALU.mult, op1=ALU.add),
                 reads=[combb[t], gsb], wpart=[gsb])
            P.op("dve", lambda h: h.reduce_max(out=gs[:, 1:2], in_=gs[:, 2:10], axis=AX.X), reads=[gsb], wpart=[gsb])
            P.op("dve", lambda h: h.tensor_scalar(out=gs[:, 10:11], in0=gs[:, 0:1], scalar1=gs[:, 1:2], scalar2=None, op0=ALU.subtract),
                 reads=[gsb], wpart=[gsb])
            P.op("act", lambda h: h.activation(out=gs[:, 11:12], in_=gs[:, 10:11], func=AF.Sigmoid), reads=[gsb], wpart=[gsb])
            P.op("dve", lambda h: h.tensor_scalar(out=gs[:, 12:13], in0=gs[:, 11:12], scalar1=-1.0, scalar2=1.0, op0=ALU.mult, op1=ALU.add),
                 reads=[gsb], wpart=[gsb])
            P.op("dve", lambda h: h.tensor_scalar(out=gs[:, 2:10], in0=gs[:, 2:10], scalar1=gs[:, 1:2], scalar2=None, op0=ALU.is_ge),
                 reads=[gsb], wpart=[gsb])
            P.op("dve", lambda h, gl=gl: h.tensor_scalar(out=gl, in0=gl, scalar1=gs[:, 0:1], scalar2=None, op0=ALU.is_ge),
                 reads=[gsb], wpart=[combb[t]])
            P.op("dve", lambda h, gl=gl: h.tensor_scalar(out=gl, in0=gl, scalar1=gs[:, 11:12], scalar2=None, op0=ALU.mult),
                 reads=[gsb], wpart=[combb[t]])
            P.op("dve", lambda h, gl=gl: h.scalar_tensor_tensor(out=gl, in0=gs[:, 2:10], scalar=gs[:, 12:13], in1=gl, op0=ALU.mult, op1=ALU.add),
                 reads=[gsb], wpart=[combb[t]])
        for t in range(nt):
            P.op("dve", lambda h, t=t: h.memset(acc[t][:, :], 0.0), writes=[accb[t]])
        for e in range(E):
            for fh in range(2):
                slot = 0
                load_half(0, e, fh)
                for tc in range(TH // 512):
                    for fc in range(8):
                        bank = P.bank()
                        for kc in range(8):
                            P.op("pe", lambda h, kc=kc, fc=fc, tc=tc, slot=slot, bank=bank: h.matmul(
                                out=bank.f32, lhsT=w1s[slot][kc][:, fc*128:(fc+1)*128], rhs=xf[kc][:, tc*512:(tc+1)*512],
                                start=(kc == 0), stop=(kc == 7)),
                                reads=[w1sb[slot][kc], xfb[kc]], writes=[bank.buf] if kc == 0 else (), wpart=[bank.buf] if kc > 0 else ())
                        P.op("act", lambda h, bank=bank: h.activation(out=usil[:, :], in_=bank.f32, func=AF.Silu),
                             reads=[bank.buf], writes=[usilb])
                        P.bank_free(bank)
                        bank = P.bank()
                        for kc in range(8):
                            P.op("pe", lambda h, kc=kc, fc=fc, tc=tc, slot=slot, bank=bank: h.matmul(
                                out=bank.f32, lhsT=w2s[slot][kc][:, fc*128:(fc+1)*128], rhs=xf[kc][:, tc*512:(tc+1)*512],
                                start=(kc == 0), stop=(kc == 7)),
                                reads=[w2sb[slot][kc], xfb[kc]], writes=[bank.buf] if kc == 0 else (), wpart=[bank.buf] if kc > 0 else ())
                        P.op("dve", lambda h, fc=fc, bank=bank: h.tensor_mul(out=hid[fc][:, :], in0=bank.f32, in1=usil[:, :]),
                             reads=[bank.buf, usilb], writes=[hidb[fc]])
                        P.bank_free(bank)
                    for tt in range(4):
                        t = tc * 4 + tt
                        for dc in range(2):
                            bank = P.bank()
                            for fc in range(8):
                                P.op("pe", lambda h, fc=fc, tt=tt, dc=dc, slot=slot, bank=bank: h.matmul(
                                    out=bank.f32, lhsT=hid[fc][:, tt*128:(tt+1)*128], rhs=w3s[slot][fc][:, dc*512:(dc+1)*512],
                                    start=(fc == 0), stop=(fc == 7)),
                                    reads=[hidb[fc], w3sb[slot][fc]], writes=[bank.buf] if fc == 0 else (), wpart=[bank.buf] if fc > 0 else ())
                            P.op("dve", lambda h, t=t, dc=dc, e=e, bank=bank: h.scalar_tensor_tensor(
                                out=acc[t][:, dc*512:(dc+1)*512], in0=bank.f32, scalar=comb[:, t*8+e:t*8+e+1],
                                in1=acc[t][:, dc*512:(dc+1)*512], op0=ALU.mult, op1=ALU.add),
                                reads=[bank.buf, combb[t]], wpart=[accb[t]])
                            P.bank_free(bank)
        for t in range(nt):
            for dc in range(2):
                sl = (t * 2 + dc) % 4
                P.op("dve", lambda h, t=t, dc=dc, sl=sl: h.tensor_copy(out=ostage[sl][:, :], in_=acc[t][:, dc*512:(dc+1)*512]),
                     reads=[accb[t]], writes=[ostb[sl]])
                P.dma("act", lambda h, t=t, dc=dc, sl=sl, hoff=hoff: h.dma_start(
                    out=OUT[hoff+t*128:hoff+(t+1)*128, dc*512:(dc+1)*512], in_=ostage[sl][:, :]),
                    reads=[ostb[sl]], stream=f"{name}o")
    P.barrier()
    P.release(m0)


def emit_l3f_attn(P, C, io, nseq=2, T=2048, name="l3f", eps=1e-6):
    m0 = P.mark()
    D, H, hd = 1024, 16, 64
    scale = 0.125
    nkT = T // 128
    nq4 = T // 512
    vw = H * (hd + 1)
    ident, identb = C.ident_sb(P)
    XN32 = io["XN32"]

    cmA = np.zeros((128, 4*512), np.float32)
    for dq in range(4):
        for qq in range(128):
            lim = dq*128 + qq
            cmA[qq, dq*512+lim+1:(dq+1)*512] = -30000.0
    cA = C.add("c_causalA", cmA, F32)
    cmB = np.zeros((128, 4*512), np.float32)
    for d in range(4):
        for kk in range(128):
            cmB[kk, d*512:d*512 + min(d*128 + kk, 512)] = -30000.0
    cB = C.add("c_causalB", cmB, F32)
    cmtA = P.salloc(f"{name}_cmA", [128, 4*512], F32); cmAb = Buf("cmA")
    P.dma("sp", lambda h: h.dma_start(out=cmtA[:, :], in_=cA), writes=[cmAb], stream="consts")
    cmtB = P.salloc(f"{name}_cmB", [128, 4*512], F32); cmBb = Buf("cmB")
    P.dma("sp", lambda h: h.dma_start(out=cmtB[:, :], in_=cB), writes=[cmBb], stream="consts")
    onesbf = P.salloc(f"{name}_1bf", [128, 128], BF16); onesbfb = Buf("onesbf")
    P.op("dve", lambda h: h.memset(onesbf[0:1, :], 1.0), writes=[onesbfb])

    wsb = [P.salloc(f"{name}_w{kc}", [128, D], BF16) for kc in range(8)]
    wbufs = [Buf(f"{name}_w{kc}") for kc in range(8)]
    anw = P.salloc(f"{name}_anw", [128, D], BF16); anwb = Buf("anw")
    P.dma("pool", lambda h: h.dma_start(out=anw[:, :], in_=io["anw_rep"]), writes=[anwb], stream=f"{name}anw")
    mnw = P.salloc(f"{name}_mnw", [128, D], F32); mnwb = Buf("mnw")
    P.dma("sp", lambda h: h.dma_start(out=mnw[:, :], in_=io["mnw_rep"]), writes=[mnwb], stream="consts")

    tnfm = [P.salloc(f"{name}_tn{kc}", [128, T], BF16) for kc in range(8)]
    tnfb = [Buf(f"{name}_tn{kc}") for kc in range(8)]
    qfm = [P.salloc(f"{name}_qf{kc}", [128, T], BF16) for kc in range(8)]
    qfb = [Buf(f"{name}_qf{kc}") for kc in range(8)]
    kfm = [P.salloc(f"{name}_kf{kc}", [128, T], BF16) for kc in range(8)]
    kfb = [Buf(f"{name}_kf{kc}") for kc in range(8)]
    vau = [P.salloc(f"{name}_va{t}", [128, vw], BF16) for t in range(nkT)]
    vab = [Buf(f"{name}_va{t}") for t in range(nkT)]
    x32 = P.salloc(f"{name}_x32", [128, D], F32); x32b = Buf("x32")
    sq32 = P.salloc(f"{name}_sq32", [128, D], F32); sq32b = Buf("sq32")
    stt = P.salloc(f"{name}_stt", [128, 8], F32); sttb = Buf("stt")
    tnb = P.salloc(f"{name}_tnb", [128, D], BF16); tnbb = Buf("tnb")

    q32 = P.salloc(f"{name}_q32", [128, D], F32); q32b = Buf("q32")
    tA = x32; tAb = x32b
    tB = sq32; tBb = sq32b
    rotb = P.salloc(f"{name}_rot", [128, D], BF16); rotbb = Buf("rot")

    cost = P.salloc(f"{name}_cos", [128, D], BF16); costb = Buf("cos")
    sint = P.salloc(f"{name}_sin", [128, D], BF16); sintb = Buf("sin")
    pt = [P.salloc(f"{name}_pt{i}", [128, 512], BF16) for i in range(2)]
    ptb = [Buf(f"{name}_pt{i}") for i in range(2)]
    mrow = P.salloc(f"{name}_mrow", [128, T], BF16); mrowb = Buf("mrow")
    negm = P.salloc(f"{name}_negm", [128, 2], BF16); negmb = Buf("negm")
    h32 = P.salloc(f"{name}_h32", [128, D], F32); h32b = Buf("h32")
    tn2 = P.salloc(f"{name}_tn232", [128, D], F32); tn2b = Buf("tn232")
    tn2bf = rotb; tn2bfb = rotbb
    tn2lo = tnb; tn2lob = tnbb
    stg0 = P.salloc(f"{name}_stg", [128, D], BF16)
    stg = [stg0, stg0]
    stgb0 = Buf(f"{name}_stg")
    stgb = [stgb0, stgb0]

    def loadw(Wt):
        for kc in range(8):
            P.dma("pool", lambda h, kc=kc, Wt=Wt: h.dma_start(out=wsb[kc][:, :], in_=Wt[kc*128:(kc+1)*128, :]),
                  writes=[wbufs[kc]], stream=f"{name}w")

    for s in range(nseq):
        toff = s * T
        for i in range(nkT):
            P.dma("sp", lambda h, i=i, toff=toff: h.dma_start(out=x32[:, :], in_=XN32[toff+i*128:toff+(i+1)*128, :]),
                  writes=[x32b], stream=f"{name}x")
            P.op("act", lambda h: h.activation(out=sq32[:, :], in_=x32[:, :], func=AF.Square), reads=[x32b], writes=[sq32b])
            P.op("dve", lambda h: h.reduce_sum(out=stt[:, 0:1], in_=sq32[:, :], axis=AX.X), reads=[sq32b], writes=[sttb])
            P.op("dve", lambda h: h.tensor_scalar(out=stt[:, 1:2], in0=stt[:, 0:1], scalar1=1.0/D, scalar2=eps, op0=ALU.mult, op1=ALU.add),
                 reads=[sttb], wpart=[sttb])
            P.op("act", lambda h: h.activation(out=stt[:, 2:3], in_=stt[:, 1:2], func=AF.Sqrt), reads=[sttb], wpart=[sttb])
            P.op("dve", lambda h: h.reciprocal(out=stt[:, 3:4], in_=stt[:, 2:3]), reads=[sttb], wpart=[sttb])
            P.op("dve", lambda h: h.tensor_scalar(out=sq32[:, :], in0=x32[:, :], scalar1=stt[:, 3:4], scalar2=None, op0=ALU.mult),
                 reads=[x32b, sttb], writes=[sq32b])
            P.op("dve", lambda h: h.tensor_mul(out=tnb[:, :], in0=sq32[:, :], in1=anw[:, :]), reads=[sq32b, anwb], writes=[tnbb])
            bank = P.bank()
            for c in range(8):
                P.op("pe", lambda h, c=c, bank=bank: h.transpose(out=bank.bf16[:, c*128:(c+1)*128], in_=tnb[:, c*128:(c+1)*128], identity=ident[:, :]),
                     reads=[tnbb, identb], writes=[bank.buf] if c == 0 else (), wpart=[bank.buf] if c else ())
            for c in range(8):
                P.op("dve", lambda h, c=c, i=i, bank=bank: h.tensor_copy(out=tnfm[c][:, i*128:(i+1)*128], in_=bank.bf16[:, c*128:(c+1)*128]),
                     reads=[bank.buf], wpart=[tnfb[c]])
            P.bank_free(bank)
        for which, Wt, dstfm, dstfb in ((0, io["WqT"], qfm, qfb), (1, io["WkT"], kfm, kfb)):
            loadw(Wt)
            for i in range(nkT):
                P.dma("sp", lambda h, i=i: h.dma_start(out=cost[:, :], in_=io["CosT"][i*128:(i+1)*128, :]),
                      writes=[costb], stream=f"{name}cs")
                P.dma("sp", lambda h, i=i: h.dma_start(out=sint[:, :], in_=io["SinT"][i*128:(i+1)*128, :]),
                      writes=[sintb], stream=f"{name}cs")
                for dc in range(2):
                    bank = P.bank()
                    for kc in range(8):
                        P.op("pe", lambda h, kc=kc, i=i, dc=dc, bank=bank: h.matmul(
                            out=bank.f32, lhsT=tnfm[kc][:, i*128:(i+1)*128], rhs=wsb[kc][:, dc*512:(dc+1)*512],
                            start=(kc == 0), stop=(kc == 7)),
                            reads=[wbufs[kc], tnfb[kc]], writes=[bank.buf] if kc == 0 else (), wpart=[bank.buf] if kc > 0 else ())
                    P.op("dve", lambda h, dc=dc, bank=bank: h.tensor_copy(out=q32[:, dc*512:(dc+1)*512], in_=bank.f32),
                         reads=[bank.buf], writes=[q32b] if dc == 0 else (), wpart=[q32b] if dc else ())
                    P.bank_free(bank)
                def v3(t, half):
                    return t[:, :].rearrange("p (h q) -> p h q", q=hd)[:, :, half*32:(half+1)*32]
                P.op("dve", lambda h: h.tensor_mul(out=v3(tA, 0), in0=v3(q32, 0), in1=v3(cost, 0)), reads=[q32b, costb], writes=[tAb])
                P.op("dve", lambda h: h.tensor_mul(out=v3(tB, 0), in0=v3(q32, 1), in1=v3(sint, 0)), reads=[q32b, sintb], writes=[tBb])
                P.op("dve", lambda h: h.tensor_sub(out=v3(rotb, 0), in0=v3(tA, 0), in1=v3(tB, 0)), reads=[tAb, tBb], writes=[rotbb])
                P.op("dve", lambda h: h.tensor_mul(out=v3(tA, 1), in0=v3(q32, 1), in1=v3(cost, 1)), reads=[q32b, costb], wpart=[tAb])
                P.op("dve", lambda h: h.tensor_mul(out=v3(tB, 1), in0=v3(q32, 0), in1=v3(sint, 1)), reads=[q32b, sintb], wpart=[tBb])
                P.op("dve", lambda h: h.tensor_add(out=v3(rotb, 1), in0=v3(tA, 1), in1=v3(tB, 1)), reads=[tAb, tBb], wpart=[rotbb])
                bank = P.bank()
                for c in range(8):
                    P.op("pe", lambda h, c=c, bank=bank: h.transpose(out=bank.bf16[:, c*128:(c+1)*128], in_=rotb[:, c*128:(c+1)*128], identity=ident[:, :]),
                         reads=[rotbb, identb], writes=[bank.buf] if c == 0 else (), wpart=[bank.buf] if c else ())
                for c in range(8):
                    P.op("dve", lambda h, c=c, i=i, bank=bank, dstfm=dstfm, dstfb=dstfb: h.tensor_copy(
                        out=dstfm[c][:, i*128:(i+1)*128], in_=bank.bf16[:, c*128:(c+1)*128]),
                        reads=[bank.buf], wpart=[dstfb[c]])
                P.bank_free(bank)
        loadw(io["WvT"])
        for t in range(nkT):
            P.op("dve", lambda h, t=t: h.memset(vau[t][:, :], 1.0), writes=[vab[t]])
            for dc in range(2):
                bank = P.bank()
                for kc in range(8):
                    P.op("pe", lambda h, kc=kc, t=t, dc=dc, bank=bank: h.matmul(
                        out=bank.f32, lhsT=tnfm[kc][:, t*128:(t+1)*128], rhs=wsb[kc][:, dc*512:(dc+1)*512],
                        start=(kc == 0), stop=(kc == 7)),
                        reads=[wbufs[kc], tnfb[kc]], writes=[bank.buf] if kc == 0 else (), wpart=[bank.buf] if kc > 0 else ())
                def vevac(h, t=t, dc=dc, bank=bank):
                    dstv = vau[t][:, :].rearrange("p (h e) -> p h e", e=hd+1)
                    src3 = bank.t[0:128, 0:512].rearrange("p (h e) -> p h e", e=hd)
                    return h.tensor_copy(out=dstv[:, (dc*512)//hd:((dc+1)*512)//hd, 0:hd], in_=src3)
                P.op("dve", vevac, reads=[bank.buf], wpart=[vab[t]])
                P.bank_free(bank)
        if T >= 1024:
            per = T // 1024
            ctx = [tnfm[t // per][:, (t % per)*1024:(t % per)*1024+1024] for t in range(nkT)]
            ctxb = [tnfb[t // per] for t in range(nkT)]
        else:
            if s == 0:
                ctx_t = [P.salloc(f"{name}_cx{t}", [128, 1024], BF16) for t in range(nkT)]
                ctx_bufs = [Buf(f"{name}_cx{t}") for t in range(nkT)]
                P._l3f_ctx = (ctx_t, ctx_bufs)
            ctx, ctxb = P._l3f_ctx
        for h in range(H):
            kc8, rb = (h * hd) // 128, (h * hd) % 128
            for qt in range(nkT):
                nkb = qt // 4 + 1
                P.op("dve", lambda h_: h_.memset(stt[:, 4:5], -30000.0), wpart=[sttb])
                for kb in range(nkb):
                    bank = P.bank()
                    P.op("pe", lambda h_, kc8=kc8, rb=rb, qt=qt, kb=kb, bank=bank: h_.matmul(
                        out=bank.f32, lhsT=qfm[kc8][rb:rb+hd, qt*128:(qt+1)*128], rhs=kfm[kc8][rb:rb+hd, kb*512:(kb+1)*512],
                        start=True, stop=True),
                        reads=[qfb[kc8], kfb[kc8]], writes=[bank.buf])
                    if kb == nkb - 1:
                        dq = qt - kb * 4
                        P.op("dve", lambda h_, dq=dq, bank=bank: h_.tensor_add(out=bank.f32, in0=bank.f32, in1=cmtA[:, dq*512:(dq+1)*512]),
                             reads=[cmAb], wpart=[bank.buf])
                    P.op("dve", lambda h_, bank=bank: h_.reduce_max(out=stt[:, 5:6], in_=bank.f32, axis=AX.X),
                         reads=[bank.buf], wpart=[sttb])
                    P.op("dve", lambda h_: h_.tensor_max(out=stt[:, 4:5], in0=stt[:, 4:5], in1=stt[:, 5:6]), wpart=[sttb])
                    P.bank_free(bank)
                P.op("dve", lambda h_: h_.tensor_scalar(out=negm[:, 0:1], in0=stt[:, 4:5], scalar1=-1.0, scalar2=None, op0=ALU.mult),
                     reads=[sttb], writes=[negmb])
                bank = P.bank()
                P.op("pe", lambda h_, bank=bank: h_.transpose(out=bank.bf16[0:1, 0:128], in_=negm[:, 0:1], identity=ident[:, :]),
                     reads=[negmb, identb], writes=[bank.buf])
                P.op("dve", lambda h_, qt=qt, bank=bank: h_.tensor_copy(out=mrow[0:1, qt*128:(qt+1)*128], in_=bank.bf16[0:1, 0:128]),
                     reads=[bank.buf], writes=[mrowb] if qt == 0 else (), wpart=[mrowb] if qt else ())
                P.bank_free(bank)
            for q4 in range(nq4):
                nk0 = (q4 + 1) * 4
                pvb = [P.bank() for _ in range(4)]
                sbk = [P.bank(), P.bank()]
                def emit_S3(k0, kc8=kc8, rb=rb, q4=q4):
                    sb = sbk[k0 % 2]
                    sl = k0 % 2
                    P.op("pe", lambda h_, kc8=kc8, rb=rb, k0=k0, q4=q4, sb=sb: h_.matmul(
                        out=sb.f32, lhsT=kfm[kc8][rb:rb+hd, k0*128:(k0+1)*128], rhs=qfm[kc8][rb:rb+hd, q4*512:(q4+1)*512],
                        start=True, stop=False),
                        reads=[kfb[kc8], qfb[kc8]], writes=[sb.buf])
                    P.op("pe", lambda h_, k0=k0, q4=q4, sb=sb: h_.matmul(
                        out=sb.f32, lhsT=onesbf[0:1, 0:128], rhs=mrow[0:1, q4*512:(q4+1)*512],
                        start=False, stop=True),
                        reads=[onesbfb, mrowb], wpart=[sb.buf])
                    if k0 >= q4 * 4:
                        dl = k0 - q4 * 4
                        P.op("dve", lambda h_, dl=dl, sb=sb: h_.tensor_add(out=sb.f32, in0=sb.f32, in1=cmtB[:, dl*512:(dl+1)*512]),
                             reads=[cmBb], wpart=[sb.buf])
                    P.op("act", lambda h_, sb=sb, sl=sl: h_.activation(out=pt[sl][:, :], in_=sb.f32, func=AF.Exp, scale=scale),
                         reads=[sb.buf], writes=[ptb[sl]])
                emit_S3(0)
                for k0 in range(nk0):
                    if k0 + 1 < nk0:
                        emit_S3(k0 + 1)
                    sl = k0 % 2
                    first = (k0 == 0)
                    last = (k0 == nk0 - 1)
                    for qq in range(4):
                        bqq = pvb[qq]
                        P.op("pe", lambda h_, sl=sl, qq=qq, k0=k0, hh=h, first=first, last=last, b=bqq: h_.matmul(
                            out=b.t[0:128, 0:hd+1], lhsT=pt[sl][:, qq*128:(qq+1)*128], rhs=vau[k0][:, hh*(hd+1):(hh+1)*(hd+1)],
                            start=first, stop=last),
                            reads=[ptb[sl], vab[k0]], writes=[bqq.buf] if first else (), wpart=[bqq.buf] if not first else ())
                for qq in range(4):
                    t = q4 * 4 + qq
                    b = pvb[qq]
                    P.op("dve", lambda h_, b=b: h_.tensor_copy(out=stt[:, 6:7], in_=b.t[0:128, hd:hd+1]),
                         reads=[b.buf], wpart=[sttb])
                    P.op("dve", lambda h_: h_.reciprocal(out=stt[:, 7:8], in_=stt[:, 6:7]), reads=[sttb], wpart=[sttb])
                    P.op("dve", lambda h_, b=b, t=t, hh=h: h_.tensor_scalar(
                        out=ctx[t][:, hh*hd:(hh+1)*hd], in0=b.t[0:128, 0:hd], scalar1=stt[:, 7:8], scalar2=None, op0=ALU.mult),
                        reads=[b.buf, sttb], wpart=[ctxb[t]])
                P.bank_free(*pvb, *sbk)
        cfm = qfm
        cfb = qfb
        for t in range(nkT):
            bank = P.bank()
            for c in range(8):
                P.op("pe", lambda h, c=c, t=t, bank=bank: h.transpose(
                    out=bank.bf16[:, c*128:(c+1)*128], in_=ctx[t][:, c*128:(c+1)*128], identity=ident[:, :]),
                    reads=[ctxb[t], identb], writes=[bank.buf] if c == 0 else (), wpart=[bank.buf] if c else ())
            for c in range(8):
                P.op("dve", lambda h, c=c, t=t, bank=bank: h.tensor_copy(out=cfm[c][:, t*128:(t+1)*128], in_=bank.bf16[:, c*128:(c+1)*128]),
                     reads=[bank.buf], wpart=[cfb[c]])
            P.bank_free(bank)
        loadw(io["WoT"])
        for t in range(nkT):
            banks = [P.bank(), P.bank()]
            for dc in range(2):
                bank = banks[dc]
                for kc in range(8):
                    P.op("pe", lambda h, kc=kc, t=t, dc=dc, bank=bank: h.matmul(
                        out=bank.f32, lhsT=cfm[kc][:, t*128:(t+1)*128], rhs=wsb[kc][:, dc*512:(dc+1)*512],
                        start=(kc == 0), stop=(kc == 7)),
                        reads=[wbufs[kc], cfb[kc]], writes=[bank.buf] if kc == 0 else (), wpart=[bank.buf] if kc > 0 else ())
            P.dma("sp", lambda h, t=t, toff=toff: h.dma_start(out=x32[:, :], in_=XN32[toff+t*128:toff+(t+1)*128, :]),
                  writes=[x32b], stream=f"{name}x")
            for dc in range(2):
                bank = banks[dc]
                P.op("dve", lambda h, dc=dc, bank=bank: h.tensor_add(out=h32[:, dc*512:(dc+1)*512], in0=bank.f32, in1=x32[:, dc*512:(dc+1)*512]),
                     reads=[bank.buf, x32b], writes=[h32b] if dc == 0 else (), wpart=[h32b] if dc else ())
                P.bank_free(bank)
            P.dma("act", lambda h, t=t, toff=toff: h.dma_start(out=io["H32"][toff+t*128:toff+(t+1)*128, :], in_=h32[:, :]),
                  reads=[h32b], stream=f"{name}h")
            P.op("act", lambda h: h.activation(out=sq32[:, :], in_=h32[:, :], func=AF.Square), reads=[h32b], writes=[sq32b])
            P.op("dve", lambda h: h.reduce_sum(out=stt[:, 0:1], in_=sq32[:, :], axis=AX.X), reads=[sq32b], wpart=[sttb])
            P.op("dve", lambda h: h.tensor_scalar(out=stt[:, 1:2], in0=stt[:, 0:1], scalar1=1.0/D, scalar2=eps, op0=ALU.mult, op1=ALU.add),
                 reads=[sttb], wpart=[sttb])
            P.op("act", lambda h: h.activation(out=stt[:, 2:3], in_=stt[:, 1:2], func=AF.Sqrt), reads=[sttb], wpart=[sttb])
            P.op("dve", lambda h: h.reciprocal(out=stt[:, 3:4], in_=stt[:, 2:3]), reads=[sttb], wpart=[sttb])
            P.op("dve", lambda h: h.tensor_scalar(out=sq32[:, :], in0=h32[:, :], scalar1=stt[:, 3:4], scalar2=None, op0=ALU.mult),
                 reads=[h32b, sttb], writes=[sq32b])
            P.op("dve", lambda h: h.tensor_mul(out=tn2[:, :], in0=sq32[:, :], in1=mnw[:, :]), reads=[sq32b, mnwb], writes=[tn2b])
            P.op("dve", lambda h: h.tensor_copy(out=tn2bf[:, :], in_=tn2[:, :]), reads=[tn2b], writes=[tn2bfb])
            P.op("dve", lambda h: h.tensor_sub(out=tn2lo[:, :], in0=tn2[:, :], in1=tn2bf[:, :]), reads=[tn2b, tn2bfb], writes=[tn2lob])
            for which2, srct, srctb, dram in ((0, tn2bf, tn2bfb, io["TN2_fm"]), (1, tn2lo, tn2lob, io["TN2_lo_fm"])):
                bank = P.bank()
                for c in range(8):
                    P.op("pe", lambda h, c=c, bank=bank, srct=srct, srctb=srctb: h.transpose(out=bank.bf16[:, c*128:(c+1)*128], in_=srct[:, c*128:(c+1)*128], identity=ident[:, :]),
                         reads=[srctb, identb], writes=[bank.buf] if c == 0 else (), wpart=[bank.buf] if c else ())
                P.op("dve", lambda h, bank=bank, which2=which2: h.tensor_copy(out=stg[which2][:, :], in_=bank.bf16[:, :]),
                     reads=[bank.buf], writes=[stgb[which2]])
                P.bank_free(bank)
                def fmdma(h, t=t, toff=toff, which2=which2, dram=dram):
                    dst = dram.rearrange("(c p) n -> p c n", p=128)[:, :, toff+t*128:toff+(t+1)*128]
                    srcv = stg[which2][:, :].rearrange("p (c j) -> p c j", c=8)
                    return h.dma_start(out=dst, in_=srcv)
                P.dma("act", fmdma, reads=[stgb[which2]], stream=f"{name}fm")
    P.barrier()
    P.release(m0)


def emit_add(P, C, OUT, srcs, ntok, name="add", barrier=True, release=True):
    m0 = P.mark()
    nt = ntok // 128
    a = [P.salloc(f"{name}_a{i}", [128, 1024], F32) for i in range(2)]
    ab = [Buf(f"{name}_a{i}") for i in range(2)]
    b = [P.salloc(f"{name}_b{i}", [128, 1024], F32) for i in range(2)]
    bb = [Buf(f"{name}_b{i}") for i in range(2)]
    for t in range(nt):
        sl = t % 2
        P.dma("sp", lambda h, t=t, sl=sl: h.dma_start(out=a[sl][:, :], in_=srcs[0](t)), writes=[ab[sl]], stream=f"{name}s0")
        for j in range(1, len(srcs)):
            P.dma("sp", lambda h, t=t, sl=sl, j=j: h.dma_start(out=b[sl][:, :], in_=srcs[j](t)), writes=[bb[sl]], stream=f"{name}s{j}")
            P.op("dve", lambda h, sl=sl: h.tensor_add(out=a[sl][:, :], in0=a[sl][:, :], in1=b[sl][:, :]),
                 reads=[bb[sl]], wpart=[ab[sl]])
        P.dma("act", lambda h, t=t, sl=sl: h.dma_start(out=OUT[t*128:(t+1)*128, :], in_=a[sl][:, :]),
              reads=[ab[sl]], stream=f"{name}o")
    if barrier:
        P.barrier()
    if release:
        P.release(m0)
    return m0



def emit_l0f_mid(P, C, io, ntok=2048, name="l0f", eps=1e-5):
    m0 = P.mark()
    D, H, hd = 1024, 16, 64
    nt = ntok // 128
    vw = H * (hd + 1)
    ident, identb = C.ident_sb(P)
    scale = 0.125

    mq = P.salloc(f"{name}_mq", [128, 128], BF16); mqb = Buf("mq")
    mk_ = P.salloc(f"{name}_mk", [128, 128], BF16); mkb = Buf("mk")
    P.dma("pool", lambda h: h.dma_start(out=mq[0:16, :], in_=io["MaskQ"]), writes=[mqb], stream=f"{name}msk")
    P.dma("pool", lambda h: h.dma_start(out=mk_[0:16, :], in_=io["MaskK"]), writes=[mkb], stream=f"{name}msk")
    onesbf = P.salloc(f"{name}_1bf", [128, 128], BF16); onesbfb = Buf("onesbf0")
    P.op("dve", lambda h: h.memset(onesbf[0:1, :], 1.0), writes=[onesbfb])

    wsb = [P.salloc(f"{name}_w{kc}", [128, D], BF16) for kc in range(8)]
    wbufs = [Buf(f"{name}_w{kc}") for kc in range(8)]
    xfm = [P.salloc(f"{name}_xf{kc}", [128, ntok], BF16) for kc in range(8)]
    xfb = [Buf(f"{name}_xf{kc}") for kc in range(8)]
    qfm = [P.salloc(f"{name}_qf{kc}", [128, ntok], BF16) for kc in range(8)]
    qfb = [Buf(f"{name}_qf{kc}") for kc in range(8)]
    kfm = [P.salloc(f"{name}_kf{kc}", [128, ntok], BF16) for kc in range(8)]
    kfb = [Buf(f"{name}_kf{kc}") for kc in range(8)]
    vau = [P.salloc(f"{name}_va{t}", [128, vw], BF16) for t in range(nt)]
    vab = [Buf(f"{name}_va{t}") for t in range(nt)]
    per = ntok // 1024
    ctx = [xfm[t // per][:, (t % per)*1024:(t % per)*1024+1024] for t in range(nt)]
    ctxb = [xfb[t // per] for t in range(nt)]
    pt2 = P.salloc(f"{name}_pt", [128, 128], BF16); pt2b = Buf("pt")
    mrow = P.salloc(f"{name}_mrow", [128, 128], BF16); mrowb = Buf("mrow")
    negm = P.salloc(f"{name}_negm", [128, 1], BF16); negmb = Buf("negm")
    stt = P.salloc(f"{name}_stt", [128, 8], F32); sttb = Buf("stt")
    bpart = {}
    for nm in ("bq", "bk"):
        bt = P.salloc(f"{name}_{nm}", [128, 8], F32); bb2 = Buf(nm)
        P.dma("sp", lambda h, bt=bt, nm=nm: h.dma_start(out=bt[:, :], in_=io[nm]), writes=[bb2], stream="consts")
        bpart[nm] = (bt, bb2)
    bvrep = P.salloc(f"{name}_bv", [128, D], F32); bvrepb = Buf("bv")
    P.dma("sp", lambda h: h.dma_start(out=bvrep[:, :], in_=io["bv_rep"]), writes=[bvrepb], stream="consts")
    borep = P.salloc(f"{name}_bo", [128, D], F32); borepb = Buf("bo")
    P.dma("sp", lambda h: h.dma_start(out=borep[:, :], in_=io["bo_rep"]), writes=[borepb], stream="consts")
    g1 = P.salloc(f"{name}_g1", [128, D], F32); g1b = Buf("g1")
    P.dma("sp", lambda h: h.dma_start(out=g1[:, :], in_=io["g1_rep"]), writes=[g1b], stream="consts")
    b1r = P.salloc(f"{name}_b1r", [128, D], F32); b1rb = Buf("b1r")
    P.dma("sp", lambda h: h.dma_start(out=b1r[:, :], in_=io["b1_rep"]), writes=[b1rb], stream="consts")
    x32 = P.salloc(f"{name}_x32", [128, D], F32); x32b = Buf("x320")
    z32 = P.salloc(f"{name}_z32", [128, D], F32); z32b = Buf("z320")
    sq32 = P.salloc(f"{name}_sq32", [128, D], F32); sq32b = Buf("sq320")
    xbf = P.salloc(f"{name}_xbf", [128, D], BF16); xbfb = Buf("xbf0")
    xlo = P.salloc(f"{name}_xlo", [128, D], BF16); xlob = Buf("xlo0")
    stg0 = P.salloc(f"{name}_stg", [128, D], BF16)
    stg = [stg0, stg0]
    stgb0 = Buf(f"{name}_stg")
    stgb = [stgb0, stgb0]
    ost = [P.salloc(f"{name}_ost{i}", [128, D], F32) for i in range(2)]
    ostb = [Buf(f"{name}_ost{i}") for i in range(2)]

    def loadw(Wt):
        for kc in range(8):
            P.dma("pool", lambda h, kc=kc, Wt=Wt: h.dma_start(out=wsb[kc][:, :], in_=Wt[kc*128:(kc+1)*128, :]),
                  writes=[wbufs[kc]], stream=f"{name}w")

    for kc in range(8):
        P.dma("sp", lambda h, kc=kc: h.dma_start(out=xfm[kc][:, :], in_=io["XNp_fm"][kc*128:(kc+1)*128, :]),
              writes=[xfb[kc]], stream=f"{name}xf")

    def proj_fm(bias, dst, dstb):
        for oc in range(8):
            for t4 in range(ntok // 512):
                bank = P.bank()
                for kc in range(8):
                    P.op("pe", lambda h, kc=kc, oc=oc, t4=t4, bank=bank: h.matmul(
                        out=bank.f32, lhsT=wsb[kc][:, oc*128:(oc+1)*128], rhs=xfm[kc][:, t4*512:(t4+1)*512],
                        start=(kc == 0), stop=(kc == 7)),
                        reads=[wbufs[kc], xfb[kc]], writes=[bank.buf] if kc == 0 else (), wpart=[bank.buf] if kc > 0 else ())
                bt, bb2 = bpart[bias]
                P.op("act", lambda h, oc=oc, t4=t4, bank=bank, bt=bt: h.activation(
                    out=dst[oc][:, t4*512:(t4+1)*512], in_=bank.f32, func=AF.Identity, bias=bt[:, oc:oc+1]),
                    reads=[bank.buf, bb2], wpart=[dstb[oc]])
                P.bank_free(bank)

    loadw(io["WqT"]); proj_fm("bq", qfm, qfb)
    loadw(io["WkT"]); proj_fm("bk", kfm, kfb)
    loadw(io["WvT"])
    for t in range(nt):
        P.op("dve", lambda h, t=t: h.memset(vau[t][:, :], 1.0), writes=[vab[t]])
        for dc in range(2):
            bank = P.bank()
            for kc in range(8):
                P.op("pe", lambda h, kc=kc, t=t, dc=dc, bank=bank: h.matmul(
                    out=bank.f32, lhsT=xfm[kc][:, t*128:(t+1)*128], rhs=wsb[kc][:, dc*512:(dc+1)*512],
                    start=(kc == 0), stop=(kc == 7)),
                    reads=[wbufs[kc], xfb[kc]], writes=[bank.buf] if kc == 0 else (), wpart=[bank.buf] if kc > 0 else ())
            def vevac(h, t=t, dc=dc, bank=bank):
                dstv = vau[t][:, :].rearrange("p (h e) -> p h e", e=hd+1)
                src3 = bank.t[0:128, 0:512].rearrange("p (h e) -> p h e", e=hd)
                b3 = bvrep[:, dc*512:(dc+1)*512].rearrange("p (h e) -> p h e", e=hd)
                return h.tensor_add(out=dstv[:, (dc*512)//hd:((dc+1)*512)//hd, 0:hd], in0=src3, in1=b3)
            P.op("dve", vevac, reads=[bank.buf, bvrepb], wpart=[vab[t]])
            P.bank_free(bank)
    iters = [(blk, h) for blk in range(nt) for h in range(H)]
    pt2x = [P.salloc(f"{name}_pt2{i}", [128, 128], BF16) for i in range(2)]
    pt2xb = [Buf(f"{name}_pt2{i}") for i in range(2)]
    mrowx = [P.salloc(f"{name}_mr{i}", [128, 128], BF16) for i in range(2)]
    mrowxb = [Buf(f"{name}_mr{i}") for i in range(2)]
    negmx = [P.salloc(f"{name}_ng{i}", [128, 1], BF16) for i in range(2)]
    negmxb = [Buf(f"{name}_ng{i}") for i in range(2)]
    state = {}
    def stage1(i):
        blk, h = iters[i]
        par = i % 2
        kc8, rb = (h * hd) // 128, (h * hd) % 128
        bank = P.bank()
        P.op("pe", lambda h_, kc8=kc8, rb=rb, blk=blk, bank=bank: h_.matmul(
            out=bank.f32[0:128, 0:128], lhsT=qfm[kc8][rb:rb+hd, blk*128:(blk+1)*128], rhs=kfm[kc8][rb:rb+hd, blk*128:(blk+1)*128],
            start=True, stop=False),
            reads=[qfb[kc8], kfb[kc8]], writes=[bank.buf])
        P.op("pe", lambda h_, bank=bank: h_.matmul(
            out=bank.f32[0:128, 0:128], lhsT=mq[0:9, :], rhs=mk_[0:9, :], start=False, stop=True),
            reads=[mqb, mkb], wpart=[bank.buf])
        P.op("dve", lambda h_, bank=bank, par=par: h_.reduce_max(out=stt[:, par:par+1], in_=bank.f32[0:128, 0:128], axis=AX.X),
             reads=[bank.buf], wpart=[sttb])
        P.bank_free(bank)
        P.op("dve", lambda h_, par=par: h_.tensor_scalar(out=negmx[par][:, 0:1], in0=stt[:, par:par+1], scalar1=-1.0, scalar2=None, op0=ALU.mult),
             reads=[sttb], writes=[negmxb[par]])
    def stage2(i):
        blk, h = iters[i]
        par = i % 2
        bank = P.bank()
        P.op("pe", lambda h_, bank=bank, par=par: h_.transpose(out=bank.bf16[0:1, 0:128], in_=negmx[par][:, 0:1], identity=ident[:, :]),
             reads=[negmxb[par], identb], writes=[bank.buf])
        P.op("dve", lambda h_, bank=bank, par=par: h_.tensor_copy(out=mrowx[par][0:1, :], in_=bank.bf16[0:1, 0:128]),
             reads=[bank.buf], writes=[mrowxb[par]])
        P.bank_free(bank)
    def stage3(i):
        blk, h = iters[i]
        par = i % 2
        kc8, rb = (h * hd) // 128, (h * hd) % 128
        bank = P.bank()
        P.op("pe", lambda h_, kc8=kc8, rb=rb, blk=blk, bank=bank: h_.matmul(
            out=bank.f32[0:128, 0:128], lhsT=kfm[kc8][rb:rb+hd, blk*128:(blk+1)*128], rhs=qfm[kc8][rb:rb+hd, blk*128:(blk+1)*128],
            start=True, stop=False),
            reads=[qfb[kc8], kfb[kc8]], writes=[bank.buf])
        P.op("pe", lambda h_, bank=bank: h_.matmul(
            out=bank.f32[0:128, 0:128], lhsT=mq[0:9, :], rhs=mk_[0:9, :], start=False, stop=False),
            reads=[mqb, mkb], wpart=[bank.buf])
        P.op("pe", lambda h_, bank=bank, par=par: h_.matmul(
            out=bank.f32[0:128, 0:128], lhsT=onesbf[0:1, :], rhs=mrowx[par][0:1, :], start=False, stop=True),
            reads=[onesbfb, mrowxb[par]], wpart=[bank.buf])
        P.op("act", lambda h_, bank=bank, par=par: h_.activation(out=pt2x[par][:, :], in_=bank.f32[0:128, 0:128], func=AF.Exp, scale=scale),
             reads=[bank.buf], writes=[pt2xb[par]])
        P.bank_free(bank)
    def stage4(i):
        blk, h = iters[i]
        par = i % 2
        bank = P.bank()
        P.op("pe", lambda h_, blk=blk, hh=h, bank=bank, par=par: h_.matmul(
            out=bank.t[0:128, 0:hd+1], lhsT=pt2x[par][:, :], rhs=vau[blk][:, hh*(hd+1):(hh+1)*(hd+1)],
            start=True, stop=True),
            reads=[pt2xb[par], vab[blk]], writes=[bank.buf])
        P.op("dve", lambda h_, bank=bank, par=par: h_.tensor_copy(out=stt[:, 2+par:3+par], in_=bank.t[0:128, hd:hd+1]),
             reads=[bank.buf], wpart=[sttb])
        P.op("dve", lambda h_, par=par: h_.reciprocal(out=stt[:, 4+par:5+par], in_=stt[:, 2+par:3+par]), reads=[sttb], wpart=[sttb])
        P.op("dve", lambda h_, blk=blk, hh=h, bank=bank, par=par: h_.tensor_scalar(
            out=ctx[blk][:, hh*hd:(hh+1)*hd], in0=bank.t[0:128, 0:hd], scalar1=stt[:, 4+par:5+par], scalar2=None, op0=ALU.mult),
            reads=[bank.buf, sttb], wpart=[ctxb[blk]])
        P.bank_free(bank)
    n_it = len(iters)
    for i in range(n_it + 3):
        if i < n_it: stage1(i)
        if 1 <= i < n_it + 1: stage2(i - 1)
        if 2 <= i < n_it + 2: stage3(i - 2)
        if 3 <= i < n_it + 3: stage4(i - 3)
    cfm = qfm
    cfb = qfb
    for t in range(nt):
        bank = P.bank()
        for c in range(8):
            P.op("pe", lambda h, c=c, t=t, bank=bank: h.transpose(
                out=bank.bf16[:, c*128:(c+1)*128], in_=ctx[t][:, c*128:(c+1)*128], identity=ident[:, :]),
                reads=[ctxb[t], identb], writes=[bank.buf] if c == 0 else (), wpart=[bank.buf] if c else ())
        for c in range(8):
            P.op("dve", lambda h, c=c, t=t, bank=bank: h.tensor_copy(out=cfm[c][:, t*128:(t+1)*128], in_=bank.bf16[:, c*128:(c+1)*128]),
                 reads=[bank.buf], wpart=[cfb[c]])
        P.bank_free(bank)
    loadw(io["WoT"])
    for t in range(nt):
        banks = [P.bank(), P.bank()]
        for dc in range(2):
            bank = banks[dc]
            for kc in range(8):
                P.op("pe", lambda h, kc=kc, t=t, dc=dc, bank=bank: h.matmul(
                    out=bank.f32, lhsT=cfm[kc][:, t*128:(t+1)*128], rhs=wsb[kc][:, dc*512:(dc+1)*512],
                    start=(kc == 0), stop=(kc == 7)),
                    reads=[wbufs[kc], cfb[kc]], writes=[bank.buf] if kc == 0 else (), wpart=[bank.buf] if kc > 0 else ())
        P.dma("sp", lambda h, t=t: h.dma_start(out=x32[:, :], in_=io["XNp_32"][t*128:(t+1)*128, :]),
              writes=[x32b], stream=f"{name}x")
        for dc in range(2):
            bank = banks[dc]
            P.op("dve", lambda h, dc=dc, bank=bank: h.tensor_add(out=z32[:, dc*512:(dc+1)*512], in0=bank.f32, in1=x32[:, dc*512:(dc+1)*512]),
                 reads=[bank.buf, x32b], writes=[z32b] if dc == 0 else (), wpart=[z32b] if dc else ())
            P.op("dve", lambda h, dc=dc, bank=bank: h.tensor_add(out=z32[:, dc*512:(dc+1)*512], in0=z32[:, dc*512:(dc+1)*512], in1=borep[:, dc*512:(dc+1)*512]),
                 reads=[borepb], wpart=[z32b])
            P.bank_free(bank)
        P.op("dve", lambda h: h.reduce_sum(out=stt[:, 0:1], in_=z32[:, :], axis=AX.X), reads=[z32b], writes=[sttb])
        P.op("dve", lambda h: h.tensor_scalar(out=stt[:, 1:2], in0=stt[:, 0:1], scalar1=1.0/D, scalar2=None, op0=ALU.mult),
             reads=[sttb], wpart=[sttb])
        P.op("dve", lambda h: h.tensor_scalar(out=sq32[:, :], in0=z32[:, :], scalar1=stt[:, 1:2], scalar2=None, op0=ALU.subtract),
             reads=[z32b, sttb], writes=[sq32b])
        P.op("act", lambda h: h.activation(out=z32[:, :], in_=sq32[:, :], func=AF.Square), reads=[sq32b], writes=[z32b])
        P.op("dve", lambda h: h.reduce_sum(out=stt[:, 2:3], in_=z32[:, :], axis=AX.X), reads=[z32b], wpart=[sttb])
        P.op("dve", lambda h: h.tensor_scalar(out=stt[:, 3:4], in0=stt[:, 2:3], scalar1=1.0/D, scalar2=eps, op0=ALU.mult, op1=ALU.add),
             reads=[sttb], wpart=[sttb])
        P.op("act", lambda h: h.activation(out=stt[:, 4:5], in_=stt[:, 3:4], func=AF.Sqrt), reads=[sttb], wpart=[sttb])
        P.op("dve", lambda h: h.reciprocal(out=stt[:, 5:6], in_=stt[:, 4:5]), reads=[sttb], wpart=[sttb])
        P.op("dve", lambda h: h.tensor_scalar(out=sq32[:, :], in0=sq32[:, :], scalar1=stt[:, 5:6], scalar2=None, op0=ALU.mult),
             reads=[sttb], wpart=[sq32b])
        P.op("dve", lambda h: h.tensor_mul(out=sq32[:, :], in0=sq32[:, :], in1=g1[:, :]), reads=[g1b], wpart=[sq32b])
        P.op("dve", lambda h: h.tensor_add(out=sq32[:, :], in0=sq32[:, :], in1=b1r[:, :]), reads=[b1rb], wpart=[sq32b])
        sl = t % 2
        P.op("dve", lambda h, sl=sl: h.tensor_copy(out=ost[sl][:, :], in_=sq32[:, :]), reads=[sq32b], writes=[ostb[sl]])
        P.dma("act", lambda h, t=t, sl=sl: h.dma_start(out=io["X1_32"][t*128:(t+1)*128, :], in_=ost[sl][:, :]),
              reads=[ostb[sl]], stream=f"{name}h")
        P.op("dve", lambda h: h.tensor_copy(out=xbf[:, :], in_=sq32[:, :]), reads=[sq32b], writes=[xbfb])
        P.op("dve", lambda h: h.tensor_sub(out=xlo[:, :], in0=sq32[:, :], in1=xbf[:, :]), reads=[sq32b, xbfb], writes=[xlob])
        for which2, srct, srctb, dram in ((0, xbf, xbfb, io["X1_fm"]), (1, xlo, xlob, io["X1_lo_fm"])):
            bank = P.bank()
            for c in range(8):
                P.op("pe", lambda h, c=c, bank=bank, srct=srct, srctb=srctb: h.transpose(out=bank.bf16[:, c*128:(c+1)*128], in_=srct[:, c*128:(c+1)*128], identity=ident[:, :]),
                     reads=[srctb, identb], writes=[bank.buf] if c == 0 else (), wpart=[bank.buf] if c else ())
            P.op("dve", lambda h, bank=bank, which2=which2: h.tensor_copy(out=stg[which2][:, :], in_=bank.bf16[:, :]),
                 reads=[bank.buf], writes=[stgb[which2]])
            P.bank_free(bank)
            def fmdma(h, t=t, which2=which2, dram=dram):
                dst = dram.rearrange("(c p) n -> p c n", p=128)[:, :, t*128:(t+1)*128]
                srcv = stg[which2][:, :].rearrange("p (c j) -> p c j", c=8)
                return h.dma_start(out=dst, in_=srcv)
            P.dma("act", fmdma, reads=[stgb[which2]], stream=f"{name}fm")
    P.barrier()
    P.release(m0)



def emit_l0f_fin(P, C, io, name="l0ffin", eps=1e-5):
    m0 = P.mark()
    D = 1024
    nt = 32
    g2 = P.salloc(f"{name}_g", [128, D], F32); g2b = Buf("g2")
    P.dma("sp", lambda h: h.dma_start(out=g2[:, :], in_=io["g_rep"]), writes=[g2b], stream="consts")
    b2r = P.salloc(f"{name}_b", [128, D], F32); b2rb = Buf("b2")
    P.dma("sp", lambda h: h.dma_start(out=b2r[:, :], in_=io["b_rep"]), writes=[b2rb], stream="consts")
    a32 = P.salloc(f"{name}_a32", [128, D], F32); a32b = Buf("a32f")
    z32 = P.salloc(f"{name}_z32", [128, D], F32); z32b = Buf("z32f")
    sq32 = P.salloc(f"{name}_sq", [128, D], F32); sq32b = Buf("sq32f")
    stt = P.salloc(f"{name}_st", [128, 8], F32); sttb = Buf("sttf")
    ost = [P.salloc(f"{name}_o{i}", [128, D], F32) for i in range(2)]
    ostb = [Buf(f"{name}_o{i}") for i in range(2)]
    A2A_in = io["A2A_in"]; A2A_out = io["A2A_out"]
    ainb, aoutb = Buf("a2ain"), Buf("a2aout")
    for t in range(nt):
        P.dma("sp", lambda h, t=t: h.dma_start(out=a32[:, :], in_=io["A32"][t*128:(t+1)*128, :]), writes=[a32b], stream=f"{name}a")
        P.dma("sp", lambda h, t=t: h.dma_start(out=z32[:, :], in_=io["B32"][t*128:(t+1)*128, :]), writes=[z32b], stream=f"{name}b")
        P.op("dve", lambda h: h.tensor_add(out=z32[:, :], in0=z32[:, :], in1=a32[:, :]), reads=[a32b], wpart=[z32b])
        P.op("dve", lambda h: h.reduce_sum(out=stt[:, 0:1], in_=z32[:, :], axis=AX.X), reads=[z32b], writes=[sttb])
        P.op("dve", lambda h: h.tensor_scalar(out=stt[:, 1:2], in0=stt[:, 0:1], scalar1=1.0/D, scalar2=None, op0=ALU.mult),
             reads=[sttb], wpart=[sttb])
        P.op("dve", lambda h: h.tensor_scalar(out=sq32[:, :], in0=z32[:, :], scalar1=stt[:, 1:2], scalar2=None, op0=ALU.subtract),
             reads=[z32b, sttb], writes=[sq32b])
        P.op("act", lambda h: h.activation(out=z32[:, :], in_=sq32[:, :], func=AF.Square), reads=[sq32b], writes=[z32b])
        P.op("dve", lambda h: h.reduce_sum(out=stt[:, 2:3], in_=z32[:, :], axis=AX.X), reads=[z32b], wpart=[sttb])
        P.op("dve", lambda h: h.tensor_scalar(out=stt[:, 3:4], in0=stt[:, 2:3], scalar1=1.0/D, scalar2=eps, op0=ALU.mult, op1=ALU.add),
             reads=[sttb], wpart=[sttb])
        P.op("act", lambda h: h.activation(out=stt[:, 4:5], in_=stt[:, 3:4], func=AF.Sqrt), reads=[sttb], wpart=[sttb])
        P.op("dve", lambda h: h.reciprocal(out=stt[:, 5:6], in_=stt[:, 4:5]), reads=[sttb], wpart=[sttb])
        P.op("dve", lambda h: h.tensor_scalar(out=sq32[:, :], in0=sq32[:, :], scalar1=stt[:, 5:6], scalar2=None, op0=ALU.mult),
             reads=[sttb], wpart=[sq32b])
        P.op("dve", lambda h: h.tensor_mul(out=sq32[:, :], in0=sq32[:, :], in1=g2[:, :]), reads=[g2b], wpart=[sq32b])
        sl = t % 2
        P.op("dve", lambda h, sl=sl: h.tensor_add(out=ost[sl][:, :], in0=sq32[:, :], in1=b2r[:, :]), reads=[sq32b, b2rb], writes=[ostb[sl]])
        def scdma(h, t=t, sl=sl):
            R = 1024 * 4
            dst = A2A_in.rearrange("(sh sl2 b) n -> b sh sl2 n", sh=8, sl2=2)[t*8:(t+1)*8, :, :, :]
            return h.dma_start(out=dst, in_=ost[sl][:, :])
        P.dma("act", scdma, reads=[ostb[sl]], wpart=[ainb], stream=f"{name}sc")
    def ccop(h):
        from concourse.aot_env import allow_ici_all_to_all
        with allow_ici_all_to_all():
            return h.collective_compute("AllToAll", ALU.bypass, replica_groups=[list(range(8))],
                                        ins=[A2A_in.opt()], outs=[A2A_out.opt()])
    P.op("pool", ccop, reads=[ainb], writes=[aoutb])
    P.barrier()
    P.release(m0)
    return aoutb


def _part8(b):
    return np.ascontiguousarray(b.reshape(8, 128).T.astype(np.float32))

def _rep(b):
    return np.ascontiguousarray(np.broadcast_to(np.asarray(b, np.float32), (128, len(b))))

def _T(w):
    return np.ascontiguousarray(np.asarray(w, np.float32).T)


def prep_all(inp):
    g = {}
    g["L0aWqT"] = _T(inp["L0_attn_Wq"]); g["L0aWkT"] = _T(inp["L0_attn_Wk"])
    g["L0aWvT"] = _T(inp["L0_attn_Wv"]); g["L0aWoT"] = _T(inp["L0_attn_Wo"])
    g["L0abq"] = _part8(inp["L0_attn_bq"]); g["L0abk"] = _part8(inp["L0_attn_bk"])
    g["L0abv"] = _rep(inp["L0_attn_bv"]); g["L0abo"] = _rep(inp["L0_attn_bo"])
    ipw = np.asarray(inp["L0_ffn_in_proj_w"], np.float32)
    ipb = np.asarray(inp["L0_ffn_in_proj_b"], np.float32)
    g["L0fWqT"] = _T(ipw[0:1024]); g["L0fWkT"] = _T(ipw[1024:2048]); g["L0fWvT"] = _T(ipw[2048:3072])
    g["L0fbq"] = _part8(ipb[0:1024]); g["L0fbk"] = _part8(ipb[1024:2048]); g["L0fbv"] = _rep(ipb[2048:3072])
    g["L0fWoT"] = _T(inp["L0_ffn_out_w"]); g["L0fbo"] = _rep(inp["L0_ffn_out_b"])
    g["L0fg1"] = _rep(inp["L0_ffn_ln1_g"]); g["L0fb1"] = _rep(inp["L0_ffn_ln1_b"])
    g["L0fg2"] = _rep(inp["L0_ffn_ln2_g"]); g["L0fb2"] = _rep(inp["L0_ffn_ln2_b"])
    g["L0fgwT"] = _T(inp["L0_ffn_gate_w"])
    g["L0fgb"] = _rep(inp["L0_ffn_gate_b"])
    w1 = np.asarray(inp["L0_ffn_w1"], np.float32)
    w2 = np.asarray(inp["L0_ffn_w2"], np.float32)
    g["L0fW1T"] = np.ascontiguousarray(w1.transpose(0, 2, 1).reshape(8192, 4096))
    g["L0fW2T"] = np.ascontiguousarray(w2.transpose(0, 2, 1).reshape(32768, 1024))
    b1 = np.asarray(inp["L0_ffn_b1"], np.float32)
    g["L0fB1"] = np.ascontiguousarray(b1.reshape(8, 32, 128).transpose(0, 2, 1).reshape(1024, 32))
    g["L0fB2"] = np.ascontiguousarray(np.asarray(inp["L0_ffn_b2"], np.float32))
    MQ = np.zeros((16, 128), np.float32); MK = np.zeros((16, 128), np.float32)
    MQ[0] = 1.0; MK[0] = -1000.0
    for j in range(8):
        MQ[1+j] = (np.arange(128)//16 == j).astype(np.float32)
        MK[1+j] = 1000.0*(np.arange(128)//16 == j)
    g["L0fMQ"] = MQ; g["L0fMK"] = MK
    wqkv = np.asarray(inp["L1_attn_w_qkv"], np.float32)
    g["L1aWqT"] = np.ascontiguousarray(wqkv[:, 0:1024])
    g["L1aWkT"] = np.ascontiguousarray(wqkv[:, 1024:2048])
    g["L1aWvT"] = np.ascontiguousarray(wqkv[:, 2048:3072])
    tp = prep_tree(np.asarray(inp["L1_ffn_Wq"], np.float32), np.asarray(inp["L1_ffn_routers"], np.float32),
                   np.asarray(inp["L1_ffn_leaf_values"], np.float32), np.asarray(inp["L1_ffn_Wout_w"], np.float32))
    g["L1fWqT"] = tp["WqT"]; g["L1fRoutT"] = tp["RoutT"]; g["L1fLV"] = tp["LV"]; g["L1fWoutT"] = tp["WoutT"]
    g["L1fbo"] = _rep(inp["L1_ffn_Wout_b"])
    g["L2aWq"] = np.ascontiguousarray(np.asarray(inp["L2_attn_wq"], np.float32))
    g["L2aWk"] = np.ascontiguousarray(np.asarray(inp["L2_attn_wk"], np.float32))
    g["L2aWv"] = np.ascontiguousarray(np.asarray(inp["L2_attn_wv"], np.float32))
    g["L2aWo"] = np.ascontiguousarray(np.asarray(inp["L2_attn_wo"], np.float32))
    g["L2abo"] = _rep(inp["L2_attn_bo"])
    proj = np.asarray(inp["L2_attn_proj"], np.float32)
    pp = np.zeros((128, 256), np.float32); pp[0:64] = proj.T; pp[64] = 1.0; pp[65] = 1.0
    ppB = np.zeros((128, 256), np.float32); ppB[64:128] = proj.T
    g["L2aPP"] = pp; g["L2aPPB"] = ppB
    g["L2fWg"] = np.ascontiguousarray(np.asarray(inp["L2_ffn_Wg"], np.float32))
    g["L2fW1"] = np.ascontiguousarray(np.asarray(inp["L2_ffn_W1"], np.float32).reshape(8192, 2048))
    g["L2fW2"] = np.ascontiguousarray(np.asarray(inp["L2_ffn_W2"], np.float32).reshape(8192, 2048))
    g["L2fW3"] = np.ascontiguousarray(np.asarray(inp["L2_ffn_W3"], np.float32).reshape(16384, 1024))
    wq3 = np.asarray(inp["L3_attn_W_qkv"], np.float32)
    bq3 = np.asarray(inp["L3_attn_b_qkv"], np.float32)
    g["L3aWqT"] = np.ascontiguousarray(wq3[:, 0:1024]); g["L3aWkT"] = np.ascontiguousarray(wq3[:, 1024:2048])
    g["L3aWvT"] = np.ascontiguousarray(wq3[:, 2048:3072])
    g["L3abq"] = _part8(bq3[0:1024]); g["L3abk"] = _part8(bq3[1024:2048]); g["L3abv"] = _rep(bq3[2048:3072])
    g["L3aWoT"] = np.ascontiguousarray(np.asarray(inp["L3_attn_W_out"], np.float32))
    g["L3abo"] = _rep(inp["L3_attn_b_out"])
    g["L3fWqT"] = _T(inp["L3_ffn_wq"]); g["L3fWkT"] = _T(inp["L3_ffn_wk"])
    g["L3fWvT"] = _T(inp["L3_ffn_wv"]); g["L3fWoT"] = _T(inp["L3_ffn_wo"])
    g["L3fanw"] = _rep(inp["L3_ffn_attn_norm_w"]); g["L3fmnw"] = _rep(inp["L3_ffn_moe_norm_w"])
    g["L3fgwT"] = _T(inp["L3_ffn_gate_w"])
    w13 = np.asarray(inp["L3_ffn_w1"], np.float32)
    w23 = np.asarray(inp["L3_ffn_w2"], np.float32)
    g["L3fW1T"] = np.ascontiguousarray(w13.transpose(0, 2, 1).reshape(8192, 4096))
    g["L3fW2T"] = np.ascontiguousarray(w23.transpose(0, 2, 1).reshape(32768, 1024))
    b13 = np.asarray(inp["L3_ffn_b1"], np.float32)
    g["L3fB1"] = np.ascontiguousarray(b13.reshape(8, 32, 128).transpose(0, 2, 1).reshape(1024, 32))
    g["L3fB2"] = np.ascontiguousarray(np.asarray(inp["L3_ffn_b2"], np.float32))
    hd = 64
    inv_freq = 1.0 / (10000.0 ** (np.arange(0, hd, 2) / hd))
    freqs = np.arange(2048)[:, None] * inv_freq[None, :]
    emb = np.concatenate([freqs, freqs], axis=1)
    g["L3fCos"] = np.ascontiguousarray(np.tile(np.cos(emb), (1, 16))).astype(BF16NP)
    g["L3fSin"] = np.ascontiguousarray(np.tile(np.sin(emb), (1, 16))).astype(BF16NP)
    return g


def dram(nc, name, shape, dtype, kind):
    return nc.dram_tensor(name, list(shape), dtype, kind=kind).ap()


def build_layer0(nc, ext_io="both"):
    kin = "ExternalInput"
    kout = "ExternalOutput"
    kint = "Internal"
    P = Prog(nc)
    C = Consts(nc)
    C.ident_sb(P)
    Xown = dram(nc, "Xown", [4096, 1024], F32, kin)
    Xpos = dram(nc, "Xpos", [4096, 1024], F32, kin)
    X1 = dram(nc, "X1", [4096, 1024], F32, kout)
    w = {}
    for nm, sh in [("L0aWqT",[1024,1024]),("L0aWkT",[1024,1024]),("L0aWvT",[1024,1024]),("L0aWoT",[1024,1024]),
                   ("L0abq",[128,8]),("L0abk",[128,8]),("L0abv",[128,1024]),("L0abo",[128,1024]),
                   ("L0fWqT",[1024,1024]),("L0fWkT",[1024,1024]),("L0fWvT",[1024,1024]),
                   ("L0fbq",[128,8]),("L0fbk",[128,8]),("L0fbv",[128,1024]),
                   ("L0fWoT",[1024,1024]),("L0fbo",[128,1024]),
                   ("L0fg1",[128,1024]),("L0fb1",[128,1024]),("L0fg2",[128,1024]),("L0fb2",[128,1024]),
                   ("L0fgwT",[1024,8]),("L0fgb",[128,8]),
                   ("L0fW1T",[8192,4096]),("L0fW2T",[32768,1024]),("L0fB1",[1024,32]),("L0fB2",[8,1024]),
                   ("L0fMQ",[16,128]),("L0fMK",[16,128])]:
        w[nm] = dram(nc, nm, sh, F32, kin)
    XNfm = dram(nc, "XNfm", [1024, 4096], BF16, kint)
    A0 = dram(nc, "A0", [4096, 1024], F32, kint)
    XNp_fm = dram(nc, "XNp_fm", [1024, 4096], BF16, kint)
    XNp_32 = dram(nc, "XNp_32", [4096, 1024], F32, kint)
    X1fm = dram(nc, "X1fm", [1024, 4096], BF16, kint)
    X1lo = dram(nc, "X1lo", [1024, 4096], BF16, kint)
    X132 = dram(nc, "X132", [4096, 1024], F32, kint)
    MOE0 = dram(nc, "MOE0", [4096, 1024], F32, kint)
    A2Ain = dram(nc, "A2Ain", [4096, 1024], F32, kint)
    A2Aout = dram(nc, "A2Aout", [4096, 1024], F32, kint)

    emit_ln(P, C, Xown, None, XNfm, 4096, name="ln0")
    emit_attn(P, C, {"XN_fm": XNfm, "OUT": A0, "WqT": w["L0aWqT"], "WkT": w["L0aWkT"], "WvT": w["L0aWvT"],
                     "WoT": w["L0aWoT"], "bq": w["L0abq"], "bk": w["L0abk"], "bv_rep": w["L0abv"], "bo_rep": w["L0abo"]},
              nheads=16, hd=64, causal=True, scale=0.125, name="l0a")
    emit_ln(P, C, Xpos, None, XNp_fm, 4096, name="lnp", XN_tm32=XNp_32)
    for half in range(2):
        hof = half * 2048
        io2 = {"XNp_fm": XNp_fm[:, hof:hof+2048], "XNp_32": XNp_32[hof:hof+2048, :],
               "WqT": w["L0fWqT"], "WkT": w["L0fWkT"], "WvT": w["L0fWvT"], "WoT": w["L0fWoT"],
               "bq": w["L0fbq"], "bk": w["L0fbk"], "bv_rep": w["L0fbv"], "bo_rep": w["L0fbo"],
               "g1_rep": w["L0fg1"], "b1_rep": w["L0fb1"], "MaskQ": w["L0fMQ"], "MaskK": w["L0fMK"],
               "X1_fm": X1fm[:, hof:hof+2048], "X1_lo_fm": X1lo[:, hof:hof+2048], "X1_32": X132[hof:hof+2048, :]}
        emit_l0f_mid(P, C, io2, ntok=2048, name=f"l0f{half}")
    emit_moe(P, C, {"X_fm": X1fm, "X_lo_fm": X1lo, "OUT": MOE0, "gate_wT": w["L0fgwT"], "gate_b_rep": w["L0fgb"],
                    "W1T": w["L0fW1T"], "W2T": w["L0fW2T"], "B1": w["L0fB1"], "B2": w["L0fB2"]},
             4096, func=AF.Relu, name="moe0")
    emit_l0f_fin(P, C, {"A32": MOE0, "B32": X132, "g_rep": w["L0fg2"], "b_rep": w["L0fb2"],
                        "A2A_in": A2Ain, "A2A_out": A2Aout}, name="l0fin")

    def src_x(t):
        return Xown[t*128:(t+1)*128, :]
    def src_a(t):
        return A0[t*128:(t+1)*128, :]
    def src_f(t):
        s_loc = t // 16
        pi = t % 16
        c = pi // 2
        b0 = (pi % 2) * 128
        r0 = c*512 + s_loc*256 + b0
        return A2Aout[r0:r0+128, :]
    emit_add(P, C, X1, [src_x, src_a, src_f], 4096, name="add0")
    P.finalize()
    return P, C


def build_layer1(nc):
    kin, kout, kint = "ExternalInput", "ExternalOutput", "Internal"
    P = Prog(nc); C = Consts(nc); C.ident_sb(P)
    X1 = dram(nc, "X1", [4096, 1024], F32, kin)
    X2 = dram(nc, "X2", [4096, 1024], F32, kout)
    w = {}
    for nm, sh in [("L1aWqT",[1024,1024]),("L1aWkT",[1024,1024]),("L1aWvT",[1024,1024]),
                   ("L1fWqT",[1024,1024]),("L1fRoutT",[1024,256]),("L1fLV",[256,1024]),
                   ("L1fWoutT",[1024,1024]),("L1fbo",[128,1024])]:
        w[nm] = dram(nc, nm, sh, F32, kin)
    XNfm = dram(nc, "XNfm", [1024, 4096], BF16, kint)
    A1 = dram(nc, "A1", [4096, 1024], F32, kint)
    F1 = dram(nc, "F1", [4096, 1024], F32, kint)
    emit_ln(P, C, X1, None, XNfm, 4096, name="ln1")
    emit_attn(P, C, {"XN_fm": XNfm, "OUT": A1, "WqT": w["L1aWqT"], "WkT": w["L1aWkT"], "WvT": w["L1aWvT"]},
              nheads=16, hd=64, causal=False, scale=1.0/32, name="l1a")
    emit_tree(P, C, {"XN_fm": XNfm, "OUT": F1, "WqT": w["L1fWqT"], "RoutT": w["L1fRoutT"], "LV": w["L1fLV"],
                     "WoutT": w["L1fWoutT"], "bo_rep": w["L1fbo"]}, 4096, name="l1f")
    emit_add(P, C, X2, [lambda t: X1[t*128:(t+1)*128, :], lambda t: A1[t*128:(t+1)*128, :], lambda t: F1[t*128:(t+1)*128, :]], 4096, name="add1")
    P.finalize()
    return P, C


def build_layer2(nc):
    kin, kout, kint = "ExternalInput", "ExternalOutput", "Internal"
    P = Prog(nc); C = Consts(nc); C.ident_sb(P)
    X2 = dram(nc, "X2", [4096, 1024], F32, kin)
    X3 = dram(nc, "X3", [4096, 1024], F32, kout)
    w = {}
    for nm, sh in [("L2aWq",[1024,1024]),("L2aWk",[1024,1024]),("L2aWv",[1024,1024]),("L2aWo",[1024,1024]),
                   ("L2abo",[128,1024]),("L2aPP",[128,256]),("L2aPPB",[128,256]),
                   ("L2fWg",[1024,8]),("L2fW1",[8192,2048]),("L2fW2",[8192,2048]),("L2fW3",[16384,1024])]:
        w[nm] = dram(nc, nm, sh, F32, kin)
    XNfm = dram(nc, "XNfm", [1024, 4096], BF16, kint)
    XNlo = dram(nc, "XNlo", [1024, 4096], BF16, kint)
    A2 = dram(nc, "A2", [4096, 1024], F32, kint)
    F2 = dram(nc, "F2", [4096, 1024], F32, kint)
    emit_ln(P, C, X2, None, XNfm, 4096, name="ln2", XN_lo_fm=XNlo)
    emit_perf(P, C, {"XN_fm": XNfm, "OUT": A2, "Wq": w["L2aWq"], "Wk": w["L2aWk"], "Wv": w["L2aWv"], "Wo": w["L2aWo"],
                     "bo_rep": w["L2abo"], "ProjPadT": w["L2aPP"], "ProjPadB": w["L2aPPB"]}, name="l2a")
    emit_glu(P, C, {"X_fm": XNfm, "X_lo_fm": XNlo, "OUT": F2, "Wg": w["L2fWg"],
                    "W1": w["L2fW1"], "W2": w["L2fW2"], "W3": w["L2fW3"]}, 4096, name="l2f")
    emit_add(P, C, X3, [lambda t: X2[t*128:(t+1)*128, :], lambda t: A2[t*128:(t+1)*128, :], lambda t: F2[t*128:(t+1)*128, :]], 4096, name="add2")
    P.finalize()
    return P, C


def build_layer3(nc):
    kin, kout, kint = "ExternalInput", "ExternalOutput", "Internal"
    P = Prog(nc); C = Consts(nc); C.ident_sb(P)
    X3 = dram(nc, "X3", [4096, 1024], F32, kin)
    X4 = dram(nc, "X4", [4096, 1024], F32, kout)
    w = {}
    for nm, sh in [("L3aWqT",[1024,1024]),("L3aWkT",[1024,1024]),("L3aWvT",[1024,1024]),("L3aWoT",[1024,1024]),
                   ("L3abq",[128,8]),("L3abk",[128,8]),("L3abv",[128,1024]),("L3abo",[128,1024]),
                   ("L3fWqT",[1024,1024]),("L3fWkT",[1024,1024]),("L3fWvT",[1024,1024]),("L3fWoT",[1024,1024]),
                   ("L3fanw",[128,1024]),("L3fmnw",[128,1024]),("L3fgwT",[1024,8]),
                   ("L3fW1T",[8192,4096]),("L3fW2T",[32768,1024]),("L3fB1",[1024,32]),("L3fB2",[8,1024])]:
        w[nm] = dram(nc, nm, sh, F32, kin)
    for nm in ["L3fCos", "L3fSin"]:
        w[nm] = dram(nc, nm, [2048, 1024], BF16, kin)
    XNfm = dram(nc, "XNfm", [1024, 4096], BF16, kint)
    XN32 = dram(nc, "XN32", [4096, 1024], F32, kint)
    A3 = dram(nc, "A3", [4096, 1024], F32, kint)
    H32 = dram(nc, "H32", [4096, 1024], F32, kint)
    TN2 = dram(nc, "TN2", [1024, 4096], BF16, kint)
    TN2L = dram(nc, "TN2L", [1024, 4096], BF16, kint)
    MOE3 = dram(nc, "MOE3", [4096, 1024], F32, kint)
    emit_ln(P, C, X3, None, XNfm, 4096, name="ln3", XN_tm32=XN32)
    emit_attn(P, C, {"XN_fm": XNfm, "OUT": A3, "WqT": w["L3aWqT"], "WkT": w["L3aWkT"], "WvT": w["L3aWvT"],
                     "WoT": w["L3aWoT"], "bq": w["L3abq"], "bk": w["L3abk"], "bv_rep": w["L3abv"], "bo_rep": w["L3abo"]},
              nheads=1, hd=1024, causal=False, scale=1.0/32, name="l3a")
    emit_l3f_attn(P, C, {"XN32": XN32, "H32": H32, "TN2_fm": TN2, "TN2_lo_fm": TN2L,
                         "WqT": w["L3fWqT"], "WkT": w["L3fWkT"], "WvT": w["L3fWvT"], "WoT": w["L3fWoT"],
                         "anw_rep": w["L3fanw"], "mnw_rep": w["L3fmnw"], "CosT": w["L3fCos"], "SinT": w["L3fSin"]},
                  name="l3f")
    emit_moe(P, C, {"X_fm": TN2, "X_lo_fm": TN2L, "OUT": MOE3, "gate_wT": w["L3fgwT"], "gate_b_rep": None,
                    "W1T": w["L3fW1T"], "W2T": w["L3fW2T"], "B1": w["L3fB1"], "B2": w["L3fB2"]},
             4096, func=AF.Gelu, name="moe3")
    emit_add(P, C, X4, [lambda t: X3[t*128:(t+1)*128, :], lambda t: A3[t*128:(t+1)*128, :],
                        lambda t: H32[t*128:(t+1)*128, :], lambda t: MOE3[t*128:(t+1)*128, :]], 4096, name="add3")
    P.finalize()
    return P, C

def build_full(nc):
    kin, kout, kint = "ExternalInput", "ExternalOutput", "Internal"
    P = Prog(nc)
    C = Consts(nc)
    C.ident_sb(P)
    Xown = dram(nc, "Xown", [4096, 1024], F32, kin)
    Xpos = dram(nc, "Xpos", [4096, 1024], F32, kin)
    X4 = dram(nc, "X4", [4096, 1024], F32, kout)
    w = {}
    WSHAPES = [
        ("L0aWqT",[1024,1024]),("L0aWkT",[1024,1024]),("L0aWvT",[1024,1024]),("L0aWoT",[1024,1024]),
        ("L0abq",[128,8]),("L0abk",[128,8]),("L0abv",[128,1024]),("L0abo",[128,1024]),
        ("L0fWqT",[1024,1024]),("L0fWkT",[1024,1024]),("L0fWvT",[1024,1024]),
        ("L0fbq",[128,8]),("L0fbk",[128,8]),("L0fbv",[128,1024]),
        ("L0fWoT",[1024,1024]),("L0fbo",[128,1024]),
        ("L0fg1",[128,1024]),("L0fb1",[128,1024]),("L0fg2",[128,1024]),("L0fb2",[128,1024]),
        ("L0fgwT",[1024,8]),("L0fgb",[128,8]),
        ("L0fW1T",[8192,4096]),("L0fW2T",[32768,1024]),("L0fB1",[1024,32]),("L0fB2",[8,1024]),
        ("L0fMQ",[16,128]),("L0fMK",[16,128]),
        ("L1aWqT",[1024,1024]),("L1aWkT",[1024,1024]),("L1aWvT",[1024,1024]),
        ("L1fWqT",[1024,1024]),("L1fRoutT",[1024,256]),("L1fLV",[256,1024]),
        ("L1fWoutT",[1024,1024]),("L1fbo",[128,1024]),
        ("L2aWq",[1024,1024]),("L2aWk",[1024,1024]),("L2aWv",[1024,1024]),("L2aWo",[1024,1024]),
        ("L2abo",[128,1024]),("L2aPP",[128,256]),("L2aPPB",[128,256]),
        ("L2fWg",[1024,8]),("L2fW1",[8192,2048]),("L2fW2",[8192,2048]),("L2fW3",[16384,1024]),
        ("L3aWqT",[1024,1024]),("L3aWkT",[1024,1024]),("L3aWvT",[1024,1024]),("L3aWoT",[1024,1024]),
        ("L3abq",[128,8]),("L3abk",[128,8]),("L3abv",[128,1024]),("L3abo",[128,1024]),
        ("L3fWqT",[1024,1024]),("L3fWkT",[1024,1024]),("L3fWvT",[1024,1024]),("L3fWoT",[1024,1024]),
        ("L3fanw",[128,1024]),("L3fmnw",[128,1024]),("L3fgwT",[1024,8]),
        ("L3fW1T",[8192,4096]),("L3fW2T",[32768,1024]),("L3fB1",[1024,32]),("L3fB2",[8,1024]),
    ]
    for nm, sh in WSHAPES:
        w[nm] = dram(nc, nm, sh, F32, kin)
    for nm in ["L3fCos", "L3fSin"]:
        w[nm] = dram(nc, nm, [2048, 1024], BF16, kin)
    X1 = dram(nc, "X1", [4096, 1024], F32, kint)
    X2 = dram(nc, "X2", [4096, 1024], F32, kint)
    X3 = dram(nc, "X3", [4096, 1024], F32, kint)
    XNfm = dram(nc, "XNfm", [1024, 4096], BF16, kint)
    XNlo = dram(nc, "XNlo", [1024, 4096], BF16, kint)
    XN32 = dram(nc, "XN32", [4096, 1024], F32, kint)
    Abr = dram(nc, "Abr", [4096, 1024], F32, kint)
    Fbr = dram(nc, "Fbr", [4096, 1024], F32, kint)
    XNp_fm = dram(nc, "XNp_fm", [1024, 4096], BF16, kint)
    XNp_32 = dram(nc, "XNp_32", [4096, 1024], F32, kint)
    X1fm = dram(nc, "X1fm", [1024, 4096], BF16, kint)
    X1lo = dram(nc, "X1lo", [1024, 4096], BF16, kint)
    X132 = dram(nc, "X132", [4096, 1024], F32, kint)
    MOE = dram(nc, "MOE", [4096, 1024], F32, kint)
    A2Ain = dram(nc, "A2Ain", [4096, 1024], F32, kint)
    A2Aout = dram(nc, "A2Aout", [4096, 1024], F32, kint)
    H32 = dram(nc, "H32", [4096, 1024], F32, kint)
    TN2 = dram(nc, "TN2", [1024, 4096], BF16, kint)
    TN2L = dram(nc, "TN2L", [1024, 4096], BF16, kint)

    emit_ln(P, C, Xown, None, XNfm, 4096, name="ln0")
    emit_attn(P, C, {"XN_fm": XNfm, "OUT": Abr, "WqT": w["L0aWqT"], "WkT": w["L0aWkT"], "WvT": w["L0aWvT"],
                     "WoT": w["L0aWoT"], "bq": w["L0abq"], "bk": w["L0abk"], "bv_rep": w["L0abv"], "bo_rep": w["L0abo"]},
              nheads=16, hd=64, causal=True, scale=0.125, name="l0a")
    emit_ln(P, C, Xpos, None, XNp_fm, 4096, name="lnp", XN_tm32=XNp_32)
    for half in range(2):
        hof = half * 2048
        io2 = {"XNp_fm": XNp_fm[:, hof:hof+2048], "XNp_32": XNp_32[hof:hof+2048, :],
               "WqT": w["L0fWqT"], "WkT": w["L0fWkT"], "WvT": w["L0fWvT"], "WoT": w["L0fWoT"],
               "bq": w["L0fbq"], "bk": w["L0fbk"], "bv_rep": w["L0fbv"], "bo_rep": w["L0fbo"],
               "g1_rep": w["L0fg1"], "b1_rep": w["L0fb1"], "MaskQ": w["L0fMQ"], "MaskK": w["L0fMK"],
               "X1_fm": X1fm[:, hof:hof+2048], "X1_lo_fm": X1lo[:, hof:hof+2048], "X1_32": X132[hof:hof+2048, :]}
        emit_l0f_mid(P, C, io2, ntok=2048, name=f"l0f{half}")
    emit_moe(P, C, {"X_fm": X1fm, "X_lo_fm": X1lo, "OUT": MOE, "gate_wT": w["L0fgwT"], "gate_b_rep": w["L0fgb"],
                    "W1T": w["L0fW1T"], "W2T": w["L0fW2T"], "B1": w["L0fB1"], "B2": w["L0fB2"]},
             4096, func=AF.Relu, name="moe0")
    def src_x(t):
        return Xown[t*128:(t+1)*128, :]
    def src_a(t):
        return Abr[t*128:(t+1)*128, :]
    def src_f(t):
        s_loc = t // 16
        pi = t % 16
        c = pi // 2
        b0 = (pi % 2) * 128
        r0 = c*512 + s_loc*256 + b0
        return A2Aout[r0:r0+128, :]
    m_add = emit_add(P, C, X2, [src_x, src_a], 4096, name="add0a", barrier=False, release=False)
    emit_l0f_fin(P, C, {"A32": MOE, "B32": X132, "g_rep": w["L0fg2"], "b_rep": w["L0fb2"],
                        "A2A_in": A2Ain, "A2A_out": A2Aout}, name="l0fin")
    P.release(m_add)
    emit_add(P, C, X1, [lambda t: X2[t*128:(t+1)*128, :], src_f], 4096, name="add0b")
    P.new_epoch()
    emit_ln(P, C, X1, None, XNfm, 4096, name="ln1")
    emit_attn(P, C, {"XN_fm": XNfm, "OUT": Abr, "WqT": w["L1aWqT"], "WkT": w["L1aWkT"], "WvT": w["L1aWvT"]},
              nheads=16, hd=64, causal=False, scale=1.0/32, name="l1a")
    emit_tree(P, C, {"XN_fm": XNfm, "OUT": Fbr, "WqT": w["L1fWqT"], "RoutT": w["L1fRoutT"], "LV": w["L1fLV"],
                     "WoutT": w["L1fWoutT"], "bo_rep": w["L1fbo"]}, 4096, name="l1f")
    emit_add(P, C, X2, [lambda t: X1[t*128:(t+1)*128, :], lambda t: Abr[t*128:(t+1)*128, :], lambda t: Fbr[t*128:(t+1)*128, :]], 4096, name="add1")
    P.new_epoch()
    emit_ln(P, C, X2, None, XNfm, 4096, name="ln2", XN_lo_fm=XNlo)
    emit_perf(P, C, {"XN_fm": XNfm, "OUT": Abr, "Wq": w["L2aWq"], "Wk": w["L2aWk"], "Wv": w["L2aWv"], "Wo": w["L2aWo"],
                     "bo_rep": w["L2abo"], "ProjPadT": w["L2aPP"], "ProjPadB": w["L2aPPB"]}, name="l2a")
    emit_glu(P, C, {"X_fm": XNfm, "X_lo_fm": XNlo, "OUT": Fbr, "Wg": w["L2fWg"],
                    "W1": w["L2fW1"], "W2": w["L2fW2"], "W3": w["L2fW3"]}, 4096, name="l2f")
    emit_add(P, C, X3, [lambda t: X2[t*128:(t+1)*128, :], lambda t: Abr[t*128:(t+1)*128, :], lambda t: Fbr[t*128:(t+1)*128, :]], 4096, name="add2")
    P.new_epoch()
    emit_ln(P, C, X3, None, XNfm, 4096, name="ln3", XN_tm32=XN32)
    emit_attn(P, C, {"XN_fm": XNfm, "OUT": Abr, "WqT": w["L3aWqT"], "WkT": w["L3aWkT"], "WvT": w["L3aWvT"],
                     "WoT": w["L3aWoT"], "bq": w["L3abq"], "bk": w["L3abk"], "bv_rep": w["L3abv"], "bo_rep": w["L3abo"]},
              nheads=1, hd=1024, causal=False, scale=1.0/32, name="l3a")
    emit_l3f_attn(P, C, {"XN32": XN32, "H32": H32, "TN2_fm": TN2, "TN2_lo_fm": TN2L,
                         "WqT": w["L3fWqT"], "WkT": w["L3fWkT"], "WvT": w["L3fWvT"], "WoT": w["L3fWoT"],
                         "anw_rep": w["L3fanw"], "mnw_rep": w["L3fmnw"], "CosT": w["L3fCos"], "SinT": w["L3fSin"]},
                  name="l3f")
    emit_moe(P, C, {"X_fm": TN2, "X_lo_fm": TN2L, "OUT": MOE, "gate_wT": w["L3fgwT"], "gate_b_rep": None,
                    "W1T": w["L3fW1T"], "W2T": w["L3fW2T"], "B1": w["L3fB1"], "B2": w["L3fB2"]},
             4096, func=AF.Gelu, name="moe3")
    emit_add(P, C, X4, [lambda t: X3[t*128:(t+1)*128, :], lambda t: Abr[t*128:(t+1)*128, :],
                        lambda t: H32[t*128:(t+1)*128, :], lambda t: MOE[t*128:(t+1)*128, :]], 4096, name="add3")
    P.finalize()
    return P, C, [nm for nm, _ in WSHAPES] + ["L3fCos", "L3fSin"]

import os as _os
_os.environ.setdefault("TRN_ALLOW_ICI_ALL_TO_ALL", "1")
from concourse.bass_utils import run_bass_kernel_spmd


def kernel(**inputs):
    patch_fast_compile()
    x = np.asarray(inputs["L0_attn_x"], np.float32)
    g = prep_all(inputs)
    xown = [np.ascontiguousarray(x[2*c:2*c+2].reshape(4096, 1024)) for c in range(8)]
    xpos = [np.ascontiguousarray(x[:, 256*c:256*(c+1), :].transpose(1, 0, 2).reshape(4096, 1024)) for c in range(8)]
    nc = bass.Bass("TRN2", target_bir_lowering=False)
    P, C, names = build_full(nc)
    maps = []
    for c in range(8):
        m = {nm: g[nm] for nm in names}
        m["Xown"] = xown[c]
        m["Xpos"] = xpos[c]
        m.update(C.arrs)
        maps.append(m)
    res = run_bass_kernel_spmd(nc, maps, core_ids=list(range(8)))
    out = np.empty((16, 2048, 1024), np.float32)
    for c in range(8):
        out[2*c:2*c+2] = res.results[c]["X4"].reshape(2, 2048, 1024)
    return out


# version 09
# speedup vs baseline: 1.0802x; 1.0802x over previous
import numpy as np
import ml_dtypes
import concourse.bass as bass
import concourse.mybir as mybir

F32 = mybir.dt.float32
BF16 = mybir.dt.bfloat16
AF = mybir.ActivationFunctionType
ALU = mybir.AluOpType
AX = mybir.AxisListType
BF16NP = ml_dtypes.bfloat16

ENGS = ("pe", "act", "dve", "pool", "sp")
_ENG_ATTR = {"pe": "tensor", "act": "scalar", "dve": "vector", "pool": "gpsimd", "sp": "sync"}


def patch_fast_compile():
    import concourse.bass_utils as bu
    if getattr(bu, "_mk_patched", False):
        return
    orig = bu.run_command
    def run2(argv, **kw):
        argv = ["--enable-birsim=false" if c == "--enable-birsim=true" else c for c in argv]
        return orig(argv, **kw)
    bu.run_command = run2
    bu._mk_patched = True


class Buf:
    __slots__ = ("name", "writers", "readers")
    def __init__(self, name):
        self.name = name
        self.writers = []
        self.readers = []


class PsumBank:
    def __init__(self, nc, i):
        self.t = nc.alloc_psum_tensor(f"psb{i}", [128, 512], F32)
        self.buf = Buf(f"psb{i}")
    @property
    def f32(self):
        return self.t[:, :]
    @property
    def bf16(self):
        return self.t[:, :].bitcast(BF16)


class Prog:
    def __init__(self, nc):
        self.nc = nc
        self.plan = {e: [] for e in ENGS}
        self.sems = {e: nc.alloc_semaphore(f"sem_{e}") for e in ENGS}
        self.carriers = {e: [] for e in ENGS}
        self.waited = {e: {} for e in ENGS}
        self.last_drain = {e: -1 for e in ENGS}
        self.streams = {}
        self.free_dma_sems = []
        self.free_eng_sems = []
        self.sem_base = {e: 0 for e in ENGS}
        self.banks = [PsumBank(nc, i) for i in range(8)]
        self._bank_free = list(range(8))
        self.sbuf_base = 17408
        self.sbuf_limit = 229344
        self._off = self.sbuf_base
        self._uid = 0

    def uid(self):
        self._uid += 1
        return self._uid

    def salloc(self, name, shape, dtype, align=64):
        size = shape[1] * (2 if dtype == BF16 else 4)
        off = (self._off + align - 1) // align * align
        assert off + size <= self.sbuf_limit, f"SBUF overflow: {name} at {off}+{size}"
        t = self.nc.alloc_sbuf_tensor_at(f"{name}_{self.uid()}", shape, dtype, offset=off)
        self._off = off + size
        return t

    def mark(self):
        return self._off

    def release(self, m):
        self._off = m

    def bank(self):
        assert self._bank_free, "no free psum banks"
        return self.banks[self._bank_free.pop(0)]

    def bank_free(self, *bs):
        for b in bs:
            self._bank_free.append(self.banks.index(b))

    def _edge(self, eng, dep):
        kind = dep[0]
        if kind == "E":
            _, e2, idx = dep
            if e2 == eng:
                if eng in ("pe", "sp"):
                    return
                if idx >= self.last_drain[eng]:
                    self.plan[eng].append({"k": "drain"})
                    self.last_drain[eng] = len(self.plan[eng]) - 1
                return
            car = self.carriers[e2]
            lo, hi = 0, len(car)
            while lo < hi:
                mid = (lo + hi) // 2
                if car[mid][0] >= idx:
                    hi = mid
                else:
                    lo = mid + 1
            if lo < len(car):
                val = car[lo][1]
                sem = self.sems[e2]
                key = ("e", e2)
            else:
                ent = self.plan[e2][idx]
                assert ent["k"] == "op" and not ent.get("dma"), f"dep on non-op {ent}"
                if ent.get("car") is not None:
                    sem, val = ent["car"]
                    key = ("sx", sem.num)
                else:
                    val = (car[-1][1] + 1) if car else (self.sem_base[e2] + 1)
                    sem = self.sems[e2]
                    ent["incs"].append((sem, 1))
                    ent["car"] = (sem, val)
                    car.append((idx, val))
                    key = ("e", e2)
            if self.waited[eng].get(key, 0) >= val:
                return
            self.plan[eng].append({"k": "wait", "sem": sem, "val": val})
            self.waited[eng][key] = val
        else:
            _, stream, cnt = dep
            sem, total = self.streams[stream]
            cnt = total
            key = ("d", stream)
            if self.waited[eng].get(key, 0) >= cnt:
                return
            self.plan[eng].append({"k": "wait", "sem": sem, "val": 16 * cnt})
            self.waited[eng][key] = cnt

    def _deps_for(self, reads, writes, wpart):
        deps = []
        seen = set()
        def add(d):
            if d not in seen:
                seen.add(d)
                deps.append(d)
        for b in reads:
            for w in b.writers:
                add(w)
        for b in writes:
            for w in b.writers:
                add(w)
            for r in b.readers:
                add(r)
        for b in wpart:
            for w in b.writers:
                add(w)
            for r in b.readers:
                add(r)
        deps.sort(key=lambda d: (d[1], -(d[2] if d[0] == "E" else 0)) if d[0] == "E" else (d[1], 0))
        return deps

    def op(self, eng, fn, reads=(), writes=(), wpart=()):
        for d in self._deps_for(reads, writes, wpart):
            self._edge(eng, d)
        idx = len(self.plan[eng])
        self.plan[eng].append({"k": "op", "fn": fn, "incs": []})
        myid = ("E", eng, idx)
        for b in reads:
            b.readers.append(myid)
        for b in writes:
            b.writers = [myid]
            b.readers = []
        for b in wpart:
            b.writers.append(myid)
        return myid

    def dma(self, queue, fn, reads=(), writes=(), wpart=(), stream=None):
        assert stream is not None
        if stream not in self.streams:
            if self.free_dma_sems:
                self.streams[stream] = list(self.free_dma_sems.pop(0))
            else:
                self.streams[stream] = [self.nc.alloc_semaphore(f"ds_{stream}"), 0]
        st = self.streams[stream]
        for d in self._deps_for(reads, writes, wpart):
            self._edge(queue, d)
        self.plan[queue].append({"k": "op", "fn": fn, "incs": [(st[0], 16)], "dma": True})
        st[1] += 1
        myid = ("D", stream, st[1])
        for b in reads:
            b.readers.append(myid)
        for b in writes:
            b.writers = [myid]
            b.readers = []
        for b in wpart:
            b.writers.append(myid)
        return myid

    def new_epoch(self):
        self._uid += 1
        keep = {}
        for s, (sem, cnt) in self.streams.items():
            if s == "consts":
                keep[s] = [sem, cnt]
            else:
                self.free_dma_sems.append((sem, cnt))
        self.streams = keep
        for e in ENGS:
            last = self.carriers[e][-1][1] if self.carriers[e] else self.sem_base[e]
            self.free_eng_sems.append((self.sems[e], last))
        old = dict(self.sems)
        for e in ENGS:
            sem, base = self.free_eng_sems.pop(0)
            if sem is old[e]:
                self.free_eng_sems.append((sem, base))
                sem, base = self.free_eng_sems.pop(0)
            self.sems[e] = sem
            self.sem_base[e] = base
            self.carriers[e] = []
            self.last_drain[e] = -1
        self.waited = {e: {} for e in ENGS}
        for b in self.banks:
            b.buf.writers = []
            b.buf.readers = []

    def barrier(self):
        deps = []
        for e in ENGS:
            idx = len(self.plan[e]) - 1
            while idx >= 0 and (self.plan[e][idx]["k"] != "op" or self.plan[e][idx].get("dma")):
                idx -= 1
            if idx >= 0:
                deps.append(("E", e, idx))
        for s, (sem, cnt) in self.streams.items():
            if cnt:
                deps.append(("D", s, cnt))
        for e in ENGS:
            for d in deps:
                if d[0] == "E" and d[1] == e:
                    continue
                self._edge(e, d)

    def finalize(self):
        nc = self.nc
        with nc.Block() as block:
            for eng in ENGS:
                entries = self.plan[eng]
                def make_body(entries):
                    def body(h):
                        for ent in entries:
                            k = ent["k"]
                            if k == "wait":
                                h.wait_ge(ent["sem"], ent["val"])
                            elif k == "drain":
                                h.drain()
                            else:
                                inst = ent["fn"](h)
                                for sem, amt in ent["incs"]:
                                    inst.then_inc(sem, amt)
                    return body
                getattr(block, _ENG_ATTR[eng])(make_body(entries))



def ceil_div(a, b):
    return (a + b - 1) // b


class Consts:
    def __init__(self, nc):
        self.nc = nc
        self.arrs = {}
        self.aps = {}

    def add(self, name, arr, dtype):
        if name in self.aps:
            return self.aps[name]
        t = self.nc.dram_tensor(name, list(arr.shape), dtype, kind="ExternalInput")
        self.arrs[name] = arr
        self.aps[name] = t.ap()
        return self.aps[name]

    def ident_bf16(self):
        return self.add("c_ident_bf16", np.eye(128, dtype=BF16NP), BF16)

    def ident_sb32(self, P):
        if not hasattr(self, "_ident_sb32"):
            ap = self.add("c_ident_f32", np.eye(128, dtype=np.float32), F32)
            t = P.salloc("identsb32", [128, 128], F32)
            b = Buf("identsb32")
            P.dma("sp", lambda h: h.dma_start(out=t[:, :], in_=ap), writes=[b], stream="consts")
            self._ident_sb32 = (t, b)
        return self._ident_sb32

    def ident_sb(self, P):
        if not hasattr(self, "_ident_sb"):
            ap = self.ident_bf16()
            t = P.salloc("identsb", [128, 128], BF16)
            b = Buf("identsb")
            P.dma("sp", lambda h: h.dma_start(out=t[:, :], in_=ap), writes=[b], stream="consts")
            self._ident_sb = (t, b)
        return self._ident_sb


def load_const_tile(P, ap, shape, dtype, name, stream="consts"):
    t = P.salloc(name, shape, dtype)
    b = Buf(name)
    P.dma("sp", lambda h, t=t, ap=ap: h.dma_start(out=t[:, :], in_=ap), writes=[b], stream=stream)
    return t, b


def emit_ln(P, C, X, XN_tm, XN_fm, ntok, eps=1e-5, name="ln", XN_lo_fm=None, XN_tm32=None):
    m0 = P.mark()
    D = 1024
    ident, identb = C.ident_sb(P)
    ntiles = ntok // 128
    xt = [P.salloc(f"{name}_x{i}", [128, D], F32) for i in range(2)]
    xtb = [Buf(f"{name}_x{i}") for i in range(2)]
    xc = P.salloc(f"{name}_xc", [128, D], F32); xcb = Buf("xc")
    sq = P.salloc(f"{name}_sq", [128, D], F32); sqb = Buf("sq")
    stat = P.salloc(f"{name}_st", [128, 8], F32); statb = Buf("stat")
    xnb = P.salloc(f"{name}_xnb", [128, D], BF16); xnbb = Buf("xnb")
    fmc = [P.salloc(f"{name}_fm{c}", [128, ntok], BF16) for c in range(8)]
    fmb = [Buf(f"{name}_fm{c}") for c in range(8)]
    need32 = (XN_lo_fm is not None) or (XN_tm32 is not None)
    if need32:
        xn32 = P.salloc(f"{name}_xn32", [128, D], F32); xn32b = Buf("xn32")
    if XN_lo_fm is not None:
        xlo = P.salloc(f"{name}_xlo", [128, D], BF16); xlob = Buf("xlo")
        loc = [P.salloc(f"{name}_lo{c}", [128, ntok], BF16) for c in range(8)]
        lob = [Buf(f"{name}_lo{c}") for c in range(8)]

    for i in range(ntiles):
        sl = i % 2
        t0 = i * 128
        P.dma("sp", lambda h, i=i, sl=sl: h.dma_start(out=xt[sl][:, :], in_=X[i*128:(i+1)*128, :]),
              reads=[], writes=[xtb[sl]], stream=f"{name}x{sl}")
        P.op("dve", lambda h, sl=sl: h.reduce_sum(out=stat[:, 0:1], in_=xt[sl][:, :], axis=AX.X),
             reads=[xtb[sl]], writes=[statb])
        P.op("dve", lambda h: h.tensor_scalar(out=stat[:, 1:2], in0=stat[:, 0:1], scalar1=1.0/D, scalar2=None, op0=ALU.mult),
             reads=[statb], wpart=[statb])
        P.op("dve", lambda h, sl=sl: h.tensor_scalar(out=xc[:, :], in0=xt[sl][:, :], scalar1=stat[:, 1:2], scalar2=None, op0=ALU.subtract),
             reads=[xtb[sl], statb], writes=[xcb])
        P.op("act", lambda h: h.activation(out=sq[:, :], in_=xc[:, :], func=AF.Square),
             reads=[xcb], writes=[sqb])
        P.op("dve", lambda h: h.reduce_sum(out=stat[:, 2:3], in_=sq[:, :], axis=AX.X),
             reads=[sqb], wpart=[statb])
        P.op("dve", lambda h: h.tensor_scalar(out=stat[:, 3:4], in0=stat[:, 2:3], scalar1=1.0/D, scalar2=eps, op0=ALU.mult, op1=ALU.add),
             reads=[statb], wpart=[statb])
        P.op("act", lambda h: h.activation(out=stat[:, 4:5], in_=stat[:, 3:4], func=AF.Sqrt),
             reads=[statb], wpart=[statb])
        P.op("dve", lambda h: h.reciprocal(out=stat[:, 5:6], in_=stat[:, 4:5]),
             reads=[statb], wpart=[statb])
        if not need32:
            P.op("dve", lambda h: h.tensor_scalar(out=xnb[:, :], in0=xc[:, :], scalar1=stat[:, 5:6], scalar2=None, op0=ALU.mult),
                 reads=[xcb, statb], writes=[xnbb])
        else:
            P.op("dve", lambda h: h.tensor_scalar(out=xn32[:, :], in0=xc[:, :], scalar1=stat[:, 5:6], scalar2=None, op0=ALU.mult),
                 reads=[xcb, statb], writes=[xn32b])
            P.op("dve", lambda h: h.tensor_copy(out=xnb[:, :], in_=xn32[:, :]), reads=[xn32b], writes=[xnbb])
            if XN_lo_fm is not None:
                P.op("dve", lambda h: h.tensor_sub(out=xlo[:, :], in0=xn32[:, :], in1=xnb[:, :]), reads=[xn32b, xnbb], writes=[xlob])
            if XN_tm32 is not None:
                P.dma("act", lambda h, i=i: h.dma_start(out=XN_tm32[i*128:(i+1)*128, :], in_=xn32[:, :]),
                      reads=[xn32b], stream=f"{name}tm32")
        if XN_tm is not None:
            P.dma("act", lambda h, i=i: h.dma_start(out=XN_tm[i*128:(i+1)*128, :], in_=xnb[:, :]),
                  reads=[xnbb], stream=f"{name}tm")
        bank = P.bank()
        for c in range(8):
            P.op("pe", lambda h, c=c, bank=bank: h.transpose(out=bank.bf16[:, c*128:(c+1)*128], in_=xnb[:, c*128:(c+1)*128], identity=ident[:, :]),
                 reads=[xnbb, identb], wpart=[bank.buf])
        for c in range(8):
            P.op("dve", lambda h, c=c, t0=t0, bank=bank: h.tensor_copy(out=fmc[c][:, t0:t0+128], in_=bank.bf16[:, c*128:(c+1)*128]),
                 reads=[bank.buf], wpart=[fmb[c]])
        P.bank_free(bank)
        if XN_lo_fm is not None:
            bank = P.bank()
            for c in range(8):
                P.op("pe", lambda h, c=c, bank=bank: h.transpose(out=bank.bf16[:, c*128:(c+1)*128], in_=xlo[:, c*128:(c+1)*128], identity=ident[:, :]),
                     reads=[xlob, identb], wpart=[bank.buf])
            for c in range(8):
                P.op("dve", lambda h, c=c, t0=t0, bank=bank: h.tensor_copy(out=loc[c][:, t0:t0+128], in_=bank.bf16[:, c*128:(c+1)*128]),
                     reads=[bank.buf], wpart=[lob[c]])
            P.bank_free(bank)
    for c in range(8):
        P.dma("act", lambda h, c=c: h.dma_start(out=XN_fm[c*128:(c+1)*128, :], in_=fmc[c][:, :]),
              reads=[fmb[c]], stream=f"{name}fm")
    if XN_lo_fm is not None:
        for c in range(8):
            P.dma("act", lambda h, c=c: h.dma_start(out=XN_lo_fm[c*128:(c+1)*128, :], in_=loc[c][:, :]),
                  reads=[lob[c]], stream=f"{name}fm")
    P.barrier()
    P.release(m0)


def load_w_chunks(P, Wdram, K, N, name, dtype=BF16, stream="w"):
    nk = ceil_div(K, 128)
    tiles, bufs = [], []
    for kc in range(nk):
        rows = min(128, K - kc * 128)
        t = P.salloc(f"{name}{kc}", [128, N], dtype)
        b = Buf(f"{name}{kc}")
        P.dma("pool", lambda h, t=t, kc=kc, rows=rows: h.dma_start(out=t[0:rows, :], in_=Wdram[kc*128:kc*128+rows, :]),
              writes=[b], stream=stream)
        tiles.append(t); bufs.append(b)
    return tiles, bufs



def emit_attn(P, C, io, nheads, hd, causal, scale, nseq=2, T=2048, name="attn"):
    assert hd in (64, 1024)
    m0 = P.mark()
    D = 1024
    XN_fm = io["XN_fm"]; OUT = io["OUT"]
    ident, identb = C.ident_sb(P)
    nkT = T // 128
    nq4 = T // 512
    vw = nheads * (hd + 1)

    if causal:
        cm = np.zeros((128, 4*512), np.float32)
        for d in range(4):
            for kk in range(128):
                cm[kk, d*512:d*512 + d*128 + kk] = -30000.0
        cmap = C.add("c_causal512", cm, F32)
        cmt = P.salloc(f"{name}_cm", [128, 4*512], F32)
        cmb = Buf("cmt")
        P.dma("sp", lambda h: h.dma_start(out=cmt[:, :], in_=cmap), writes=[cmb], stream="consts")

    wsb = [P.salloc(f"{name}_w{kc}", [128, D], BF16) for kc in range(8)]
    wbufs = [Buf(f"{name}_w{kc}") for kc in range(8)]
    xfm = [P.salloc(f"{name}_xf{kc}", [128, T], BF16) for kc in range(8)]
    xfb = [Buf(f"{name}_xf{kc}") for kc in range(8)]
    qfm = [P.salloc(f"{name}_qf{kc}", [128, T], BF16) for kc in range(8)]
    qfb = [Buf(f"{name}_qf{kc}") for kc in range(8)]
    kfm = [P.salloc(f"{name}_kf{kc}", [128, T], BF16) for kc in range(8)]
    kfb = [Buf(f"{name}_kf{kc}") for kc in range(8)]
    vau = [P.salloc(f"{name}_va{t}", [128, vw], BF16) for t in range(nkT)]
    vab = [Buf(f"{name}_va{t}") for t in range(nkT)]
    ctx = [P.salloc(f"{name}_cx{t}", [128, D], BF16) for t in range(nkT)]
    ctxb = [Buf(f"{name}_cx{t}") for t in range(nkT)]
    if hd == 64:
        pt = [P.salloc(f"{name}_pt{i}", [128, 512], BF16) for i in range(2)]
        ptb = [Buf(f"{name}_pt{i}") for i in range(2)]
    else:
        pta = [P.salloc(f"{name}_pta{i}", [128, 512], BF16) for i in range(nkT)]
        ptab = [Buf(f"{name}_pta{i}") for i in range(nkT)]
    st = P.salloc(f"{name}_stat", [128, 8], F32)
    stb = Buf(f"{name}_stat")
    ob = [P.salloc(f"{name}_ob{i}", [128, 512], F32) for i in range(2)]
    obb = [Buf(f"{name}_ob{i}") for i in range(2)]
    cfm = xfm
    cfb = xfb
    borep = borepb = None
    if io.get("bo_rep") is not None:
        borep = P.salloc(f"{name}_bo", [128, D], F32); borepb = Buf("bo")
        P.dma("sp", lambda h: h.dma_start(out=borep[:, :], in_=io["bo_rep"]), writes=[borepb], stream="consts")
    bvrep = bvrepb = None
    if io.get("bv_rep") is not None:
        bvrep = P.salloc(f"{name}_bv", [128, D], F32); bvrepb = Buf("bv")
        P.dma("sp", lambda h: h.dma_start(out=bvrep[:, :], in_=io["bv_rep"]), writes=[bvrepb], stream="consts")
    bpart = {}
    for nm in ("bq", "bk"):
        if io.get(nm) is not None:
            bt = P.salloc(f"{name}_{nm}", [128, 8], F32); bb = Buf(nm)
            P.dma("sp", lambda h, bt=bt, nm=nm: h.dma_start(out=bt[:, :], in_=io[nm]), writes=[bb], stream="consts")
            bpart[nm] = (bt, bb)

    for s in range(nseq):
        toff = s * T
        for kc in range(8):
            P.dma("sp", lambda h, kc=kc, toff=toff: h.dma_start(out=xfm[kc][:, :], in_=XN_fm[kc*128:(kc+1)*128, toff:toff+T]),
                  writes=[xfb[kc]], stream=f"{name}xf")

        def loadw(Wt):
            for kc in range(8):
                P.dma("pool", lambda h, kc=kc, Wt=Wt: h.dma_start(out=wsb[kc][:, :], in_=Wt[kc*128:(kc+1)*128, :]),
                      writes=[wbufs[kc]], stream=f"{name}w")

        def proj_fm(bias, dst, dstb):
            for oc in range(8):
                for t4 in range(T // 512):
                    bank = P.bank()
                    for kc in range(8):
                        P.op("pe", lambda h, kc=kc, oc=oc, t4=t4, bank=bank: h.matmul(
                            out=bank.f32, lhsT=wsb[kc][:, oc*128:(oc+1)*128], rhs=xfm[kc][:, t4*512:(t4+1)*512],
                            start=(kc == 0), stop=(kc == 7)),
                            reads=[wbufs[kc], xfb[kc]], writes=[bank.buf] if kc == 0 else (), wpart=[bank.buf] if kc > 0 else ())
                    if bias is not None:
                        bt, bb = bpart[bias]
                        P.op("act", lambda h, oc=oc, t4=t4, bank=bank, bt=bt: h.activation(
                            out=dst[oc][:, t4*512:(t4+1)*512], in_=bank.f32, func=AF.Identity, bias=bt[:, oc:oc+1]),
                            reads=[bank.buf, bb], wpart=[dstb[oc]])
                    else:
                        P.op("act", lambda h, oc=oc, t4=t4, bank=bank: h.copy(
                            out=dst[oc][:, t4*512:(t4+1)*512], in_=bank.f32),
                            reads=[bank.buf], wpart=[dstb[oc]])
                    P.bank_free(bank)

        loadw(io["WqT"]); proj_fm("bq" if "bq" in bpart else None, qfm, qfb)
        loadw(io["WkT"]); proj_fm("bk" if "bk" in bpart else None, kfm, kfb)
        loadw(io["WvT"])
        for t in range(nkT):
            if nheads == 1:
                P.op("dve", lambda h, t=t: h.memset(vau[t][:, 1024:1025], 1.0), writes=[vab[t]])
            else:
                P.op("dve", lambda h, t=t: h.memset(vau[t][:, :], 1.0), writes=[vab[t]])
            for dc in range(2):
                bank = P.bank()
                for kc in range(8):
                    P.op("pe", lambda h, kc=kc, t=t, dc=dc, bank=bank: h.matmul(
                        out=bank.f32, lhsT=xfm[kc][:, t*128:(t+1)*128], rhs=wsb[kc][:, dc*512:(dc+1)*512],
                        start=(kc == 0), stop=(kc == 7)),
                        reads=[wbufs[kc], xfb[kc]], writes=[bank.buf] if kc == 0 else (), wpart=[bank.buf] if kc > 0 else ())
                def vevac(h, t=t, dc=dc, bank=bank):
                    if nheads == 1:
                        dst = vau[t][:, dc*512:(dc+1)*512]
                        if bvrep is not None:
                            return h.tensor_add(out=dst, in0=bank.f32, in1=bvrep[:, dc*512:(dc+1)*512])
                        return h.tensor_copy(out=dst, in_=bank.f32)
                    dstv = vau[t][:, :].rearrange("p (h e) -> p h e", e=hd+1)
                    src3 = bank.t[0:128, 0:512].rearrange("p (h e) -> p h e", e=hd)
                    nh0 = (dc * 512) // hd
                    nh1 = ((dc + 1) * 512) // hd
                    dst = dstv[:, nh0:nh1, 0:hd]
                    if bvrep is not None:
                        b3 = bvrep[:, dc*512:(dc+1)*512].rearrange("p (h e) -> p h e", e=hd)
                        return h.tensor_add(out=dst, in0=src3, in1=b3)
                    return h.tensor_copy(out=dst, in_=src3)
                P.op("dve", vevac, reads=[bank.buf] + ([bvrepb] if bvrep is not None else []), wpart=[vab[t]])
                P.bank_free(bank)

        if hd == 1024:
            for q4 in range(nq4):
                nk0 = nkT
                sbk = [P.bank(), P.bank()]
                for k0 in range(nk0):
                    sb = sbk[k0 % 2]
                    for kc in range(8):
                        P.op("pe", lambda h_, kc=kc, k0=k0, q4=q4, sb=sb: h_.matmul(
                            out=sb.f32, lhsT=kfm[kc][:, k0*128:(k0+1)*128], rhs=qfm[kc][:, q4*512:(q4+1)*512],
                            start=(kc == 0), stop=(kc == 7)),
                            reads=[kfb[kc], qfb[kc]], writes=[sb.buf] if kc == 0 else (), wpart=[sb.buf] if kc > 0 else ())
                    P.op("act", lambda h_, sb=sb, k0=k0: h_.activation(out=pta[k0][:, :], in_=sb.f32, func=AF.Exp, scale=scale),
                         reads=[sb.buf], writes=[ptab[k0]])
                P.bank_free(*sbk)
                for qq in range(4):
                    t = q4 * 4 + qq
                    lb = P.bank()
                    for k0 in range(nk0):
                        P.op("pe", lambda h_, qq=qq, k0=k0, lb=lb, first=(k0 == 0), last=(k0 == nk0-1): h_.matmul(
                            out=lb.t[0:128, 0:1], lhsT=pta[k0][:, qq*128:(qq+1)*128], rhs=vau[k0][:, 1024:1025],
                            start=first, stop=last),
                            reads=[ptab[k0], vab[k0]], writes=[lb.buf] if k0 == 0 else (), wpart=[lb.buf] if k0 > 0 else ())
                    P.op("dve", lambda h_, lb=lb: h_.tensor_copy(out=st[:, 0:1], in_=lb.t[0:128, 0:1]),
                         reads=[lb.buf], writes=[stb])
                    P.op("dve", lambda h_: h_.reciprocal(out=st[:, 1:2], in_=st[:, 0:1]), reads=[stb], wpart=[stb])
                    P.bank_free(lb)
                    for piece in range(2):
                        pb = P.bank()
                        for k0 in range(nk0):
                            P.op("pe", lambda h_, qq=qq, k0=k0, piece=piece, pb=pb, first=(k0 == 0), last=(k0 == nk0-1): h_.matmul(
                                out=pb.f32, lhsT=pta[k0][:, qq*128:(qq+1)*128], rhs=vau[k0][:, piece*512:(piece+1)*512],
                                start=first, stop=last),
                                reads=[ptab[k0], vab[k0]], writes=[pb.buf] if k0 == 0 else (), wpart=[pb.buf] if k0 > 0 else ())
                        P.op("dve", lambda h_, pb=pb, t=t, piece=piece: h_.tensor_scalar(
                            out=ctx[t][:, piece*512:(piece+1)*512], in0=pb.f32, scalar1=st[:, 1:2], scalar2=None, op0=ALU.mult),
                            reads=[pb.buf, stb], wpart=[ctxb[t]])
                        P.bank_free(pb)
        for h in (range(nheads) if hd == 64 else []):
            kc8 = (h * hd) // 128
            rb = (h * hd) % 128
            for q4 in range(nq4):
                nk0 = (q4 + 1) * 4 if causal else nkT
                pvb = [P.bank() for _ in range(4)]
                sbk = [P.bank(), P.bank()]
                def emit_S(k0, kc8=kc8, rb=rb, q4=q4):
                    sb = sbk[k0 % 2]
                    sl = k0 % 2
                    P.op("pe", lambda h_, kc8=kc8, rb=rb, k0=k0, q4=q4, sb=sb: h_.matmul(
                        out=sb.f32, lhsT=kfm[kc8][rb:rb+hd, k0*128:(k0+1)*128], rhs=qfm[kc8][rb:rb+hd, q4*512:(q4+1)*512],
                        start=True, stop=True),
                        reads=[kfb[kc8], qfb[kc8]], writes=[sb.buf])
                    if causal and k0 >= q4 * 4:
                        dl = k0 - q4 * 4
                        P.op("dve", lambda h_, dl=dl, sb=sb: h_.tensor_add(out=sb.f32, in0=sb.f32, in1=cmt[:, dl*512:(dl+1)*512]),
                             reads=[cmb], wpart=[sb.buf])
                    P.op("act", lambda h_, sb=sb, sl=sl: h_.activation(out=pt[sl][:, :], in_=sb.f32, func=AF.Exp, scale=scale),
                         reads=[sb.buf], writes=[ptb[sl]])
                emit_S(0)
                for k0 in range(nk0):
                    if k0 + 1 < nk0:
                        emit_S(k0 + 1)
                    sl = k0 % 2
                    first = (k0 == 0)
                    last = (k0 == nk0 - 1)
                    for qq in range(4):
                        bqq = pvb[qq]
                        P.op("pe", lambda h_, sl=sl, qq=qq, k0=k0, hh=h, first=first, last=last, b=bqq: h_.matmul(
                            out=b.t[0:128, 0:hd+1], lhsT=pt[sl][:, qq*128:(qq+1)*128], rhs=vau[k0][:, hh*(hd+1):(hh+1)*(hd+1)],
                            start=first, stop=last),
                            reads=[ptb[sl], vab[k0]], writes=[bqq.buf] if first else (), wpart=[bqq.buf] if not first else ())
                for qq in range(4):
                    t = q4 * 4 + qq
                    b = pvb[qq]
                    P.op("dve", lambda h_, b=b: h_.tensor_copy(out=st[:, 0:1], in_=b.t[0:128, hd:hd+1]),
                         reads=[b.buf], writes=[stb])
                    P.op("dve", lambda h_: h_.reciprocal(out=st[:, 1:2], in_=st[:, 0:1]),
                         reads=[stb], wpart=[stb])
                    P.op("dve", lambda h_, b=b, t=t, hh=h: h_.tensor_scalar(
                        out=ctx[t][:, hh*hd:(hh+1)*hd], in0=b.t[0:128, 0:hd], scalar1=st[:, 1:2], scalar2=None, op0=ALU.mult),
                        reads=[b.buf, stb], wpart=[ctxb[t]])
                P.bank_free(*pvb, *sbk)

        if io.get("WoT") is None:
            for t in range(nkT):
                for half in range(2):
                    sl = (t * 2 + half) % 2
                    P.op("dve", lambda h, t=t, sl=sl, half=half: h.tensor_copy(out=ob[sl][:, :], in_=ctx[t][:, half*512:(half+1)*512]),
                         reads=[ctxb[t]], writes=[obb[sl]])
                    P.dma("act", lambda h, t=t, sl=sl, half=half, toff=toff: h.dma_start(
                        out=OUT[toff+t*128:toff+(t+1)*128, half*512:(half+1)*512], in_=ob[sl][:, :]),
                        reads=[obb[sl]], stream=f"{name}o")
        else:
            for t in range(nkT):
                bank = P.bank()
                for c in range(8):
                    P.op("pe", lambda h, c=c, t=t, bank=bank: h.transpose(
                        out=bank.bf16[:, c*128:(c+1)*128], in_=ctx[t][:, c*128:(c+1)*128], identity=ident[:, :]),
                        reads=[ctxb[t], identb], writes=[bank.buf] if c == 0 else (), wpart=[bank.buf] if c > 0 else ())
                for c in range(8):
                    P.op("dve", lambda h, c=c, t=t, bank=bank: h.tensor_copy(out=cfm[c][:, t*128:(t+1)*128], in_=bank.bf16[:, c*128:(c+1)*128]),
                         reads=[bank.buf], wpart=[cfb[c]])
                P.bank_free(bank)
            loadw(io["WoT"])
            for t in range(nkT):
                for dc in range(2):
                    bank = P.bank()
                    for kc in range(8):
                        P.op("pe", lambda h, kc=kc, t=t, dc=dc, bank=bank: h.matmul(
                            out=bank.f32, lhsT=cfm[kc][:, t*128:(t+1)*128], rhs=wsb[kc][:, dc*512:(dc+1)*512],
                            start=(kc == 0), stop=(kc == 7)),
                            reads=[wbufs[kc], cfb[kc]], writes=[bank.buf] if kc == 0 else (), wpart=[bank.buf] if kc > 0 else ())
                    sl = (t * 2 + dc) % 2
                    if borep is not None:
                        P.op("dve", lambda h, bank=bank, dc=dc, sl=sl: h.tensor_add(out=ob[sl][:, :], in0=bank.f32, in1=borep[:, dc*512:(dc+1)*512]),
                             reads=[bank.buf, borepb], writes=[obb[sl]])
                    else:
                        P.op("dve", lambda h, bank=bank, sl=sl: h.tensor_copy(out=ob[sl][:, :], in_=bank.f32),
                             reads=[bank.buf], writes=[obb[sl]])
                    P.dma("act", lambda h, t=t, dc=dc, sl=sl, toff=toff: h.dma_start(
                        out=OUT[toff+t*128:toff+(t+1)*128, dc*512:(dc+1)*512], in_=ob[sl][:, :]),
                        reads=[obb[sl]], stream=f"{name}o")
                    P.bank_free(bank)
    P.barrier()
    P.release(m0)


def emit_moe(P, C, io, ntok, func=AF.Relu, name="moe"):
    m0 = P.mark()
    D, F, E = 1024, 4096, 8
    Xfm = io["X_fm"]; OUT = io["OUT"]
    ident, identb = C.ident_sb(P)

    gwf = P.salloc(f"{name}_gwf", [128, 8*8], F32)
    gwfb = Buf(f"{name}_gwf")
    gw = P.salloc(f"{name}_gw", [128, 8*8], BF16)
    gwb = Buf(f"{name}_gw")
    gwl = P.salloc(f"{name}_gwl", [128, 8*8], BF16)
    gwlb = Buf(f"{name}_gwl")
    for kc in range(8):
        P.dma("sp", lambda h, kc=kc: h.dma_start(out=gwf[:, kc*8:(kc+1)*8], in_=io["gate_wT"][kc*128:(kc+1)*128, :]),
              wpart=[gwfb], stream=f"{name}gwf")
    P.op("dve", lambda h: h.tensor_copy(out=gw[:, :], in_=gwf[:, :]), reads=[gwfb], writes=[gwb])
    P.op("dve", lambda h: h.tensor_sub(out=gwl[:, :], in0=gwf[:, :], in1=gw[:, :]), reads=[gwfb, gwb], writes=[gwlb])
    gbr = gbrb = None
    if io.get("gate_b_rep") is not None:
        gbr = P.salloc(f"{name}_gb", [128, 8], F32); gbrb = Buf("gb")
        P.dma("sp", lambda h: h.dma_start(out=gbr[:, :], in_=io["gate_b_rep"]), writes=[gbrb], stream="consts")
    b2sb = P.salloc(f"{name}_b2", [128, D], BF16)
    b2b = Buf(f"{name}_b2")
    P.dma("pool", lambda h: h.dma_start(out=b2sb[0:8, :], in_=io["B2"]), writes=[b2b], stream=f"{name}b2")

    nhalf = 2
    TH = ntok // nhalf
    nt = TH // 128
    xfw = P.salloc(f"{name}_xfw", [128, 8*TH], BF16)
    xfwb = Buf(f"{name}_xfw")
    xf = [xfw[:, kc*TH:(kc+1)*TH] for kc in range(8)]
    xfb = [xfwb for kc in range(8)]
    acc = [P.salloc(f"{name}_ac{t}", [128, D], F32) for t in range(nt)]
    accb = [Buf(f"{name}_ac{t}") for t in range(nt)]
    w1w = [P.salloc(f"{name}_w1w{s}", [128, 8192], BF16) for s in range(2)]
    w1wb = [Buf(f"{name}_w1w{s}") for s in range(2)]
    w2w = [P.salloc(f"{name}_w2w{s}", [128, 8192], BF16) for s in range(2)]
    w2wb = [Buf(f"{name}_w2w{s}") for s in range(2)]
    w1s = [[w1w[s][:, kc*1024:(kc+1)*1024] for kc in range(8)] for s in range(2)]
    w1sb = [[w1wb[s] for kc in range(8)] for s in range(2)]
    w2s = [[w2w[s][:, kc*1024:(kc+1)*1024] for kc in range(8)] for s in range(2)]
    w2sb = [[w2wb[s] for kc in range(8)] for s in range(2)]
    b1s = [P.salloc(f"{name}_b1_{s}", [128, 8], F32) for s in range(2)]
    b1sb = [Buf(f"{name}_b1_{s}") for s in range(2)]
    hid = [P.salloc(f"{name}_hid{fc}", [128, 512], BF16) for fc in range(8)]
    hidb = [Buf(f"{name}_hid{fc}") for fc in range(8)]
    comb = P.salloc(f"{name}_comb", [128, 8*nt], F32)
    combb = [Buf(f"{name}_cb{t}") for t in range(nt)]
    combf = P.salloc(f"{name}_cfm", [128, TH], BF16)
    combfb = Buf(f"{name}_cfm")
    gs = P.salloc(f"{name}_gs", [128, 16], F32)
    gsb = Buf(f"{name}_gs")
    cbf = P.salloc(f"{name}_cbf", [128, 8], BF16)
    cbfb = Buf(f"{name}_cbf")
    ostage = [P.salloc(f"{name}_os{i}", [128, 512], F32) for i in range(4)]
    ostb = [Buf(f"{name}_os{i}") for i in range(4)]

    def load_quarter(slot, e, fq):
        def l1(h, e=e, fq=fq, slot=slot):
            srcv = io["W1T"][e*1024:(e+1)*1024, fq*1024:(fq+1)*1024].rearrange("(kc p) j -> p kc j", p=128)
            return h.dma_start(out=w1w[slot][:, :].rearrange("p (kc j) -> p kc j", kc=8), in_=srcv)
        P.dma("pool", l1, writes=[w1wb[slot]], stream=f"{name}w1")
        def l2(h, e=e, fq=fq, slot=slot):
            srcv = io["W2T"][e*4096+fq*1024:e*4096+(fq+1)*1024, :].rearrange("(kc p) j -> p kc j", p=128)
            return h.dma_start(out=w2w[slot][:, :].rearrange("p (kc j) -> p kc j", kc=8), in_=srcv)
        P.dma("pool", l2, writes=[w2wb[slot]], stream=f"{name}w2")
        P.dma("pool", lambda h, e=e, fq=fq, slot=slot: h.dma_start(
            out=b1s[slot][:, :], in_=io["B1"][e*128:(e+1)*128, fq*8:(fq+1)*8]),
            writes=[b1sb[slot]], stream=f"{name}w1")

    Xlo = io["X_lo_fm"]
    xl = [P.salloc(f"{name}_xl{kc}", [128, 512], BF16) for kc in range(8)]
    xlb = [Buf(f"{name}_xl{kc}") for kc in range(8)]
    for half in range(nhalf):
        hoff = half * TH
        def lx(h, hoff=hoff):
            srcv = Xfm.rearrange("(kc p) n -> p kc n", p=128)[:, :, hoff:hoff+TH]
            return h.dma_start(out=xfw[:, :].rearrange("p (kc n) -> p kc n", kc=8), in_=srcv)
        P.dma("sp", lx, writes=[xfwb], stream=f"{name}xf")
        for t in range(nt):
            if t % 4 == 0:
                for kc in range(8):
                    P.dma("sp", lambda h, kc=kc, hoff=hoff, t=t: h.dma_start(
                        out=xl[kc][:, :], in_=Xlo[kc*128:(kc+1)*128, hoff+t*128:hoff+t*128+512]),
                        writes=[xlb[kc]], stream=f"{name}xl")
            xo = (t % 4) * 128
            bank = P.bank()
            for kc in range(8):
                P.op("pe", lambda h, kc=kc, t=t, bank=bank: h.matmul(
                    out=bank.t[0:128, 0:8], lhsT=xf[kc][:, t*128:(t+1)*128], rhs=gw[:, kc*8:(kc+1)*8],
                    start=(kc == 0), stop=False),
                    reads=[xfb[kc], gwb], writes=[bank.buf] if kc == 0 else (), wpart=[bank.buf] if kc > 0 else ())
            for kc in range(8):
                P.op("pe", lambda h, kc=kc, t=t, xo=xo, bank=bank: h.matmul(
                    out=bank.t[0:128, 0:8], lhsT=xl[kc][:, xo:xo+128], rhs=gw[:, kc*8:(kc+1)*8],
                    start=False, stop=False),
                    reads=[xlb[kc], gwb], wpart=[bank.buf])
            for kc in range(8):
                P.op("pe", lambda h, kc=kc, t=t, bank=bank: h.matmul(
                    out=bank.t[0:128, 0:8], lhsT=xf[kc][:, t*128:(t+1)*128], rhs=gwl[:, kc*8:(kc+1)*8],
                    start=False, stop=(kc == 7)),
                    reads=[xfb[kc], gwlb], wpart=[bank.buf])
            gl = comb[:, t*8:(t+1)*8]
            if gbr is not None:
                P.op("dve", lambda h, bank=bank, gl=gl: h.tensor_add(out=gl, in0=bank.t[0:128, 0:8], in1=gbr[:, :]),
                     reads=[bank.buf, gbrb], writes=[combb[t]])
            else:
                P.op("dve", lambda h, bank=bank, gl=gl: h.tensor_copy(out=gl, in_=bank.t[0:128, 0:8]),
                     reads=[bank.buf], writes=[combb[t]])
            P.bank_free(bank)
            P.op("dve", lambda h, gl=gl: h.reduce_max(out=gs[:, 0:1], in_=gl, axis=AX.X), reads=[combb[t]], writes=[gsb])
            P.op("dve", lambda h, gl=gl: h.tensor_scalar(out=gs[:, 2:10], in0=gl, scalar1=gs[:, 0:1], scalar2=None, op0=ALU.is_ge),
                 reads=[combb[t], gsb], wpart=[gsb])
            P.op("dve", lambda h, gl=gl: h.scalar_tensor_tensor(out=gs[:, 2:10], in0=gs[:, 2:10], scalar=-1e30, in1=gl, op0=ALU.mult, op1=ALU.add),
                 reads=[combb[t], gsb], wpart=[gsb])
            P.op("dve", lambda h: h.reduce_max(out=gs[:, 1:2], in_=gs[:, 2:10], axis=AX.X), reads=[gsb], wpart=[gsb])
            P.op("dve", lambda h: h.tensor_scalar(out=gs[:, 10:11], in0=gs[:, 0:1], scalar1=gs[:, 1:2], scalar2=None, op0=ALU.subtract),
                 reads=[gsb], wpart=[gsb])
            P.op("act", lambda h: h.activation(out=gs[:, 11:12], in_=gs[:, 10:11], func=AF.Sigmoid), reads=[gsb], wpart=[gsb])
            P.op("dve", lambda h: h.tensor_scalar(out=gs[:, 12:13], in0=gs[:, 11:12], scalar1=-1.0, scalar2=1.0, op0=ALU.mult, op1=ALU.add),
                 reads=[gsb], wpart=[gsb])
            P.op("dve", lambda h: h.tensor_scalar(out=gs[:, 2:10], in0=gs[:, 2:10], scalar1=gs[:, 1:2], scalar2=None, op0=ALU.is_ge),
                 reads=[gsb], wpart=[gsb])
            P.op("dve", lambda h, gl=gl: h.tensor_scalar(out=gl, in0=gl, scalar1=gs[:, 0:1], scalar2=None, op0=ALU.is_ge),
                 reads=[gsb], wpart=[combb[t]])
            P.op("dve", lambda h, gl=gl: h.tensor_scalar(out=gl, in0=gl, scalar1=gs[:, 11:12], scalar2=None, op0=ALU.mult),
                 reads=[gsb], wpart=[combb[t]])
            P.op("dve", lambda h, gl=gl: h.scalar_tensor_tensor(out=gl, in0=gs[:, 2:10], scalar=gs[:, 12:13], in1=gl, op0=ALU.mult, op1=ALU.add),
                 reads=[gsb], wpart=[combb[t]])
            P.op("dve", lambda h, gl=gl: h.tensor_copy(out=cbf[:, :], in_=gl), reads=[combb[t]], writes=[cbfb])
            bank = P.bank()
            P.op("pe", lambda h, bank=bank: h.transpose(out=bank.bf16[0:8, 0:128], in_=cbf[:, 0:8], identity=ident[:, :]),
                 reads=[cbfb, identb], writes=[bank.buf])
            P.op("dve", lambda h, t=t, bank=bank: h.tensor_copy(out=combf[0:8, t*128:(t+1)*128], in_=bank.bf16[0:8, 0:128]),
                 reads=[bank.buf], wpart=[combfb])
            P.bank_free(bank)
        for t in range(nt):
            for dc in range(2):
                bank = P.bank()
                P.op("pe", lambda h, t=t, dc=dc, bank=bank: h.matmul(
                    out=bank.f32, lhsT=combf[0:8, t*128:(t+1)*128], rhs=b2sb[0:8, dc*512:(dc+1)*512],
                    start=True, stop=True),
                    reads=[combfb, b2b], writes=[bank.buf])
                P.op("dve", lambda h, t=t, dc=dc, bank=bank: h.tensor_copy(out=acc[t][:, dc*512:(dc+1)*512], in_=bank.f32),
                     reads=[bank.buf], writes=[accb[t]] if dc == 0 else (), wpart=[accb[t]] if dc else ())
                P.bank_free(bank)
        if half == 0:
            load_quarter(0, 0, 0)
        for e in range(E):
            for fq in range(4):
                slot = (e * 4 + fq) % 2
                nslot = 1 - slot
                ne, nfq = (e, fq + 1) if fq < 3 else (e + 1, 0)
                if ne < E:
                    load_quarter(nslot, ne, nfq)
                elif half == 0:
                    load_quarter(nslot, 0, 0)
                for tc in range(TH // 512):
                    for fc in range(8):
                        bank = P.bank()
                        for kc in range(8):
                            P.op("pe", lambda h, kc=kc, fc=fc, tc=tc, slot=slot, bank=bank: h.matmul(
                                out=bank.f32, lhsT=w1s[slot][kc][:, fc*128:(fc+1)*128], rhs=xf[kc][:, tc*512:(tc+1)*512],
                                start=(kc == 0), stop=(kc == 7)),
                                reads=[w1sb[slot][kc], xfb[kc]], writes=[bank.buf] if kc == 0 else (), wpart=[bank.buf] if kc > 0 else ())
                        P.op("act", lambda h, fc=fc, slot=slot, bank=bank: h.activation(
                            out=hid[fc][:, :], in_=bank.f32, func=func, bias=b1s[slot][:, fc:fc+1]),
                            reads=[bank.buf, b1sb[slot]], writes=[hidb[fc]])
                        P.bank_free(bank)
                    for tt in range(4):
                        t = tc * 4 + tt
                        for dc in range(2):
                            bank = P.bank()
                            for fc in range(8):
                                P.op("pe", lambda h, fc=fc, tt=tt, dc=dc, slot=slot, bank=bank: h.matmul(
                                    out=bank.f32, lhsT=hid[fc][:, tt*128:(tt+1)*128], rhs=w2s[slot][fc][:, dc*512:(dc+1)*512],
                                    start=(fc == 0), stop=(fc == 7)),
                                    reads=[hidb[fc], w2sb[slot][fc]], writes=[bank.buf] if fc == 0 else (), wpart=[bank.buf] if fc > 0 else ())
                            P.op("dve", lambda h, t=t, dc=dc, e=e, bank=bank: h.scalar_tensor_tensor(
                                out=acc[t][:, dc*512:(dc+1)*512], in0=bank.f32, scalar=comb[:, t*8+e:t*8+e+1],
                                in1=acc[t][:, dc*512:(dc+1)*512], op0=ALU.mult, op1=ALU.add),
                                reads=[bank.buf, combb[t]], wpart=[accb[t]])
                            P.bank_free(bank)
        for t in range(nt):
            for dc in range(2):
                sl = (t * 2 + dc) % 4
                P.op("dve", lambda h, t=t, dc=dc, sl=sl: h.tensor_copy(out=ostage[sl][:, :], in_=acc[t][:, dc*512:(dc+1)*512]),
                     reads=[accb[t]], writes=[ostb[sl]])
                P.dma("act", lambda h, t=t, dc=dc, sl=sl, hoff=hoff: h.dma_start(
                    out=OUT[hoff+t*128:hoff+(t+1)*128, dc*512:(dc+1)*512], in_=ostage[sl][:, :]),
                    reads=[ostb[sl]], stream=f"{name}o")
    P.barrier()
    P.release(m0)


def emit_tree(P, C, io, ntok, name="l1f"):
    m0 = P.mark()
    D = 1024
    ident, identb = C.ident_sb(P)
    nt = ntok // 128
    Xfm = io["XN_fm"]; OUT = io["OUT"]

    gate = [P.salloc(f"{name}_g{t}", [128, 256], BF16) for t in range(nt)]
    gb = [Buf(f"{name}_g{t}") for t in range(nt)]
    lpf = [P.salloc(f"{name}_lpf{c}", [128, ntok], BF16) for c in range(2)]
    lpfb = [Buf(f"{name}_lpf{c}") for c in range(2)]
    xfm = [P.salloc(f"{name}_xf{kc}", [128, ntok], BF16) for kc in range(8)]
    xfb = [Buf(f"{name}_xf{kc}") for kc in range(8)]
    wsb = [P.salloc(f"{name}_w{kc}", [128, D], BF16) for kc in range(8)]
    wbufs = [Buf(f"{name}_w{kc}") for kc in range(8)]
    qfm = [P.salloc(f"{name}_qf{kc}", [128, ntok], BF16) for kc in range(8)]
    qfb = [Buf(f"{name}_qf{kc}") for kc in range(8)]
    rsb = [P.salloc(f"{name}_r{kc}", [128, 256], BF16) for kc in range(8)]
    rbufs = [Buf(f"{name}_r{kc}") for kc in range(8)]
    lv = [P.salloc(f"{name}_lv{c}", [128, D], BF16) for c in range(2)]
    lvb = [Buf(f"{name}_lv{c}") for c in range(2)]
    lpA2 = [P.salloc(f"{name}_lpA{i}", [128, 256], F32) for i in range(2)]
    lpA2b = [Buf(f"{name}_lpA{i}") for i in range(2)]
    lpB2 = [P.salloc(f"{name}_lpB{i}", [128, 256], F32) for i in range(2)]
    lpB2b = [Buf(f"{name}_lpB{i}") for i in range(2)]
    gc2 = [P.salloc(f"{name}_gc{i}", [128, 128], F32) for i in range(2)]
    gc2b = [Buf(f"{name}_gc{i}") for i in range(2)]
    lpbf2 = [P.salloc(f"{name}_lpbf{i}", [128, 256], BF16) for i in range(2)]
    lpbf2b = [Buf(f"{name}_lpbf{i}") for i in range(2)]
    retf = xfm
    retfb = xfb
    borep = P.salloc(f"{name}_bo", [128, D], F32); borepb = Buf("bo")
    P.dma("sp", lambda h: h.dma_start(out=borep[:, :], in_=io["bo_rep"]), writes=[borepb], stream="consts")
    ostage = [P.salloc(f"{name}_os{i}", [128, 512], F32) for i in range(4)]
    ostb = [Buf(f"{name}_os{i}") for i in range(4)]

    for kc in range(8):
        P.dma("sp", lambda h, kc=kc: h.dma_start(out=xfm[kc][:, :], in_=Xfm[kc*128:(kc+1)*128, :]),
              writes=[xfb[kc]], stream=f"{name}xf")
        P.dma("pool", lambda h, kc=kc: h.dma_start(out=wsb[kc][:, :], in_=io["WqT"][kc*128:(kc+1)*128, :]),
              writes=[wbufs[kc]], stream=f"{name}w")
        P.dma("pool", lambda h, kc=kc: h.dma_start(out=rsb[kc][:, :], in_=io["RoutT"][kc*128:(kc+1)*128, :]),
              writes=[rbufs[kc]], stream=f"{name}rt")
    for c in range(2):
        P.dma("pool", lambda h, c=c: h.dma_start(out=lv[c][:, :], in_=io["LV"][c*128:(c+1)*128, :]),
              writes=[lvb[c]], stream=f"{name}rt")

    for oc in range(8):
        for t4 in range(ntok // 512):
            bank = P.bank()
            for kc in range(8):
                P.op("pe", lambda h, kc=kc, oc=oc, t4=t4, bank=bank: h.matmul(
                    out=bank.f32, lhsT=wsb[kc][:, oc*128:(oc+1)*128], rhs=xfm[kc][:, t4*512:(t4+1)*512],
                    start=(kc == 0), stop=(kc == 7)),
                    reads=[wbufs[kc], xfb[kc]], writes=[bank.buf] if kc == 0 else (), wpart=[bank.buf] if kc > 0 else ())
            P.op("act", lambda h, oc=oc, t4=t4, bank=bank: h.copy(out=qfm[oc][:, t4*512:(t4+1)*512], in_=bank.f32),
                 reads=[bank.buf], wpart=[qfb[oc]])
            P.bank_free(bank)
    for t in range(nt):
        bank = P.bank()
        for kc in range(8):
            P.op("pe", lambda h, kc=kc, t=t, bank=bank: h.matmul(
                out=bank.t[0:128, 0:256], lhsT=qfm[kc][:, t*128:(t+1)*128], rhs=rsb[kc][:, :],
                start=(kc == 0), stop=(kc == 7)),
                reads=[qfb[kc], rbufs[kc]], writes=[bank.buf] if kc == 0 else (), wpart=[bank.buf] if kc > 0 else ())
        P.op("act", lambda h, t=t, bank=bank: h.activation(out=gate[t][:, :], in_=bank.t[0:128, 0:256], func=AF.Sigmoid),
             reads=[bank.buf], writes=[gb[t]])
        P.bank_free(bank)
        par = t % 2
        lpA, lpAb, lpB, lpBb = lpA2[par], lpA2b[par], lpB2[par], lpB2b[par]
        gc, gcb = gc2[par], gc2b[par]
        lpbf, lpbfb = lpbf2[par], lpbf2b[par]
        eng = "dve"
        P.op(eng, lambda h, t=t, lpA=lpA: h.tensor_scalar(out=lpA[:, 0:1], in0=gate[t][:, 0:1], scalar1=-1.0, scalar2=1.0, op0=ALU.mult, op1=ALU.add),
             reads=[gb[t]], writes=[lpAb])
        P.op(eng, lambda h, t=t, lpA=lpA: h.tensor_copy(out=lpA[:, 1:2], in_=gate[t][:, 0:1]), reads=[gb[t]], wpart=[lpAb])
        cur, curb, oth, othb = lpA, lpAb, lpB, lpBb
        for lvl in range(1, 8):
            n = 1 << lvl
            P.op(eng, lambda h, t=t, n=n, gc=gc: h.tensor_scalar(out=gc[:, 0:n], in0=gate[t][:, n-1:2*n-1], scalar1=-1.0, scalar2=1.0, op0=ALU.mult, op1=ALU.add),
                 reads=[gb[t]], writes=[gcb])
            P.op(eng, lambda h, cur=cur, oth=oth, n=n, gc=gc: h.tensor_mul(out=oth[:, 0:n], in0=cur[:, 0:n], in1=gc[:, 0:n]),
                 reads=[curb, gcb], writes=[othb])
            P.op(eng, lambda h, cur=cur, oth=oth, t=t, n=n: h.tensor_mul(out=oth[:, n:2*n], in0=cur[:, 0:n], in1=gate[t][:, n-1:2*n-1]),
                 reads=[curb, gb[t]], wpart=[othb])
            cur, curb, oth, othb = oth, othb, cur, curb
        P.op(eng, lambda h, cur=cur, lpbf=lpbf: h.tensor_copy(out=lpbf[:, :], in_=cur[:, :]), reads=[curb], writes=[lpbfb])
        bank = P.bank()
        for c in range(2):
            P.op("pe", lambda h, c=c, bank=bank, lpbf=lpbf: h.transpose(out=bank.bf16[:, c*128:(c+1)*128], in_=lpbf[:, c*128:(c+1)*128], identity=ident[:, :]),
                 reads=[lpbfb, identb], writes=[bank.buf] if c == 0 else (), wpart=[bank.buf] if c else ())
        for c in range(2):
            P.op("dve", lambda h, c=c, t=t, bank=bank: h.tensor_copy(out=lpf[c][:, t*128:(t+1)*128], in_=bank.bf16[:, c*128:(c+1)*128]),
                 reads=[bank.buf], wpart=[lpfb[c]])
        P.bank_free(bank)
    for dc in range(8):
        for t4 in range(ntok // 512):
            bank = P.bank()
            for c in range(2):
                P.op("pe", lambda h, c=c, dc=dc, t4=t4, bank=bank: h.matmul(
                    out=bank.f32, lhsT=lv[c][:, dc*128:(dc+1)*128], rhs=lpf[c][:, t4*512:(t4+1)*512],
                    start=(c == 0), stop=(c == 1)),
                    reads=[lvb[c], lpfb[c]], writes=[bank.buf] if c == 0 else (), wpart=[bank.buf] if c else ())
            P.op("act", lambda h, dc=dc, t4=t4, bank=bank: h.copy(out=retf[dc][:, t4*512:(t4+1)*512], in_=bank.f32),
                 reads=[bank.buf], wpart=[retfb[dc]])
            P.bank_free(bank)
    for kc in range(8):
        P.dma("pool", lambda h, kc=kc: h.dma_start(out=wsb[kc][:, :], in_=io["WoutT"][kc*128:(kc+1)*128, :]),
              writes=[wbufs[kc]], stream=f"{name}w")
    for t in range(nt):
        for dc in range(2):
            bank = P.bank()
            for kc in range(8):
                P.op("pe", lambda h, kc=kc, t=t, dc=dc, bank=bank: h.matmul(
                    out=bank.f32, lhsT=retf[kc][:, t*128:(t+1)*128], rhs=wsb[kc][:, dc*512:(dc+1)*512],
                    start=(kc == 0), stop=(kc == 7)),
                    reads=[wbufs[kc], retfb[kc]], writes=[bank.buf] if kc == 0 else (), wpart=[bank.buf] if kc > 0 else ())
            sl = (t * 2 + dc) % 4
            P.op("dve", lambda h, bank=bank, dc=dc, sl=sl: h.tensor_add(out=ostage[sl][:, :], in0=bank.f32, in1=borep[:, dc*512:(dc+1)*512]),
                 reads=[bank.buf, borepb], writes=[ostb[sl]])
            P.dma("act", lambda h, t=t, dc=dc, sl=sl: h.dma_start(
                out=OUT[t*128:(t+1)*128, dc*512:(dc+1)*512], in_=ostage[sl][:, :]),
                reads=[ostb[sl]], stream=f"{name}o")
            P.bank_free(bank)
    P.barrier()
    P.release(m0)


def prep_tree(Wq, routers, leaf_values, Wout_w):
    D = 1024
    M = 256
    def bitrev(x, bits):
        r = 0
        for i in range(bits):
            r = (r << 1) | ((x >> i) & 1)
        return r
    routP = np.zeros((M, D), routers.dtype)
    for lvl in range(8):
        n = 1 << lvl
        for j in range(n):
            routP[n-1+j] = routers[n-1+bitrev(j, lvl)] if lvl > 0 else routers[0]
    lvP = leaf_values[[bitrev(j, 8) for j in range(M)]]
    return {
        "WqT": np.ascontiguousarray(Wq.T),
        "RoutT": np.ascontiguousarray(routP.T),
        "LV": np.ascontiguousarray(lvP),
        "WoutT": np.ascontiguousarray(Wout_w.T),
    }


def emit_perf(P, C, io, nseq=2, T=2048, name="l2a", cc_tag="l2amax"):
    m0 = P.mark()
    D, H, hd, M = 1024, 16, 64, 256
    c_ = 64.0 ** -0.25
    ratio = M ** -0.5
    EPS = 1e-4
    nkT = T // 128
    Xfm = io["XN_fm"]; OUT = io["OUT"]
    ident, identb = C.ident_sb(P)
    id32, id32b = C.ident_sb32(P)
    nc = P.nc
    ones_row = C.add("c_ones_row32", np.ones((1, 128), np.float32), F32)

    cci = nc.dram_tensor(f"{cc_tag}_in", [128, 8], F32, kind="Internal").ap().opt()
    cco = nc.dram_tensor(f"{cc_tag}_out", [128, 8], F32, kind="Internal").ap().opt()
    ccib, ccob = Buf("cci"), Buf("cco")

    wsb = [P.salloc(f"{name}_w{kc}", [128, D], BF16) for kc in range(8)]
    wbufs = [Buf(f"{name}_w{kc}") for kc in range(8)]
    xfm = [P.salloc(f"{name}_xf{kc}", [128, T], BF16) for kc in range(8)]
    xfb = [Buf(f"{name}_xf{kc}") for kc in range(8)]
    kall = [P.salloc(f"{name}_ka{kc}", [128, T], BF16) for kc in range(8)]
    kab = [Buf(f"{name}_ka{kc}") for kc in range(8)]
    qfm = [P.salloc(f"{name}_qf{kc}", [128, T], BF16) for kc in range(8)]
    qfb = [Buf(f"{name}_qf{kc}") for kc in range(8)]
    vtm = [P.salloc(f"{name}_vt{t}", [128, D], BF16) for t in range(nkT)]
    vtb = [Buf(f"{name}_vt{t}") for t in range(nkT)]
    prj = P.salloc(f"{name}_prj", [128, M], BF16)
    prjb = Buf(f"{name}_prj")
    P.dma("pool", lambda h: h.dma_start(out=prj[:, :], in_=io["ProjPadT"]), writes=[prjb], stream=f"{name}prj")
    prjB = P.salloc(f"{name}_prjB", [128, M], BF16)
    prjBb = Buf(f"{name}_prjB")
    P.dma("pool", lambda h: h.dma_start(out=prjB[:, :], in_=io["ProjPadB"]), writes=[prjBb], stream=f"{name}prj")
    onesr = P.salloc(f"{name}_onr", [128, 128], F32)
    onesrb = Buf(f"{name}_onr")
    P.dma("sp", lambda h: h.dma_start(out=onesr[0:1, :], in_=ones_row), writes=[onesrb], stream="consts")
    onec = P.salloc(f"{name}_one", [128, 1], BF16)
    onecb = Buf(f"{name}_one")
    P.op("dve", lambda h: h.memset(onec[:, :], 1.0), writes=[onecb])
    gmx = P.salloc(f"{name}_gmx", [128, 16], F32)
    gmxb = Buf(f"{name}_gmx")
    stab = P.salloc(f"{name}_stab", [128, 1], F32)
    stabb = Buf(f"{name}_stab")
    sqt = P.salloc(f"{name}_sq", [128, T], BF16)
    sqb = Buf(f"{name}_sq")
    diag = P.salloc(f"{name}_diag", [128, 2*nkT], F32)
    diagb = Buf(f"{name}_diag")
    brx = P.salloc(f"{name}_brx", [128, T], F32)
    brf = brx; brfb = Buf(f"{name}_brf")
    brgb = Buf(f"{name}_brg")
    brep = P.salloc(f"{name}_brep", [128, T], F32)
    brepb = Buf(f"{name}_brep")
    qpad = P.salloc(f"{name}_qpad", [128, T], BF16)
    qpadb = Buf(f"{name}_qpad")
    qpad2 = P.salloc(f"{name}_qpad2", [128, T], BF16)
    qpad2b = Buf(f"{name}_qpad2")
    qpt = [P.salloc(f"{name}_qpt{c}", [128, T], BF16) for c in range(2)]
    qptb = [Buf(f"{name}_qpt{c}") for c in range(2)]
    kpt = [P.salloc(f"{name}_kp{t}", [128, M], BF16) for t in range(nkT)]
    kpb = [Buf(f"{name}_kp{t}") for t in range(nkT)]
    ctxm = P.salloc(f"{name}_cxm", [128, 2*hd], BF16)
    ctxmb = Buf(f"{name}_cxm")
    kcs = P.salloc(f"{name}_kcs", [128, 2], BF16)
    kcsb = Buf(f"{name}_kcs")
    dinv = P.salloc(f"{name}_dinv", [128, nkT], F32)
    dinvb = Buf(f"{name}_dinv")
    aofm = xfm
    aofb = xfb
    borep = P.salloc(f"{name}_bo", [128, D], F32); borepb = Buf("bo")
    P.dma("sp", lambda h: h.dma_start(out=borep[:, :], in_=io["bo_rep"]), writes=[borepb], stream="consts")
    ostage = [P.salloc(f"{name}_os{i}", [128, 512], F32) for i in range(2)]
    ostb = [Buf(f"{name}_os{i}") for i in range(2)]
    st = P.salloc(f"{name}_stt", [128, 8], F32)
    stb = Buf(f"{name}_stt")

    def loadw(Wt):
        for kc in range(8):
            P.dma("pool", lambda h, kc=kc, Wt=Wt: h.dma_start(out=wsb[kc][:, :], in_=Wt[kc*128:(kc+1)*128, :]),
                  writes=[wbufs[kc]], stream=f"{name}w")

    def proj_fm(dst, dstb, Toff):
        for oc in range(8):
            for t4 in range(T // 512):
                bank = P.bank()
                for kc in range(8):
                    P.op("pe", lambda h, kc=kc, oc=oc, t4=t4, bank=bank: h.matmul(
                        out=bank.f32, lhsT=wsb[kc][:, oc*128:(oc+1)*128], rhs=xfm[kc][:, t4*512:(t4+1)*512],
                        start=(kc == 0), stop=(kc == 7)),
                        reads=[wbufs[kc], xfb[kc]], writes=[bank.buf] if kc == 0 else (), wpart=[bank.buf] if kc > 0 else ())
                P.op("act", lambda h, oc=oc, t4=t4, bank=bank, Toff=Toff: h.copy(out=dst[oc][:, Toff+t4*512:Toff+(t4+1)*512], in_=bank.f32),
                     reads=[bank.buf], wpart=[dstb[oc]])
                P.bank_free(bank)

    def loadx(s):
        for kc in range(8):
            P.dma("sp", lambda h, kc=kc, s=s: h.dma_start(out=xfm[kc][:, :], in_=Xfm[kc*128:(kc+1)*128, s*T:(s+1)*T]),
                  writes=[xfb[kc]], stream=f"{name}xf")

    P.op("dve", lambda h: h.memset(gmx[:, :], -1e30), writes=[gmxb])
    for s in range(nseq):
        loadx(s)
        loadw(io["Wk"])
        proj_fm(kall, kab, 0)
        for h in range(H):
            kc8, rb = (h * hd) // 128, (h * hd) % 128
            for nt2 in range(nkT // 2):
                bank = P.bank()
                P.op("pe", lambda h_, kc8=kc8, rb=rb, nt2=nt2, s=s, bank=bank: h_.matmul(
                    out=bank.f32[0:128, 0:256], lhsT=kall[kc8][rb:rb+hd, nt2*256:nt2*256+128], rhs=(prj if rb == 0 else prjB)[rb:rb+hd, :],
                    start=True, stop=True),
                    reads=[kab[kc8], prjb, prjBb], writes=[bank.buf])
                P.op("pe", lambda h_, kc8=kc8, rb=rb, nt2=nt2, s=s, bank=bank: h_.matmul(
                    out=bank.f32[0:128, 256:512], lhsT=kall[kc8][rb:rb+hd, nt2*256+128:nt2*256+256], rhs=(prj if rb == 0 else prjB)[rb:rb+hd, :],
                    start=True, stop=True),
                    reads=[kab[kc8], prjb, prjBb], wpart=[bank.buf])
                P.op("dve", lambda h_, bank=bank: h_.reduce_max(out=st[:, 0:1], in_=bank.f32, axis=AX.X),
                     reads=[bank.buf], writes=[stb])
                P.op("dve", lambda h_: h_.tensor_max(out=gmx[:, 0:1], in0=gmx[:, 0:1], in1=st[:, 0:1]),
                     reads=[stb], wpart=[gmxb])
                P.bank_free(bank)
    bank = P.bank()
    P.op("pe", lambda h, bank=bank: h.transpose(out=bank.f32[0:1, 0:128], in_=gmx[:, 0:1], identity=id32[:, :]),
         reads=[gmxb, id32b], writes=[bank.buf])
    P.op("dve", lambda h, bank=bank: h.reduce_max(out=gmx[0:1, 1:2], in_=bank.f32[0:1, 0:128], axis=AX.X),
         reads=[bank.buf], wpart=[gmxb])
    P.bank_free(bank)
    P.op("dve", lambda h: h.tensor_scalar(out=gmx[:, 2:3], in0=gmx[:, 1:2], scalar1=c_, scalar2=None, op0=ALU.mult),
         reads=[gmxb], wpart=[gmxb])
    P.op("dve", lambda h: h.memset(gmx[:, 8:16], -1e30), wpart=[gmxb])
    P.op("dve", lambda h: h.tensor_copy(out=gmx[0:1, 8:9], in_=gmx[0:1, 2:3]), reads=[gmxb], wpart=[gmxb])
    P.dma("sp", lambda h: h.dma_start(out=cci, in_=gmx[:, 8:16]), reads=[gmxb], writes=[ccib], stream=f"{name}cc")
    P.op("pool", lambda h: h.collective_compute("AllReduce", ALU.max, replica_groups=[list(range(8))], ins=[cci], outs=[cco]),
         reads=[ccib], writes=[ccob])
    P.dma("sp", lambda h: h.dma_start(out=gmx[:, 8:16], in_=cco), reads=[ccob], writes=[gmxb], stream=f"{name}cc")
    bank = P.bank()
    P.op("pe", lambda h, bank=bank: h.matmul(out=bank.f32[0:128, 0:1], lhsT=onesr[0:1, 0:128], rhs=gmx[0:1, 8:9],
                                             start=True, stop=True),
         reads=[onesrb, gmxb], writes=[bank.buf])
    P.op("dve", lambda h, bank=bank: h.tensor_copy(out=stab[:, :], in_=bank.f32[0:128, 0:1]), reads=[bank.buf], writes=[stabb])
    P.bank_free(bank)

    for s in range(nseq):
        loadx(s)
        loadw(io["Wk"]); proj_fm(kall, kab, 0)
        loadw(io["Wq"]); proj_fm(qfm, qfb, 0)
        loadw(io["Wv"])
        for t in range(nkT):
            for dc in range(2):
                bank = P.bank()
                for kc in range(8):
                    P.op("pe", lambda h, kc=kc, t=t, dc=dc, bank=bank: h.matmul(
                        out=bank.f32, lhsT=xfm[kc][:, t*128:(t+1)*128], rhs=wsb[kc][:, dc*512:(dc+1)*512],
                        start=(kc == 0), stop=(kc == 7)),
                        reads=[wbufs[kc], xfb[kc]], writes=[bank.buf] if kc == 0 else (), wpart=[bank.buf] if kc > 0 else ())
                P.op("dve", lambda h, t=t, dc=dc, bank=bank: h.tensor_copy(out=vtm[t][:, dc*512:(dc+1)*512], in_=bank.f32),
                     reads=[bank.buf], writes=[vtb[t]] if dc == 0 else (), wpart=[vtb[t]] if dc else ())
                P.bank_free(bank)
        for h in range(H):
            kc8, rb = (h * hd) // 128, (h * hd) % 128
            for which, src_t, srcb, off in ((0, qfm[kc8], qfb[kc8], 0), (1, kall[kc8], kab[kc8], 0)):
                P.op("act", lambda h_, src_t=src_t, srcb=srcb, off=off, rb=rb: h_.activation(
                    out=sqt[0:hd, :], in_=src_t[rb:rb+hd, off:off+T], func=AF.Square),
                    reads=[srcb], writes=[sqb])
                bks = [P.bank(), P.bank()]
                for nt in range(nkT):
                    bank = bks[nt % 2]
                    P.op("pe", lambda h_, nt=nt, bank=bank: h_.matmul(
                        out=bank.f32[0:128, 0:1], lhsT=sqt[0:hd, nt*128:(nt+1)*128], rhs=onec[0:hd, :],
                        start=True, stop=True),
                        reads=[sqb, onecb], writes=[bank.buf])
                    P.op("dve", lambda h_, which=which, nt=nt, bank=bank: h_.tensor_scalar(
                        out=diag[:, which*nkT+nt:which*nkT+nt+1], in0=bank.f32[0:128, 0:1], scalar1=0.5*c_*c_, scalar2=None, op0=ALU.mult),
                        reads=[bank.buf], writes=[diagb] if (which == 0 and nt == 0) else (), wpart=[diagb] if (which or nt) else ())
                P.bank_free(*bks)
            P.op("dve", lambda h_: h_.tensor_scalar(out=diag[:, nkT:2*nkT], in0=diag[:, nkT:2*nkT], scalar1=stab[:, 0:1], scalar2=None, op0=ALU.add),
                 reads=[stabb, diagb], wpart=[diagb])
            if s == 0 and h == 0 and io.get("DBG_DIAG") is not None:
                P.dma("act", lambda h_: h_.dma_start(out=io["DBG_DIAG"], in_=diag[:, :]), reads=[diagb], stream=f"{name}dbg")
                P.dma("act", lambda h_: h_.dma_start(out=io["DBG_QF"], in_=qfm[0][:, :]), reads=[qfb[0]], stream=f"{name}dbg")
                P.dma("act", lambda h_: h_.dma_start(out=io["DBG_SQ"], in_=sqt[0:64, :]), reads=[sqb], stream=f"{name}dbg")
            for nt in range(nkT):
                par = nt % 2
                bank = P.bank()
                P.op("pe", lambda h_, nt=nt, bank=bank, kc8=kc8, rb=rb: h_.matmul(
                    out=bank.f32[0:128, 0:M], lhsT=kall[kc8][rb:rb+hd, nt*128:(nt+1)*128], rhs=(prj if rb == 0 else prjB)[rb:rb+hd, :],
                    start=True, stop=True),
                    reads=[kab[kc8], prjb, prjBb], writes=[bank.buf])
                P.op("dve", lambda h_, nt=nt, par=par: h_.tensor_scalar(out=st[:, 1+par:2+par], in0=diag[:, nkT+nt:nkT+nt+1], scalar1=-1.0, scalar2=None, op0=ALU.mult),
                     reads=[diagb], wpart=[stb])
                P.op("act", lambda h_, nt=nt, par=par, bank=bank: h_.activation(out=kpt[nt][:, :], in_=bank.f32[0:128, 0:M], func=AF.Exp, scale=c_, bias=st[:, 1+par:2+par]),
                     reads=[bank.buf, stb], writes=[kpb[nt]])
                P.op("dve", lambda h_, nt=nt: h_.tensor_scalar(out=kpt[nt][:, :], in0=kpt[nt][:, :], scalar1=ratio, scalar2=ratio*EPS, op0=ALU.mult, op1=ALU.add),
                     reads=[kpb[nt]], wpart=[kpb[nt]])
                P.bank_free(bank)
            for ng in range(ceil_div(nkT, 4)):
                bank2 = P.bank()
                for nt in range(ng*4, min((ng+1)*4, nkT)):
                    bank = P.bank()
                    P.op("pe", lambda h_, nt=nt, bank=bank, kc8=kc8, rb=rb: h_.matmul(
                        out=bank.f32[0:128, 0:M], lhsT=qfm[kc8][rb:rb+hd, nt*128:(nt+1)*128], rhs=(prj if rb == 0 else prjB)[rb:rb+hd, :],
                        start=True, stop=True),
                        reads=[qfb[kc8], prjb, prjBb], writes=[bank.buf])
                    P.op("dve", lambda h_, bank=bank: h_.reduce_max(out=st[:, 0:1], in_=bank.f32[0:128, 0:M], axis=AX.X),
                         reads=[bank.buf], writes=[stb])
                    P.op("dve", lambda h_, nt=nt: h_.scalar_tensor_tensor(out=st[:, 2:3], in0=diag[:, nt:nt+1], scalar=1.0/c_, in1=st[:, 0:1], op0=ALU.mult, op1=ALU.add),
                         reads=[stb, diagb], wpart=[stb])
                    P.op("pe", lambda h_, nt=nt, bank2=bank2: h_.transpose(out=bank2.f32[0:1, (nt % 4)*128:(nt % 4)*128+128], in_=st[:, 2:3], identity=id32[:, :]),
                         reads=[stb, id32b], wpart=[bank2.buf])
                    P.bank_free(bank)
                P.op("dve", lambda h_, ng=ng, bank2=bank2: h_.tensor_copy(out=brf[0:1, ng*512:(ng+1)*512], in_=bank2.f32[0:1, 0:512]),
                     reads=[bank2.buf], writes=[brfb] if ng == 0 else (), wpart=[brfb] if ng else ())
                P.bank_free(bank2)
            P.op("dve", lambda h_, kc8=kc8, rb=rb: h_.tensor_copy(out=qpad[0:hd, :], in_=qfm[kc8][rb:rb+hd, :]),
                 reads=[qfb[kc8]], writes=[qpadb])
            P.op("dve", lambda h_: h_.tensor_scalar(out=qpad2[0:1, :], in0=brf[0:1, :], scalar1=-1.0, scalar2=None, op0=ALU.mult),
                 reads=[brfb], writes=[qpad2b])
            P.op("dve", lambda h_: h_.tensor_copy(out=qpad[64:65, :], in_=qpad2[0:1, :]),
                 reads=[qpad2b], wpart=[qpadb])
            P.op("dve", lambda h_: h_.tensor_add(out=brx[64:65, :], in0=brf[0:1, :], in1=qpad2[0:1, :]),
                 reads=[brfb, qpad2b], writes=[brgb])
            P.op("dve", lambda h_: h_.tensor_scalar(out=qpad2[64:65, :], in0=brx[64:65, :], scalar1=-1.0, scalar2=None, op0=ALU.mult),
                 reads=[brgb], wpart=[qpad2b])
            for mc in range(2):
                for j in range(T // 512):
                    bank = P.bank()
                    P.op("pe", lambda h_, mc=mc, j=j, bank=bank: h_.matmul(
                        out=bank.f32, lhsT=prj[0:65, mc*128:(mc+1)*128], rhs=qpad[0:65, j*512:(j+1)*512],
                        start=True, stop=False),
                        reads=[prjb, qpadb], writes=[bank.buf])
                    P.op("pe", lambda h_, mc=mc, j=j, bank=bank: h_.matmul(
                        out=bank.f32, lhsT=prj[64:65, mc*128:(mc+1)*128], rhs=qpad2[64:65, j*512:(j+1)*512],
                        start=False, stop=True),
                        reads=[prjb, qpad2b], wpart=[bank.buf])
                    P.op("act", lambda h_, mc=mc, j=j, bank=bank: h_.activation(out=qpt[mc][:, j*512:(j+1)*512], in_=bank.f32, func=AF.Exp, scale=c_),
                         reads=[bank.buf], writes=[qptb[mc]] if j == 0 else (), wpart=[qptb[mc]] if j else ())
                    P.bank_free(bank)
                P.op("dve", lambda h_, mc=mc: h_.tensor_scalar(out=qpt[mc][:, :], in0=qpt[mc][:, :], scalar1=ratio, scalar2=ratio*EPS, op0=ALU.mult, op1=ALU.add),
                     reads=[qptb[mc]], wpart=[qptb[mc]])
            for mc in range(2):
                bank = P.bank()
                for nt in range(nkT):
                    P.op("pe", lambda h_, mc=mc, nt=nt, bank=bank, first=(nt == 0), last=(nt == nkT-1): h_.matmul(
                        out=bank.f32[0:128, 0:1], lhsT=kpt[nt][:, mc*128:(mc+1)*128], rhs=onec[0:128, :],
                        start=first, stop=last),
                        reads=[kpb[nt], onecb], writes=[bank.buf] if nt == 0 else (), wpart=[bank.buf] if nt else ())
                P.op("dve", lambda h_, mc=mc, bank=bank: h_.tensor_copy(out=kcs[:, mc:mc+1], in_=bank.f32[0:128, 0:1]),
                     reads=[bank.buf], writes=[kcsb] if mc == 0 else (), wpart=[kcsb] if mc else ())
                P.bank_free(bank)
            for j in range(T // 512):
                bank = P.bank()
                for mc in range(2):
                    P.op("pe", lambda h_, mc=mc, j=j, bank=bank: h_.matmul(
                        out=bank.f32[0:1, 0:512], lhsT=kcs[:, mc:mc+1], rhs=qpt[mc][:, j*512:(j+1)*512],
                        start=(mc == 0), stop=(mc == 1)),
                        reads=[kcsb, qptb[mc]], writes=[bank.buf] if mc == 0 else (), wpart=[bank.buf] if mc else ())
                P.op("dve", lambda h_, j=j, bank=bank: h_.reciprocal(out=brx[0:1, j*512:(j+1)*512], in_=bank.f32[0:1, 0:512]),
                     reads=[bank.buf], writes=[brfb] if j == 0 else (), wpart=[brfb] if j else ())
                P.bank_free(bank)
            for j in range(T // 512):
                bank = P.bank()
                P.op("pe", lambda h_, j=j, bank=bank: h_.matmul(
                    out=bank.f32, lhsT=onesr[0:1, 0:128], rhs=brx[0:1, j*512:(j+1)*512], start=True, stop=True),
                    reads=[onesrb, brfb], writes=[bank.buf])
                P.op("dve", lambda h_, j=j, bank=bank: h_.tensor_copy(out=brep[:, j*512:(j+1)*512], in_=bank.f32),
                     reads=[bank.buf], writes=[brepb] if j == 0 else (), wpart=[brepb] if j else ())
                P.bank_free(bank)
            for mc in range(2):
                P.op("dve", lambda h_, mc=mc: h_.tensor_mul(out=qpt[mc][:, :], in0=qpt[mc][:, :], in1=brep[:, :]),
                     reads=[brepb, qptb[mc]], wpart=[qptb[mc]])
            for mc in range(2):
                bank = P.bank()
                for nt in range(nkT):
                    P.op("pe", lambda h_, mc=mc, nt=nt, bank=bank, h=h, first=(nt == 0), last=(nt == nkT-1): h_.matmul(
                        out=bank.f32[0:128, 0:hd], lhsT=kpt[nt][:, mc*128:(mc+1)*128], rhs=vtm[nt][:, h*hd:(h+1)*hd],
                        start=first, stop=last),
                        reads=[kpb[nt], vtb[nt]], writes=[bank.buf] if nt == 0 else (), wpart=[bank.buf] if nt else ())
                P.op("dve", lambda h_, mc=mc, bank=bank: h_.tensor_copy(out=ctxm[:, mc*hd:(mc+1)*hd], in_=bank.f32[0:128, 0:hd]),
                     reads=[bank.buf], writes=[ctxmb] if mc == 0 else (), wpart=[ctxmb] if mc else ())
                P.bank_free(bank)
            if s == 0 and h == 0 and io.get("DBG_KP0") is not None:
                P.dma("act", lambda h_: h_.dma_start(out=io["DBG_KP0"], in_=kpt[0][:, :]), reads=[kpb[0]], stream=f"{name}dbg")
                P.dma("act", lambda h_: h_.dma_start(out=io["DBG_QP0"], in_=qpt[0][:, :]), reads=[qptb[0]], stream=f"{name}dbg")
                P.dma("act", lambda h_: h_.dma_start(out=io["DBG_KCS"], in_=kcs[:, :]), reads=[kcsb], stream=f"{name}dbg")
                P.dma("act", lambda h_: h_.dma_start(out=io["DBG_CTX"], in_=ctxm[:, :]), reads=[ctxmb], stream=f"{name}dbg")
                P.dma("act", lambda h_: h_.dma_start(out=io["DBG_STAB"], in_=stab[:, :]), reads=[stabb], stream=f"{name}dbg")
                P.dma("act", lambda h_: h_.dma_start(out=io["DBG_BRF"], in_=brx[0:1, :]), reads=[brfb, qpadb], stream=f"{name}dbg")
            for j in range(T // 512):
                bank = P.bank()
                for mc in range(2):
                    P.op("pe", lambda h_, mc=mc, j=j, bank=bank: h_.matmul(
                        out=bank.f32[0:hd, 0:512], lhsT=ctxm[:, mc*hd:(mc+1)*hd], rhs=qpt[mc][:, j*512:(j+1)*512],
                        start=(mc == 0), stop=(mc == 1)),
                        reads=[ctxmb, qptb[mc]], writes=[bank.buf] if mc == 0 else (), wpart=[bank.buf] if mc else ())
                P.op("dve", lambda h_, j=j, bank=bank, kc8=kc8, rb=rb: h_.tensor_copy(out=aofm[kc8][rb:rb+hd, j*512:(j+1)*512], in_=bank.f32[0:hd, 0:512]),
                     reads=[bank.buf], wpart=[aofb[kc8]])
                P.bank_free(bank)
        loadw(io["Wo"])
        for t in range(nkT):
            for dc in range(2):
                bank = P.bank()
                for kc in range(8):
                    P.op("pe", lambda h, kc=kc, t=t, dc=dc, bank=bank: h.matmul(
                        out=bank.f32, lhsT=aofm[kc][:, t*128:(t+1)*128], rhs=wsb[kc][:, dc*512:(dc+1)*512],
                        start=(kc == 0), stop=(kc == 7)),
                        reads=[wbufs[kc], aofb[kc]], writes=[bank.buf] if kc == 0 else (), wpart=[bank.buf] if kc > 0 else ())
                sl = (t * 2 + dc) % 2
                P.op("dve", lambda h, bank=bank, dc=dc, sl=sl: h.tensor_add(out=ostage[sl][:, :], in0=bank.f32, in1=borep[:, dc*512:(dc+1)*512]),
                     reads=[bank.buf, borepb], writes=[ostb[sl]])
                P.dma("act", lambda h, t=t, dc=dc, sl=sl, s=s: h.dma_start(
                    out=OUT[s*T+t*128:s*T+(t+1)*128, dc*512:(dc+1)*512], in_=ostage[sl][:, :]),
                    reads=[ostb[sl]], stream=f"{name}o")
                P.bank_free(bank)
    P.barrier()
    P.release(m0)


def emit_glu(P, C, io, ntok, name="l2f"):
    m0 = P.mark()
    D, Hf, E = 1024, 2048, 8
    Xfm = io["X_fm"]; OUT = io["OUT"]
    ident, identb = C.ident_sb(P)

    gwf = P.salloc(f"{name}_gwf", [128, 8*8], F32)
    gwfb = Buf(f"{name}_gwf")
    gw = P.salloc(f"{name}_gw", [128, 8*8], BF16)
    gwb = Buf(f"{name}_gw")
    gwl = P.salloc(f"{name}_gwl", [128, 8*8], BF16)
    gwlb = Buf(f"{name}_gwl")
    for kc in range(8):
        P.dma("sp", lambda h, kc=kc: h.dma_start(out=gwf[:, kc*8:(kc+1)*8], in_=io["Wg"][kc*128:(kc+1)*128, :]),
              wpart=[gwfb], stream=f"{name}gwf")
    P.op("dve", lambda h: h.tensor_copy(out=gw[:, :], in_=gwf[:, :]), reads=[gwfb], writes=[gwb])
    P.op("dve", lambda h: h.tensor_sub(out=gwl[:, :], in0=gwf[:, :], in1=gw[:, :]), reads=[gwfb, gwb], writes=[gwlb])

    nhalf = 2
    TH = ntok // nhalf
    nt = TH // 128
    xfw = P.salloc(f"{name}_xfw", [128, 8*TH], BF16)
    xfwb = Buf(f"{name}_xfw")
    xf = [xfw[:, kc*TH:(kc+1)*TH] for kc in range(8)]
    xfb = [xfwb for kc in range(8)]
    xl = [P.salloc(f"{name}_xl{kc}", [128, 512], BF16) for kc in range(8)]
    xlb = [Buf(f"{name}_xl{kc}") for kc in range(8)]
    acc = [P.salloc(f"{name}_ac{t}", [128, D], F32) for t in range(nt)]
    accb = [Buf(f"{name}_ac{t}") for t in range(nt)]
    w1w = [P.salloc(f"{name}_w1w{s}", [128, 8192], BF16) for s in range(2)]
    w1wb = [Buf(f"{name}_w1w{s}") for s in range(2)]
    w2w = [P.salloc(f"{name}_w2w{s}", [128, 8192], BF16) for s in range(1)]
    w2wb = [Buf(f"{name}_w2w{s}") for s in range(1)]
    w3w = [P.salloc(f"{name}_w3w{s}", [128, 8192], BF16) for s in range(1)]
    w3wb = [Buf(f"{name}_w3w{s}") for s in range(1)]
    w1s = [[w1w[s][:, kc*1024:(kc+1)*1024] for kc in range(8)] for s in range(2)]
    w1sb = [[w1wb[s] for kc in range(8)] for s in range(2)]
    w2s = [[w2w[0][:, kc*1024:(kc+1)*1024] for kc in range(8)]]
    w2sb = [[w2wb[0] for kc in range(8)]]
    w3s = [[w3w[0][:, kc*1024:(kc+1)*1024] for kc in range(8)]]
    w3sb = [[w3wb[0] for kc in range(8)]]
    usil = P.salloc(f"{name}_usil", [128, 512], BF16)
    usilb = Buf(f"{name}_usil")
    hid = [P.salloc(f"{name}_hid{fc}", [128, 512], BF16) for fc in range(8)]
    hidb = [Buf(f"{name}_hid{fc}") for fc in range(8)]
    comb = P.salloc(f"{name}_comb", [128, 8*nt], F32)
    combb = [Buf(f"{name}_cb{t}") for t in range(nt)]
    gs = P.salloc(f"{name}_gs", [128, 16], F32)
    gsb = Buf(f"{name}_gs")
    ostage = [P.salloc(f"{name}_os{i}", [128, 512], F32) for i in range(4)]
    ostb = [Buf(f"{name}_os{i}") for i in range(4)]

    def load_half(slot, e, fh):
        def l1(h, e=e, fh=fh, slot=slot):
            srcv = io["W1"][e*1024:(e+1)*1024, fh*1024:(fh+1)*1024].rearrange("(kc p) j -> p kc j", p=128)
            return h.dma_start(out=w1w[slot][:, :].rearrange("p (kc j) -> p kc j", kc=8), in_=srcv)
        P.dma("pool", l1, writes=[w1wb[slot]], stream=f"{name}w1")
        def l2(h, e=e, fh=fh):
            srcv = io["W2"][e*1024:(e+1)*1024, fh*1024:(fh+1)*1024].rearrange("(kc p) j -> p kc j", p=128)
            return h.dma_start(out=w2w[0][:, :].rearrange("p (kc j) -> p kc j", kc=8), in_=srcv)
        P.dma("pool", l2, writes=[w2wb[0]], stream=f"{name}w2")
        def l3(h, e=e, fh=fh):
            srcv = io["W3"][e*2048+fh*1024:e*2048+(fh+1)*1024, :].rearrange("(kc p) j -> p kc j", p=128)
            return h.dma_start(out=w3w[0][:, :].rearrange("p (kc j) -> p kc j", kc=8), in_=srcv)
        P.dma("pool", l3, writes=[w3wb[0]], stream=f"{name}w3")

    for half in range(nhalf):
        hoff = half * TH
        def lx(h, hoff=hoff):
            srcv = Xfm.rearrange("(kc p) n -> p kc n", p=128)[:, :, hoff:hoff+TH]
            return h.dma_start(out=xfw[:, :].rearrange("p (kc n) -> p kc n", kc=8), in_=srcv)
        P.dma("sp", lx, writes=[xfwb], stream=f"{name}xf")
        for t in range(nt):
            if t % 4 == 0:
                for kc in range(8):
                    P.dma("sp", lambda h, kc=kc, hoff=hoff, t=t: h.dma_start(
                        out=xl[kc][:, :], in_=io["X_lo_fm"][kc*128:(kc+1)*128, hoff+t*128:hoff+t*128+512]),
                        writes=[xlb[kc]], stream=f"{name}xl")
            xo = (t % 4) * 128
            bank = P.bank()
            for kc in range(8):
                P.op("pe", lambda h, kc=kc, t=t, bank=bank: h.matmul(
                    out=bank.t[0:128, 0:8], lhsT=xf[kc][:, t*128:(t+1)*128], rhs=gw[:, kc*8:(kc+1)*8],
                    start=(kc == 0), stop=False),
                    reads=[xfb[kc], gwb], writes=[bank.buf] if kc == 0 else (), wpart=[bank.buf] if kc > 0 else ())
            for kc in range(8):
                P.op("pe", lambda h, kc=kc, t=t, xo=xo, bank=bank: h.matmul(
                    out=bank.t[0:128, 0:8], lhsT=xl[kc][:, xo:xo+128], rhs=gw[:, kc*8:(kc+1)*8],
                    start=False, stop=False),
                    reads=[xlb[kc], gwb], wpart=[bank.buf])
            for kc in range(8):
                P.op("pe", lambda h, kc=kc, t=t, bank=bank: h.matmul(
                    out=bank.t[0:128, 0:8], lhsT=xf[kc][:, t*128:(t+1)*128], rhs=gwl[:, kc*8:(kc+1)*8],
                    start=False, stop=(kc == 7)),
                    reads=[xfb[kc], gwlb], wpart=[bank.buf])
            gl = comb[:, t*8:(t+1)*8]
            P.op("dve", lambda h, bank=bank, gl=gl: h.tensor_copy(out=gl, in_=bank.t[0:128, 0:8]),
                 reads=[bank.buf], writes=[combb[t]])
            P.bank_free(bank)
            P.op("dve", lambda h, gl=gl: h.reduce_max(out=gs[:, 0:1], in_=gl, axis=AX.X), reads=[combb[t]], writes=[gsb])
            P.op("dve", lambda h, gl=gl: h.tensor_scalar(out=gs[:, 2:10], in0=gl, scalar1=gs[:, 0:1], scalar2=None, op0=ALU.is_ge),
                 reads=[combb[t], gsb], wpart=[gsb])
            P.op("dve", lambda h, gl=gl: h.scalar_tensor_tensor(out=gs[:, 2:10], in0=gs[:, 2:10], scalar=-1e30, in1=gl, op0=ALU.mult, op1=ALU.add),
                 reads=[combb[t], gsb], wpart=[gsb])
            P.op("dve", lambda h: h.reduce_max(out=gs[:, 1:2], in_=gs[:, 2:10], axis=AX.X), reads=[gsb], wpart=[gsb])
            P.op("dve", lambda h: h.tensor_scalar(out=gs[:, 10:11], in0=gs[:, 0:1], scalar1=gs[:, 1:2], scalar2=None, op0=ALU.subtract),
                 reads=[gsb], wpart=[gsb])
            P.op("act", lambda h: h.activation(out=gs[:, 11:12], in_=gs[:, 10:11], func=AF.Sigmoid), reads=[gsb], wpart=[gsb])
            P.op("dve", lambda h: h.tensor_scalar(out=gs[:, 12:13], in0=gs[:, 11:12], scalar1=-1.0, scalar2=1.0, op0=ALU.mult, op1=ALU.add),
                 reads=[gsb], wpart=[gsb])
            P.op("dve", lambda h: h.tensor_scalar(out=gs[:, 2:10], in0=gs[:, 2:10], scalar1=gs[:, 1:2], scalar2=None, op0=ALU.is_ge),
                 reads=[gsb], wpart=[gsb])
            P.op("dve", lambda h, gl=gl: h.tensor_scalar(out=gl, in0=gl, scalar1=gs[:, 0:1], scalar2=None, op0=ALU.is_ge),
                 reads=[gsb], wpart=[combb[t]])
            P.op("dve", lambda h, gl=gl: h.tensor_scalar(out=gl, in0=gl, scalar1=gs[:, 11:12], scalar2=None, op0=ALU.mult),
                 reads=[gsb], wpart=[combb[t]])
            P.op("dve", lambda h, gl=gl: h.scalar_tensor_tensor(out=gl, in0=gs[:, 2:10], scalar=gs[:, 12:13], in1=gl, op0=ALU.mult, op1=ALU.add),
                 reads=[gsb], wpart=[combb[t]])
        for t in range(nt):
            P.op("dve", lambda h, t=t: h.memset(acc[t][:, :], 0.0), writes=[accb[t]])
        for e in range(E):
            for fh in range(2):
                slot = 0
                load_half(0, e, fh)
                for tc in range(TH // 512):
                    for fc in range(8):
                        bank = P.bank()
                        for kc in range(8):
                            P.op("pe", lambda h, kc=kc, fc=fc, tc=tc, slot=slot, bank=bank: h.matmul(
                                out=bank.f32, lhsT=w1s[slot][kc][:, fc*128:(fc+1)*128], rhs=xf[kc][:, tc*512:(tc+1)*512],
                                start=(kc == 0), stop=(kc == 7)),
                                reads=[w1sb[slot][kc], xfb[kc]], writes=[bank.buf] if kc == 0 else (), wpart=[bank.buf] if kc > 0 else ())
                        P.op("act", lambda h, bank=bank: h.activation(out=usil[:, :], in_=bank.f32, func=AF.Silu),
                             reads=[bank.buf], writes=[usilb])
                        P.bank_free(bank)
                        bank = P.bank()
                        for kc in range(8):
                            P.op("pe", lambda h, kc=kc, fc=fc, tc=tc, slot=slot, bank=bank: h.matmul(
                                out=bank.f32, lhsT=w2s[slot][kc][:, fc*128:(fc+1)*128], rhs=xf[kc][:, tc*512:(tc+1)*512],
                                start=(kc == 0), stop=(kc == 7)),
                                reads=[w2sb[slot][kc], xfb[kc]], writes=[bank.buf] if kc == 0 else (), wpart=[bank.buf] if kc > 0 else ())
                        P.op("dve", lambda h, fc=fc, bank=bank: h.tensor_mul(out=hid[fc][:, :], in0=bank.f32, in1=usil[:, :]),
                             reads=[bank.buf, usilb], writes=[hidb[fc]])
                        P.bank_free(bank)
                    for tt in range(4):
                        t = tc * 4 + tt
                        for dc in range(2):
                            bank = P.bank()
                            for fc in range(8):
                                P.op("pe", lambda h, fc=fc, tt=tt, dc=dc, slot=slot, bank=bank: h.matmul(
                                    out=bank.f32, lhsT=hid[fc][:, tt*128:(tt+1)*128], rhs=w3s[slot][fc][:, dc*512:(dc+1)*512],
                                    start=(fc == 0), stop=(fc == 7)),
                                    reads=[hidb[fc], w3sb[slot][fc]], writes=[bank.buf] if fc == 0 else (), wpart=[bank.buf] if fc > 0 else ())
                            P.op("dve", lambda h, t=t, dc=dc, e=e, bank=bank: h.scalar_tensor_tensor(
                                out=acc[t][:, dc*512:(dc+1)*512], in0=bank.f32, scalar=comb[:, t*8+e:t*8+e+1],
                                in1=acc[t][:, dc*512:(dc+1)*512], op0=ALU.mult, op1=ALU.add),
                                reads=[bank.buf, combb[t]], wpart=[accb[t]])
                            P.bank_free(bank)
        for t in range(nt):
            for dc in range(2):
                sl = (t * 2 + dc) % 4
                P.op("dve", lambda h, t=t, dc=dc, sl=sl: h.tensor_copy(out=ostage[sl][:, :], in_=acc[t][:, dc*512:(dc+1)*512]),
                     reads=[accb[t]], writes=[ostb[sl]])
                P.dma("act", lambda h, t=t, dc=dc, sl=sl, hoff=hoff: h.dma_start(
                    out=OUT[hoff+t*128:hoff+(t+1)*128, dc*512:(dc+1)*512], in_=ostage[sl][:, :]),
                    reads=[ostb[sl]], stream=f"{name}o")
    P.barrier()
    P.release(m0)


def emit_l3f_attn(P, C, io, nseq=2, T=2048, name="l3f", eps=1e-6):
    m0 = P.mark()
    D, H, hd = 1024, 16, 64
    scale = 0.125
    nkT = T // 128
    nq4 = T // 512
    vw = H * (hd + 1)
    ident, identb = C.ident_sb(P)
    XN32 = io["XN32"]

    cmA = np.zeros((128, 4*512), np.float32)
    for dq in range(4):
        for qq in range(128):
            lim = dq*128 + qq
            cmA[qq, dq*512+lim+1:(dq+1)*512] = -30000.0
    cA = C.add("c_causalA", cmA, F32)
    cmB = np.zeros((128, 4*512), np.float32)
    for d in range(4):
        for kk in range(128):
            cmB[kk, d*512:d*512 + min(d*128 + kk, 512)] = -30000.0
    cB = C.add("c_causalB", cmB, F32)
    cmtA = P.salloc(f"{name}_cmA", [128, 4*512], F32); cmAb = Buf("cmA")
    P.dma("sp", lambda h: h.dma_start(out=cmtA[:, :], in_=cA), writes=[cmAb], stream="consts")
    cmtB = P.salloc(f"{name}_cmB", [128, 4*512], F32); cmBb = Buf("cmB")
    P.dma("sp", lambda h: h.dma_start(out=cmtB[:, :], in_=cB), writes=[cmBb], stream="consts")
    onesbf = P.salloc(f"{name}_1bf", [128, 128], BF16); onesbfb = Buf("onesbf")
    P.op("dve", lambda h: h.memset(onesbf[0:1, :], 1.0), writes=[onesbfb])

    wsb = [P.salloc(f"{name}_w{kc}", [128, D], BF16) for kc in range(8)]
    wbufs = [Buf(f"{name}_w{kc}") for kc in range(8)]
    anw = P.salloc(f"{name}_anw", [128, D], BF16); anwb = Buf("anw")
    P.dma("pool", lambda h: h.dma_start(out=anw[:, :], in_=io["anw_rep"]), writes=[anwb], stream=f"{name}anw")
    mnw = P.salloc(f"{name}_mnw", [128, D], F32); mnwb = Buf("mnw")
    P.dma("sp", lambda h: h.dma_start(out=mnw[:, :], in_=io["mnw_rep"]), writes=[mnwb], stream="consts")

    tnfm = [P.salloc(f"{name}_tn{kc}", [128, T], BF16) for kc in range(8)]
    tnfb = [Buf(f"{name}_tn{kc}") for kc in range(8)]
    qfm = [P.salloc(f"{name}_qf{kc}", [128, T], BF16) for kc in range(8)]
    qfb = [Buf(f"{name}_qf{kc}") for kc in range(8)]
    kfm = [P.salloc(f"{name}_kf{kc}", [128, T], BF16) for kc in range(8)]
    kfb = [Buf(f"{name}_kf{kc}") for kc in range(8)]
    vau = [P.salloc(f"{name}_va{t}", [128, vw], BF16) for t in range(nkT)]
    vab = [Buf(f"{name}_va{t}") for t in range(nkT)]
    x32 = P.salloc(f"{name}_x32", [128, D], F32); x32b = Buf("x32")
    sq32 = P.salloc(f"{name}_sq32", [128, D], F32); sq32b = Buf("sq32")
    stt = P.salloc(f"{name}_stt", [128, 8], F32); sttb = Buf("stt")
    tnb = P.salloc(f"{name}_tnb", [128, D], BF16); tnbb = Buf("tnb")

    q32 = P.salloc(f"{name}_q32", [128, D], F32); q32b = Buf("q32")
    tA = x32; tAb = x32b
    tB = sq32; tBb = sq32b
    rotb = P.salloc(f"{name}_rot", [128, D], BF16); rotbb = Buf("rot")

    cost = P.salloc(f"{name}_cos", [128, D], BF16); costb = Buf("cos")
    sint = P.salloc(f"{name}_sin", [128, D], BF16); sintb = Buf("sin")
    pt = [P.salloc(f"{name}_pt{i}", [128, 512], BF16) for i in range(2)]
    ptb = [Buf(f"{name}_pt{i}") for i in range(2)]
    mrow = P.salloc(f"{name}_mrow", [128, T], BF16); mrowb = Buf("mrow")
    negm = P.salloc(f"{name}_negm", [128, 2], BF16); negmb = Buf("negm")
    h32 = P.salloc(f"{name}_h32", [128, D], F32); h32b = Buf("h32")
    tn2 = P.salloc(f"{name}_tn232", [128, D], F32); tn2b = Buf("tn232")
    tn2bf = rotb; tn2bfb = rotbb
    tn2lo = tnb; tn2lob = tnbb
    stg0 = P.salloc(f"{name}_stg", [128, D], BF16)
    stg = [stg0, stg0]
    stgb0 = Buf(f"{name}_stg")
    stgb = [stgb0, stgb0]

    def loadw(Wt):
        for kc in range(8):
            P.dma("pool", lambda h, kc=kc, Wt=Wt: h.dma_start(out=wsb[kc][:, :], in_=Wt[kc*128:(kc+1)*128, :]),
                  writes=[wbufs[kc]], stream=f"{name}w")

    for s in range(nseq):
        toff = s * T
        for i in range(nkT):
            P.dma("sp", lambda h, i=i, toff=toff: h.dma_start(out=x32[:, :], in_=XN32[toff+i*128:toff+(i+1)*128, :]),
                  writes=[x32b], stream=f"{name}x")
            P.op("act", lambda h: h.activation(out=sq32[:, :], in_=x32[:, :], func=AF.Square), reads=[x32b], writes=[sq32b])
            P.op("dve", lambda h: h.reduce_sum(out=stt[:, 0:1], in_=sq32[:, :], axis=AX.X), reads=[sq32b], writes=[sttb])
            P.op("dve", lambda h: h.tensor_scalar(out=stt[:, 1:2], in0=stt[:, 0:1], scalar1=1.0/D, scalar2=eps, op0=ALU.mult, op1=ALU.add),
                 reads=[sttb], wpart=[sttb])
            P.op("act", lambda h: h.activation(out=stt[:, 2:3], in_=stt[:, 1:2], func=AF.Sqrt), reads=[sttb], wpart=[sttb])
            P.op("dve", lambda h: h.reciprocal(out=stt[:, 3:4], in_=stt[:, 2:3]), reads=[sttb], wpart=[sttb])
            P.op("dve", lambda h: h.tensor_scalar(out=sq32[:, :], in0=x32[:, :], scalar1=stt[:, 3:4], scalar2=None, op0=ALU.mult),
                 reads=[x32b, sttb], writes=[sq32b])
            P.op("dve", lambda h: h.tensor_mul(out=tnb[:, :], in0=sq32[:, :], in1=anw[:, :]), reads=[sq32b, anwb], writes=[tnbb])
            bank = P.bank()
            for c in range(8):
                P.op("pe", lambda h, c=c, bank=bank: h.transpose(out=bank.bf16[:, c*128:(c+1)*128], in_=tnb[:, c*128:(c+1)*128], identity=ident[:, :]),
                     reads=[tnbb, identb], writes=[bank.buf] if c == 0 else (), wpart=[bank.buf] if c else ())
            for c in range(8):
                P.op("dve", lambda h, c=c, i=i, bank=bank: h.tensor_copy(out=tnfm[c][:, i*128:(i+1)*128], in_=bank.bf16[:, c*128:(c+1)*128]),
                     reads=[bank.buf], wpart=[tnfb[c]])
            P.bank_free(bank)
        for which, Wt, dstfm, dstfb in ((0, io["WqT"], qfm, qfb), (1, io["WkT"], kfm, kfb)):
            loadw(Wt)
            for i in range(nkT):
                P.dma("sp", lambda h, i=i: h.dma_start(out=cost[:, :], in_=io["CosT"][i*128:(i+1)*128, :]),
                      writes=[costb], stream=f"{name}cs")
                P.dma("sp", lambda h, i=i: h.dma_start(out=sint[:, :], in_=io["SinT"][i*128:(i+1)*128, :]),
                      writes=[sintb], stream=f"{name}cs")
                for dc in range(2):
                    bank = P.bank()
                    for kc in range(8):
                        P.op("pe", lambda h, kc=kc, i=i, dc=dc, bank=bank: h.matmul(
                            out=bank.f32, lhsT=tnfm[kc][:, i*128:(i+1)*128], rhs=wsb[kc][:, dc*512:(dc+1)*512],
                            start=(kc == 0), stop=(kc == 7)),
                            reads=[wbufs[kc], tnfb[kc]], writes=[bank.buf] if kc == 0 else (), wpart=[bank.buf] if kc > 0 else ())
                    P.op("dve", lambda h, dc=dc, bank=bank: h.tensor_copy(out=q32[:, dc*512:(dc+1)*512], in_=bank.f32),
                         reads=[bank.buf], writes=[q32b] if dc == 0 else (), wpart=[q32b] if dc else ())
                    P.bank_free(bank)
                def v3(t, half):
                    return t[:, :].rearrange("p (h q) -> p h q", q=hd)[:, :, half*32:(half+1)*32]
                P.op("dve", lambda h: h.tensor_mul(out=v3(tA, 0), in0=v3(q32, 0), in1=v3(cost, 0)), reads=[q32b, costb], writes=[tAb])
                P.op("dve", lambda h: h.tensor_mul(out=v3(tB, 0), in0=v3(q32, 1), in1=v3(sint, 0)), reads=[q32b, sintb], writes=[tBb])
                P.op("dve", lambda h: h.tensor_sub(out=v3(rotb, 0), in0=v3(tA, 0), in1=v3(tB, 0)), reads=[tAb, tBb], writes=[rotbb])
                P.op("dve", lambda h: h.tensor_mul(out=v3(tA, 1), in0=v3(q32, 1), in1=v3(cost, 1)), reads=[q32b, costb], wpart=[tAb])
                P.op("dve", lambda h: h.tensor_mul(out=v3(tB, 1), in0=v3(q32, 0), in1=v3(sint, 1)), reads=[q32b, sintb], wpart=[tBb])
                P.op("dve", lambda h: h.tensor_add(out=v3(rotb, 1), in0=v3(tA, 1), in1=v3(tB, 1)), reads=[tAb, tBb], wpart=[rotbb])
                bank = P.bank()
                for c in range(8):
                    P.op("pe", lambda h, c=c, bank=bank: h.transpose(out=bank.bf16[:, c*128:(c+1)*128], in_=rotb[:, c*128:(c+1)*128], identity=ident[:, :]),
                         reads=[rotbb, identb], writes=[bank.buf] if c == 0 else (), wpart=[bank.buf] if c else ())
                for c in range(8):
                    P.op("dve", lambda h, c=c, i=i, bank=bank, dstfm=dstfm, dstfb=dstfb: h.tensor_copy(
                        out=dstfm[c][:, i*128:(i+1)*128], in_=bank.bf16[:, c*128:(c+1)*128]),
                        reads=[bank.buf], wpart=[dstfb[c]])
                P.bank_free(bank)
        loadw(io["WvT"])
        for t in range(nkT):
            P.op("dve", lambda h, t=t: h.memset(vau[t][:, :], 1.0), writes=[vab[t]])
            for dc in range(2):
                bank = P.bank()
                for kc in range(8):
                    P.op("pe", lambda h, kc=kc, t=t, dc=dc, bank=bank: h.matmul(
                        out=bank.f32, lhsT=tnfm[kc][:, t*128:(t+1)*128], rhs=wsb[kc][:, dc*512:(dc+1)*512],
                        start=(kc == 0), stop=(kc == 7)),
                        reads=[wbufs[kc], tnfb[kc]], writes=[bank.buf] if kc == 0 else (), wpart=[bank.buf] if kc > 0 else ())
                def vevac(h, t=t, dc=dc, bank=bank):
                    dstv = vau[t][:, :].rearrange("p (h e) -> p h e", e=hd+1)
                    src3 = bank.t[0:128, 0:512].rearrange("p (h e) -> p h e", e=hd)
                    return h.tensor_copy(out=dstv[:, (dc*512)//hd:((dc+1)*512)//hd, 0:hd], in_=src3)
                P.op("dve", vevac, reads=[bank.buf], wpart=[vab[t]])
                P.bank_free(bank)
        if T >= 1024:
            per = T // 1024
            ctx = [tnfm[t // per][:, (t % per)*1024:(t % per)*1024+1024] for t in range(nkT)]
            ctxb = [tnfb[t // per] for t in range(nkT)]
        else:
            if s == 0:
                ctx_t = [P.salloc(f"{name}_cx{t}", [128, 1024], BF16) for t in range(nkT)]
                ctx_bufs = [Buf(f"{name}_cx{t}") for t in range(nkT)]
                P._l3f_ctx = (ctx_t, ctx_bufs)
            ctx, ctxb = P._l3f_ctx
        for h in range(H):
            kc8, rb = (h * hd) // 128, (h * hd) % 128
            for qt in range(nkT):
                nkb = qt // 4 + 1
                P.op("dve", lambda h_: h_.memset(stt[:, 4:5], -30000.0), wpart=[sttb])
                for kb in range(nkb):
                    bank = P.bank()
                    P.op("pe", lambda h_, kc8=kc8, rb=rb, qt=qt, kb=kb, bank=bank: h_.matmul(
                        out=bank.f32, lhsT=qfm[kc8][rb:rb+hd, qt*128:(qt+1)*128], rhs=kfm[kc8][rb:rb+hd, kb*512:(kb+1)*512],
                        start=True, stop=True),
                        reads=[qfb[kc8], kfb[kc8]], writes=[bank.buf])
                    if kb == nkb - 1:
                        dq = qt - kb * 4
                        P.op("dve", lambda h_, dq=dq, bank=bank: h_.tensor_add(out=bank.f32, in0=bank.f32, in1=cmtA[:, dq*512:(dq+1)*512]),
                             reads=[cmAb], wpart=[bank.buf])
                    P.op("dve", lambda h_, bank=bank: h_.reduce_max(out=stt[:, 5:6], in_=bank.f32, axis=AX.X),
                         reads=[bank.buf], wpart=[sttb])
                    P.op("dve", lambda h_: h_.tensor_max(out=stt[:, 4:5], in0=stt[:, 4:5], in1=stt[:, 5:6]), wpart=[sttb])
                    P.bank_free(bank)
                P.op("dve", lambda h_: h_.tensor_scalar(out=negm[:, 0:1], in0=stt[:, 4:5], scalar1=-1.0, scalar2=None, op0=ALU.mult),
                     reads=[sttb], writes=[negmb])
                bank = P.bank()
                P.op("pe", lambda h_, bank=bank: h_.transpose(out=bank.bf16[0:1, 0:128], in_=negm[:, 0:1], identity=ident[:, :]),
                     reads=[negmb, identb], writes=[bank.buf])
                P.op("dve", lambda h_, qt=qt, bank=bank: h_.tensor_copy(out=mrow[0:1, qt*128:(qt+1)*128], in_=bank.bf16[0:1, 0:128]),
                     reads=[bank.buf], writes=[mrowb] if qt == 0 else (), wpart=[mrowb] if qt else ())
                P.bank_free(bank)
            for q4 in range(nq4):
                nk0 = (q4 + 1) * 4
                pvb = [P.bank() for _ in range(4)]
                sbk = [P.bank(), P.bank()]
                def emit_S3(k0, kc8=kc8, rb=rb, q4=q4):
                    sb = sbk[k0 % 2]
                    sl = k0 % 2
                    P.op("pe", lambda h_, kc8=kc8, rb=rb, k0=k0, q4=q4, sb=sb: h_.matmul(
                        out=sb.f32, lhsT=kfm[kc8][rb:rb+hd, k0*128:(k0+1)*128], rhs=qfm[kc8][rb:rb+hd, q4*512:(q4+1)*512],
                        start=True, stop=False),
                        reads=[kfb[kc8], qfb[kc8]], writes=[sb.buf])
                    P.op("pe", lambda h_, k0=k0, q4=q4, sb=sb: h_.matmul(
                        out=sb.f32, lhsT=onesbf[0:1, 0:128], rhs=mrow[0:1, q4*512:(q4+1)*512],
                        start=False, stop=True),
                        reads=[onesbfb, mrowb], wpart=[sb.buf])
                    if k0 >= q4 * 4:
                        dl = k0 - q4 * 4
                        P.op("dve", lambda h_, dl=dl, sb=sb: h_.tensor_add(out=sb.f32, in0=sb.f32, in1=cmtB[:, dl*512:(dl+1)*512]),
                             reads=[cmBb], wpart=[sb.buf])
                    P.op("act", lambda h_, sb=sb, sl=sl: h_.activation(out=pt[sl][:, :], in_=sb.f32, func=AF.Exp, scale=scale),
                         reads=[sb.buf], writes=[ptb[sl]])
                emit_S3(0)
                for k0 in range(nk0):
                    if k0 + 1 < nk0:
                        emit_S3(k0 + 1)
                    sl = k0 % 2
                    first = (k0 == 0)
                    last = (k0 == nk0 - 1)
                    for qq in range(4):
                        bqq = pvb[qq]
                        P.op("pe", lambda h_, sl=sl, qq=qq, k0=k0, hh=h, first=first, last=last, b=bqq: h_.matmul(
                            out=b.t[0:128, 0:hd+1], lhsT=pt[sl][:, qq*128:(qq+1)*128], rhs=vau[k0][:, hh*(hd+1):(hh+1)*(hd+1)],
                            start=first, stop=last),
                            reads=[ptb[sl], vab[k0]], writes=[bqq.buf] if first else (), wpart=[bqq.buf] if not first else ())
                for qq in range(4):
                    t = q4 * 4 + qq
                    b = pvb[qq]
                    P.op("dve", lambda h_, b=b: h_.tensor_copy(out=stt[:, 6:7], in_=b.t[0:128, hd:hd+1]),
                         reads=[b.buf], wpart=[sttb])
                    P.op("dve", lambda h_: h_.reciprocal(out=stt[:, 7:8], in_=stt[:, 6:7]), reads=[sttb], wpart=[sttb])
                    P.op("dve", lambda h_, b=b, t=t, hh=h: h_.tensor_scalar(
                        out=ctx[t][:, hh*hd:(hh+1)*hd], in0=b.t[0:128, 0:hd], scalar1=stt[:, 7:8], scalar2=None, op0=ALU.mult),
                        reads=[b.buf, sttb], wpart=[ctxb[t]])
                P.bank_free(*pvb, *sbk)
        cfm = qfm
        cfb = qfb
        for t in range(nkT):
            bank = P.bank()
            for c in range(8):
                P.op("pe", lambda h, c=c, t=t, bank=bank: h.transpose(
                    out=bank.bf16[:, c*128:(c+1)*128], in_=ctx[t][:, c*128:(c+1)*128], identity=ident[:, :]),
                    reads=[ctxb[t], identb], writes=[bank.buf] if c == 0 else (), wpart=[bank.buf] if c else ())
            for c in range(8):
                P.op("dve", lambda h, c=c, t=t, bank=bank: h.tensor_copy(out=cfm[c][:, t*128:(t+1)*128], in_=bank.bf16[:, c*128:(c+1)*128]),
                     reads=[bank.buf], wpart=[cfb[c]])
            P.bank_free(bank)
        loadw(io["WoT"])
        for t in range(nkT):
            banks = [P.bank(), P.bank()]
            for dc in range(2):
                bank = banks[dc]
                for kc in range(8):
                    P.op("pe", lambda h, kc=kc, t=t, dc=dc, bank=bank: h.matmul(
                        out=bank.f32, lhsT=cfm[kc][:, t*128:(t+1)*128], rhs=wsb[kc][:, dc*512:(dc+1)*512],
                        start=(kc == 0), stop=(kc == 7)),
                        reads=[wbufs[kc], cfb[kc]], writes=[bank.buf] if kc == 0 else (), wpart=[bank.buf] if kc > 0 else ())
            P.dma("sp", lambda h, t=t, toff=toff: h.dma_start(out=x32[:, :], in_=XN32[toff+t*128:toff+(t+1)*128, :]),
                  writes=[x32b], stream=f"{name}x")
            for dc in range(2):
                bank = banks[dc]
                P.op("dve", lambda h, dc=dc, bank=bank: h.tensor_add(out=h32[:, dc*512:(dc+1)*512], in0=bank.f32, in1=x32[:, dc*512:(dc+1)*512]),
                     reads=[bank.buf, x32b], writes=[h32b] if dc == 0 else (), wpart=[h32b] if dc else ())
                P.bank_free(bank)
            P.dma("act", lambda h, t=t, toff=toff: h.dma_start(out=io["H32"][toff+t*128:toff+(t+1)*128, :], in_=h32[:, :]),
                  reads=[h32b], stream=f"{name}h")
            P.op("act", lambda h: h.activation(out=sq32[:, :], in_=h32[:, :], func=AF.Square), reads=[h32b], writes=[sq32b])
            P.op("dve", lambda h: h.reduce_sum(out=stt[:, 0:1], in_=sq32[:, :], axis=AX.X), reads=[sq32b], wpart=[sttb])
            P.op("dve", lambda h: h.tensor_scalar(out=stt[:, 1:2], in0=stt[:, 0:1], scalar1=1.0/D, scalar2=eps, op0=ALU.mult, op1=ALU.add),
                 reads=[sttb], wpart=[sttb])
            P.op("act", lambda h: h.activation(out=stt[:, 2:3], in_=stt[:, 1:2], func=AF.Sqrt), reads=[sttb], wpart=[sttb])
            P.op("dve", lambda h: h.reciprocal(out=stt[:, 3:4], in_=stt[:, 2:3]), reads=[sttb], wpart=[sttb])
            P.op("dve", lambda h: h.tensor_scalar(out=sq32[:, :], in0=h32[:, :], scalar1=stt[:, 3:4], scalar2=None, op0=ALU.mult),
                 reads=[h32b, sttb], writes=[sq32b])
            P.op("dve", lambda h: h.tensor_mul(out=tn2[:, :], in0=sq32[:, :], in1=mnw[:, :]), reads=[sq32b, mnwb], writes=[tn2b])
            P.op("dve", lambda h: h.tensor_copy(out=tn2bf[:, :], in_=tn2[:, :]), reads=[tn2b], writes=[tn2bfb])
            P.op("dve", lambda h: h.tensor_sub(out=tn2lo[:, :], in0=tn2[:, :], in1=tn2bf[:, :]), reads=[tn2b, tn2bfb], writes=[tn2lob])
            for which2, srct, srctb, dram in ((0, tn2bf, tn2bfb, io["TN2_fm"]), (1, tn2lo, tn2lob, io["TN2_lo_fm"])):
                bank = P.bank()
                for c in range(8):
                    P.op("pe", lambda h, c=c, bank=bank, srct=srct, srctb=srctb: h.transpose(out=bank.bf16[:, c*128:(c+1)*128], in_=srct[:, c*128:(c+1)*128], identity=ident[:, :]),
                         reads=[srctb, identb], writes=[bank.buf] if c == 0 else (), wpart=[bank.buf] if c else ())
                P.op("dve", lambda h, bank=bank, which2=which2: h.tensor_copy(out=stg[which2][:, :], in_=bank.bf16[:, :]),
                     reads=[bank.buf], writes=[stgb[which2]])
                P.bank_free(bank)
                def fmdma(h, t=t, toff=toff, which2=which2, dram=dram):
                    dst = dram.rearrange("(c p) n -> p c n", p=128)[:, :, toff+t*128:toff+(t+1)*128]
                    srcv = stg[which2][:, :].rearrange("p (c j) -> p c j", c=8)
                    return h.dma_start(out=dst, in_=srcv)
                P.dma("act", fmdma, reads=[stgb[which2]], stream=f"{name}fm")
    P.barrier()
    P.release(m0)


def emit_add(P, C, OUT, srcs, ntok, name="add", barrier=True, release=True):
    m0 = P.mark()
    nt = ntok // 128
    a = [P.salloc(f"{name}_a{i}", [128, 1024], F32) for i in range(2)]
    ab = [Buf(f"{name}_a{i}") for i in range(2)]
    b = [P.salloc(f"{name}_b{i}", [128, 1024], F32) for i in range(2)]
    bb = [Buf(f"{name}_b{i}") for i in range(2)]
    for t in range(nt):
        sl = t % 2
        P.dma("sp", lambda h, t=t, sl=sl: h.dma_start(out=a[sl][:, :], in_=srcs[0](t)), writes=[ab[sl]], stream=f"{name}s0")
        for j in range(1, len(srcs)):
            P.dma("sp", lambda h, t=t, sl=sl, j=j: h.dma_start(out=b[sl][:, :], in_=srcs[j](t)), writes=[bb[sl]], stream=f"{name}s{j}")
            P.op("dve", lambda h, sl=sl: h.tensor_add(out=a[sl][:, :], in0=a[sl][:, :], in1=b[sl][:, :]),
                 reads=[bb[sl]], wpart=[ab[sl]])
        P.dma("act", lambda h, t=t, sl=sl: h.dma_start(out=OUT[t*128:(t+1)*128, :], in_=a[sl][:, :]),
              reads=[ab[sl]], stream=f"{name}o")
    if barrier:
        P.barrier()
    if release:
        P.release(m0)
    return m0



def emit_l0f_mid(P, C, io, ntok=2048, name="l0f", eps=1e-5):
    m0 = P.mark()
    D, H, hd = 1024, 16, 64
    nt = ntok // 128
    vw = H * (hd + 1)
    ident, identb = C.ident_sb(P)
    scale = 0.125

    mq = P.salloc(f"{name}_mq", [128, 128], BF16); mqb = Buf("mq")
    mk_ = P.salloc(f"{name}_mk", [128, 128], BF16); mkb = Buf("mk")
    P.dma("pool", lambda h: h.dma_start(out=mq[0:16, :], in_=io["MaskQ"]), writes=[mqb], stream=f"{name}msk")
    P.dma("pool", lambda h: h.dma_start(out=mk_[0:16, :], in_=io["MaskK"]), writes=[mkb], stream=f"{name}msk")
    onesbf = P.salloc(f"{name}_1bf", [128, 128], BF16); onesbfb = Buf("onesbf0")
    P.op("dve", lambda h: h.memset(onesbf[0:1, :], 1.0), writes=[onesbfb])

    wsb = [P.salloc(f"{name}_w{kc}", [128, D], BF16) for kc in range(8)]
    wbufs = [Buf(f"{name}_w{kc}") for kc in range(8)]
    xfm = [P.salloc(f"{name}_xf{kc}", [128, ntok], BF16) for kc in range(8)]
    xfb = [Buf(f"{name}_xf{kc}") for kc in range(8)]
    qfm = [P.salloc(f"{name}_qf{kc}", [128, ntok], BF16) for kc in range(8)]
    qfb = [Buf(f"{name}_qf{kc}") for kc in range(8)]
    kfm = [P.salloc(f"{name}_kf{kc}", [128, ntok], BF16) for kc in range(8)]
    kfb = [Buf(f"{name}_kf{kc}") for kc in range(8)]
    vau = [P.salloc(f"{name}_va{t}", [128, vw], BF16) for t in range(nt)]
    vab = [Buf(f"{name}_va{t}") for t in range(nt)]
    per = ntok // 1024
    ctx = [xfm[t // per][:, (t % per)*1024:(t % per)*1024+1024] for t in range(nt)]
    ctxb = [xfb[t // per] for t in range(nt)]
    pt2 = P.salloc(f"{name}_pt", [128, 128], BF16); pt2b = Buf("pt")
    mrow = P.salloc(f"{name}_mrow", [128, 128], BF16); mrowb = Buf("mrow")
    negm = P.salloc(f"{name}_negm", [128, 1], BF16); negmb = Buf("negm")
    stt = P.salloc(f"{name}_stt", [128, 8], F32); sttb = Buf("stt")
    bpart = {}
    for nm in ("bq", "bk"):
        bt = P.salloc(f"{name}_{nm}", [128, 8], F32); bb2 = Buf(nm)
        P.dma("sp", lambda h, bt=bt, nm=nm: h.dma_start(out=bt[:, :], in_=io[nm]), writes=[bb2], stream="consts")
        bpart[nm] = (bt, bb2)
    bvrep = P.salloc(f"{name}_bv", [128, D], F32); bvrepb = Buf("bv")
    P.dma("sp", lambda h: h.dma_start(out=bvrep[:, :], in_=io["bv_rep"]), writes=[bvrepb], stream="consts")
    borep = P.salloc(f"{name}_bo", [128, D], F32); borepb = Buf("bo")
    P.dma("sp", lambda h: h.dma_start(out=borep[:, :], in_=io["bo_rep"]), writes=[borepb], stream="consts")
    g1 = P.salloc(f"{name}_g1", [128, D], F32); g1b = Buf("g1")
    P.dma("sp", lambda h: h.dma_start(out=g1[:, :], in_=io["g1_rep"]), writes=[g1b], stream="consts")
    b1r = P.salloc(f"{name}_b1r", [128, D], F32); b1rb = Buf("b1r")
    P.dma("sp", lambda h: h.dma_start(out=b1r[:, :], in_=io["b1_rep"]), writes=[b1rb], stream="consts")
    x32 = P.salloc(f"{name}_x32", [128, D], F32); x32b = Buf("x320")
    z32 = P.salloc(f"{name}_z32", [128, D], F32); z32b = Buf("z320")
    sq32 = P.salloc(f"{name}_sq32", [128, D], F32); sq32b = Buf("sq320")
    xbf = P.salloc(f"{name}_xbf", [128, D], BF16); xbfb = Buf("xbf0")
    xlo = P.salloc(f"{name}_xlo", [128, D], BF16); xlob = Buf("xlo0")
    stg0 = P.salloc(f"{name}_stg", [128, D], BF16)
    stg = [stg0, stg0]
    stgb0 = Buf(f"{name}_stg")
    stgb = [stgb0, stgb0]
    ost = [P.salloc(f"{name}_ost{i}", [128, D], F32) for i in range(2)]
    ostb = [Buf(f"{name}_ost{i}") for i in range(2)]

    def loadw(Wt):
        for kc in range(8):
            P.dma("pool", lambda h, kc=kc, Wt=Wt: h.dma_start(out=wsb[kc][:, :], in_=Wt[kc*128:(kc+1)*128, :]),
                  writes=[wbufs[kc]], stream=f"{name}w")

    for kc in range(8):
        P.dma("sp", lambda h, kc=kc: h.dma_start(out=xfm[kc][:, :], in_=io["XNp_fm"][kc*128:(kc+1)*128, :]),
              writes=[xfb[kc]], stream=f"{name}xf")

    def proj_fm(bias, dst, dstb):
        for oc in range(8):
            for t4 in range(ntok // 512):
                bank = P.bank()
                for kc in range(8):
                    P.op("pe", lambda h, kc=kc, oc=oc, t4=t4, bank=bank: h.matmul(
                        out=bank.f32, lhsT=wsb[kc][:, oc*128:(oc+1)*128], rhs=xfm[kc][:, t4*512:(t4+1)*512],
                        start=(kc == 0), stop=(kc == 7)),
                        reads=[wbufs[kc], xfb[kc]], writes=[bank.buf] if kc == 0 else (), wpart=[bank.buf] if kc > 0 else ())
                bt, bb2 = bpart[bias]
                P.op("act", lambda h, oc=oc, t4=t4, bank=bank, bt=bt: h.activation(
                    out=dst[oc][:, t4*512:(t4+1)*512], in_=bank.f32, func=AF.Identity, bias=bt[:, oc:oc+1]),
                    reads=[bank.buf, bb2], wpart=[dstb[oc]])
                P.bank_free(bank)

    loadw(io["WqT"]); proj_fm("bq", qfm, qfb)
    loadw(io["WkT"]); proj_fm("bk", kfm, kfb)
    loadw(io["WvT"])
    for t in range(nt):
        P.op("dve", lambda h, t=t: h.memset(vau[t][:, :], 1.0), writes=[vab[t]])
        for dc in range(2):
            bank = P.bank()
            for kc in range(8):
                P.op("pe", lambda h, kc=kc, t=t, dc=dc, bank=bank: h.matmul(
                    out=bank.f32, lhsT=xfm[kc][:, t*128:(t+1)*128], rhs=wsb[kc][:, dc*512:(dc+1)*512],
                    start=(kc == 0), stop=(kc == 7)),
                    reads=[wbufs[kc], xfb[kc]], writes=[bank.buf] if kc == 0 else (), wpart=[bank.buf] if kc > 0 else ())
            def vevac(h, t=t, dc=dc, bank=bank):
                dstv = vau[t][:, :].rearrange("p (h e) -> p h e", e=hd+1)
                src3 = bank.t[0:128, 0:512].rearrange("p (h e) -> p h e", e=hd)
                b3 = bvrep[:, dc*512:(dc+1)*512].rearrange("p (h e) -> p h e", e=hd)
                return h.tensor_add(out=dstv[:, (dc*512)//hd:((dc+1)*512)//hd, 0:hd], in0=src3, in1=b3)
            P.op("dve", vevac, reads=[bank.buf, bvrepb], wpart=[vab[t]])
            P.bank_free(bank)
    iters = [(blk, h) for blk in range(nt) for h in range(H)]
    pt2x = [P.salloc(f"{name}_pt2{i}", [128, 128], BF16) for i in range(2)]
    pt2xb = [Buf(f"{name}_pt2{i}") for i in range(2)]
    mrowx = [P.salloc(f"{name}_mr{i}", [128, 128], BF16) for i in range(2)]
    mrowxb = [Buf(f"{name}_mr{i}") for i in range(2)]
    negmx = [P.salloc(f"{name}_ng{i}", [128, 1], BF16) for i in range(2)]
    negmxb = [Buf(f"{name}_ng{i}") for i in range(2)]
    state = {}
    def stage1(i):
        blk, h = iters[i]
        par = i % 2
        kc8, rb = (h * hd) // 128, (h * hd) % 128
        bank = P.bank()
        P.op("pe", lambda h_, kc8=kc8, rb=rb, blk=blk, bank=bank: h_.matmul(
            out=bank.f32[0:128, 0:128], lhsT=qfm[kc8][rb:rb+hd, blk*128:(blk+1)*128], rhs=kfm[kc8][rb:rb+hd, blk*128:(blk+1)*128],
            start=True, stop=False),
            reads=[qfb[kc8], kfb[kc8]], writes=[bank.buf])
        P.op("pe", lambda h_, bank=bank: h_.matmul(
            out=bank.f32[0:128, 0:128], lhsT=mq[0:9, :], rhs=mk_[0:9, :], start=False, stop=True),
            reads=[mqb, mkb], wpart=[bank.buf])
        P.op("dve", lambda h_, bank=bank, par=par: h_.reduce_max(out=stt[:, par:par+1], in_=bank.f32[0:128, 0:128], axis=AX.X),
             reads=[bank.buf], wpart=[sttb])
        P.bank_free(bank)
        P.op("dve", lambda h_, par=par: h_.tensor_scalar(out=negmx[par][:, 0:1], in0=stt[:, par:par+1], scalar1=-1.0, scalar2=None, op0=ALU.mult),
             reads=[sttb], writes=[negmxb[par]])
    def stage2(i):
        blk, h = iters[i]
        par = i % 2
        bank = P.bank()
        P.op("pe", lambda h_, bank=bank, par=par: h_.transpose(out=bank.bf16[0:1, 0:128], in_=negmx[par][:, 0:1], identity=ident[:, :]),
             reads=[negmxb[par], identb], writes=[bank.buf])
        P.op("dve", lambda h_, bank=bank, par=par: h_.tensor_copy(out=mrowx[par][0:1, :], in_=bank.bf16[0:1, 0:128]),
             reads=[bank.buf], writes=[mrowxb[par]])
        P.bank_free(bank)
    def stage3(i):
        blk, h = iters[i]
        par = i % 2
        kc8, rb = (h * hd) // 128, (h * hd) % 128
        bank = P.bank()
        P.op("pe", lambda h_, kc8=kc8, rb=rb, blk=blk, bank=bank: h_.matmul(
            out=bank.f32[0:128, 0:128], lhsT=kfm[kc8][rb:rb+hd, blk*128:(blk+1)*128], rhs=qfm[kc8][rb:rb+hd, blk*128:(blk+1)*128],
            start=True, stop=False),
            reads=[qfb[kc8], kfb[kc8]], writes=[bank.buf])
        P.op("pe", lambda h_, bank=bank: h_.matmul(
            out=bank.f32[0:128, 0:128], lhsT=mq[0:9, :], rhs=mk_[0:9, :], start=False, stop=False),
            reads=[mqb, mkb], wpart=[bank.buf])
        P.op("pe", lambda h_, bank=bank, par=par: h_.matmul(
            out=bank.f32[0:128, 0:128], lhsT=onesbf[0:1, :], rhs=mrowx[par][0:1, :], start=False, stop=True),
            reads=[onesbfb, mrowxb[par]], wpart=[bank.buf])
        P.op("act", lambda h_, bank=bank, par=par: h_.activation(out=pt2x[par][:, :], in_=bank.f32[0:128, 0:128], func=AF.Exp, scale=scale),
             reads=[bank.buf], writes=[pt2xb[par]])
        P.bank_free(bank)
    def stage4(i):
        blk, h = iters[i]
        par = i % 2
        bank = P.bank()
        P.op("pe", lambda h_, blk=blk, hh=h, bank=bank, par=par: h_.matmul(
            out=bank.t[0:128, 0:hd+1], lhsT=pt2x[par][:, :], rhs=vau[blk][:, hh*(hd+1):(hh+1)*(hd+1)],
            start=True, stop=True),
            reads=[pt2xb[par], vab[blk]], writes=[bank.buf])
        P.op("dve", lambda h_, bank=bank, par=par: h_.tensor_copy(out=stt[:, 2+par:3+par], in_=bank.t[0:128, hd:hd+1]),
             reads=[bank.buf], wpart=[sttb])
        P.op("dve", lambda h_, par=par: h_.reciprocal(out=stt[:, 4+par:5+par], in_=stt[:, 2+par:3+par]), reads=[sttb], wpart=[sttb])
        P.op("dve", lambda h_, blk=blk, hh=h, bank=bank, par=par: h_.tensor_scalar(
            out=ctx[blk][:, hh*hd:(hh+1)*hd], in0=bank.t[0:128, 0:hd], scalar1=stt[:, 4+par:5+par], scalar2=None, op0=ALU.mult),
            reads=[bank.buf, sttb], wpart=[ctxb[blk]])
        P.bank_free(bank)
    n_it = len(iters)
    for i in range(n_it + 3):
        if i < n_it: stage1(i)
        if 1 <= i < n_it + 1: stage2(i - 1)
        if 2 <= i < n_it + 2: stage3(i - 2)
        if 3 <= i < n_it + 3: stage4(i - 3)
    cfm = qfm
    cfb = qfb
    for t in range(nt):
        bank = P.bank()
        for c in range(8):
            P.op("pe", lambda h, c=c, t=t, bank=bank: h.transpose(
                out=bank.bf16[:, c*128:(c+1)*128], in_=ctx[t][:, c*128:(c+1)*128], identity=ident[:, :]),
                reads=[ctxb[t], identb], writes=[bank.buf] if c == 0 else (), wpart=[bank.buf] if c else ())
        for c in range(8):
            P.op("dve", lambda h, c=c, t=t, bank=bank: h.tensor_copy(out=cfm[c][:, t*128:(t+1)*128], in_=bank.bf16[:, c*128:(c+1)*128]),
                 reads=[bank.buf], wpart=[cfb[c]])
        P.bank_free(bank)
    loadw(io["WoT"])
    for t in range(nt):
        banks = [P.bank(), P.bank()]
        for dc in range(2):
            bank = banks[dc]
            for kc in range(8):
                P.op("pe", lambda h, kc=kc, t=t, dc=dc, bank=bank: h.matmul(
                    out=bank.f32, lhsT=cfm[kc][:, t*128:(t+1)*128], rhs=wsb[kc][:, dc*512:(dc+1)*512],
                    start=(kc == 0), stop=(kc == 7)),
                    reads=[wbufs[kc], cfb[kc]], writes=[bank.buf] if kc == 0 else (), wpart=[bank.buf] if kc > 0 else ())
        P.dma("sp", lambda h, t=t: h.dma_start(out=x32[:, :], in_=io["XNp_32"][t*128:(t+1)*128, :]),
              writes=[x32b], stream=f"{name}x")
        for dc in range(2):
            bank = banks[dc]
            P.op("dve", lambda h, dc=dc, bank=bank: h.tensor_add(out=z32[:, dc*512:(dc+1)*512], in0=bank.f32, in1=x32[:, dc*512:(dc+1)*512]),
                 reads=[bank.buf, x32b], writes=[z32b] if dc == 0 else (), wpart=[z32b] if dc else ())
            P.op("dve", lambda h, dc=dc, bank=bank: h.tensor_add(out=z32[:, dc*512:(dc+1)*512], in0=z32[:, dc*512:(dc+1)*512], in1=borep[:, dc*512:(dc+1)*512]),
                 reads=[borepb], wpart=[z32b])
            P.bank_free(bank)
        P.op("dve", lambda h: h.reduce_sum(out=stt[:, 0:1], in_=z32[:, :], axis=AX.X), reads=[z32b], writes=[sttb])
        P.op("dve", lambda h: h.tensor_scalar(out=stt[:, 1:2], in0=stt[:, 0:1], scalar1=1.0/D, scalar2=None, op0=ALU.mult),
             reads=[sttb], wpart=[sttb])
        P.op("dve", lambda h: h.tensor_scalar(out=sq32[:, :], in0=z32[:, :], scalar1=stt[:, 1:2], scalar2=None, op0=ALU.subtract),
             reads=[z32b, sttb], writes=[sq32b])
        P.op("act", lambda h: h.activation(out=z32[:, :], in_=sq32[:, :], func=AF.Square), reads=[sq32b], writes=[z32b])
        P.op("dve", lambda h: h.reduce_sum(out=stt[:, 2:3], in_=z32[:, :], axis=AX.X), reads=[z32b], wpart=[sttb])
        P.op("dve", lambda h: h.tensor_scalar(out=stt[:, 3:4], in0=stt[:, 2:3], scalar1=1.0/D, scalar2=eps, op0=ALU.mult, op1=ALU.add),
             reads=[sttb], wpart=[sttb])
        P.op("act", lambda h: h.activation(out=stt[:, 4:5], in_=stt[:, 3:4], func=AF.Sqrt), reads=[sttb], wpart=[sttb])
        P.op("dve", lambda h: h.reciprocal(out=stt[:, 5:6], in_=stt[:, 4:5]), reads=[sttb], wpart=[sttb])
        P.op("dve", lambda h: h.tensor_scalar(out=sq32[:, :], in0=sq32[:, :], scalar1=stt[:, 5:6], scalar2=None, op0=ALU.mult),
             reads=[sttb], wpart=[sq32b])
        P.op("dve", lambda h: h.tensor_mul(out=sq32[:, :], in0=sq32[:, :], in1=g1[:, :]), reads=[g1b], wpart=[sq32b])
        P.op("dve", lambda h: h.tensor_add(out=sq32[:, :], in0=sq32[:, :], in1=b1r[:, :]), reads=[b1rb], wpart=[sq32b])
        sl = t % 2
        P.op("dve", lambda h, sl=sl: h.tensor_copy(out=ost[sl][:, :], in_=sq32[:, :]), reads=[sq32b], writes=[ostb[sl]])
        P.dma("act", lambda h, t=t, sl=sl: h.dma_start(out=io["X1_32"][t*128:(t+1)*128, :], in_=ost[sl][:, :]),
              reads=[ostb[sl]], stream=f"{name}h")
        P.op("dve", lambda h: h.tensor_copy(out=xbf[:, :], in_=sq32[:, :]), reads=[sq32b], writes=[xbfb])
        P.op("dve", lambda h: h.tensor_sub(out=xlo[:, :], in0=sq32[:, :], in1=xbf[:, :]), reads=[sq32b, xbfb], writes=[xlob])
        for which2, srct, srctb, dram in ((0, xbf, xbfb, io["X1_fm"]), (1, xlo, xlob, io["X1_lo_fm"])):
            bank = P.bank()
            for c in range(8):
                P.op("pe", lambda h, c=c, bank=bank, srct=srct, srctb=srctb: h.transpose(out=bank.bf16[:, c*128:(c+1)*128], in_=srct[:, c*128:(c+1)*128], identity=ident[:, :]),
                     reads=[srctb, identb], writes=[bank.buf] if c == 0 else (), wpart=[bank.buf] if c else ())
            P.op("dve", lambda h, bank=bank, which2=which2: h.tensor_copy(out=stg[which2][:, :], in_=bank.bf16[:, :]),
                 reads=[bank.buf], writes=[stgb[which2]])
            P.bank_free(bank)
            def fmdma(h, t=t, which2=which2, dram=dram):
                dst = dram.rearrange("(c p) n -> p c n", p=128)[:, :, t*128:(t+1)*128]
                srcv = stg[which2][:, :].rearrange("p (c j) -> p c j", c=8)
                return h.dma_start(out=dst, in_=srcv)
            P.dma("act", fmdma, reads=[stgb[which2]], stream=f"{name}fm")
    P.barrier()
    P.release(m0)



def emit_l0f_fin(P, C, io, name="l0ffin", eps=1e-5):
    m0 = P.mark()
    D = 1024
    nt = 32
    g2 = P.salloc(f"{name}_g", [128, D], F32); g2b = Buf("g2")
    P.dma("sp", lambda h: h.dma_start(out=g2[:, :], in_=io["g_rep"]), writes=[g2b], stream="consts")
    b2r = P.salloc(f"{name}_b", [128, D], F32); b2rb = Buf("b2")
    P.dma("sp", lambda h: h.dma_start(out=b2r[:, :], in_=io["b_rep"]), writes=[b2rb], stream="consts")
    a32 = P.salloc(f"{name}_a32", [128, D], F32); a32b = Buf("a32f")
    z32 = P.salloc(f"{name}_z32", [128, D], F32); z32b = Buf("z32f")
    sq32 = P.salloc(f"{name}_sq", [128, D], F32); sq32b = Buf("sq32f")
    stt = P.salloc(f"{name}_st", [128, 8], F32); sttb = Buf("sttf")
    ost = [P.salloc(f"{name}_o{i}", [128, D], F32) for i in range(2)]
    ostb = [Buf(f"{name}_o{i}") for i in range(2)]
    A2A_in = io["A2A_in"]; A2A_out = io["A2A_out"]
    ainb, aoutb = Buf("a2ain"), Buf("a2aout")
    for t in range(nt):
        P.dma("sp", lambda h, t=t: h.dma_start(out=a32[:, :], in_=io["A32"][t*128:(t+1)*128, :]), writes=[a32b], stream=f"{name}a")
        P.dma("sp", lambda h, t=t: h.dma_start(out=z32[:, :], in_=io["B32"][t*128:(t+1)*128, :]), writes=[z32b], stream=f"{name}b")
        P.op("dve", lambda h: h.tensor_add(out=z32[:, :], in0=z32[:, :], in1=a32[:, :]), reads=[a32b], wpart=[z32b])
        P.op("dve", lambda h: h.reduce_sum(out=stt[:, 0:1], in_=z32[:, :], axis=AX.X), reads=[z32b], writes=[sttb])
        P.op("dve", lambda h: h.tensor_scalar(out=stt[:, 1:2], in0=stt[:, 0:1], scalar1=1.0/D, scalar2=None, op0=ALU.mult),
             reads=[sttb], wpart=[sttb])
        P.op("dve", lambda h: h.tensor_scalar(out=sq32[:, :], in0=z32[:, :], scalar1=stt[:, 1:2], scalar2=None, op0=ALU.subtract),
             reads=[z32b, sttb], writes=[sq32b])
        P.op("act", lambda h: h.activation(out=z32[:, :], in_=sq32[:, :], func=AF.Square), reads=[sq32b], writes=[z32b])
        P.op("dve", lambda h: h.reduce_sum(out=stt[:, 2:3], in_=z32[:, :], axis=AX.X), reads=[z32b], wpart=[sttb])
        P.op("dve", lambda h: h.tensor_scalar(out=stt[:, 3:4], in0=stt[:, 2:3], scalar1=1.0/D, scalar2=eps, op0=ALU.mult, op1=ALU.add),
             reads=[sttb], wpart=[sttb])
        P.op("act", lambda h: h.activation(out=stt[:, 4:5], in_=stt[:, 3:4], func=AF.Sqrt), reads=[sttb], wpart=[sttb])
        P.op("dve", lambda h: h.reciprocal(out=stt[:, 5:6], in_=stt[:, 4:5]), reads=[sttb], wpart=[sttb])
        P.op("dve", lambda h: h.tensor_scalar(out=sq32[:, :], in0=sq32[:, :], scalar1=stt[:, 5:6], scalar2=None, op0=ALU.mult),
             reads=[sttb], wpart=[sq32b])
        P.op("dve", lambda h: h.tensor_mul(out=sq32[:, :], in0=sq32[:, :], in1=g2[:, :]), reads=[g2b], wpart=[sq32b])
        sl = t % 2
        P.op("dve", lambda h, sl=sl: h.tensor_add(out=ost[sl][:, :], in0=sq32[:, :], in1=b2r[:, :]), reads=[sq32b, b2rb], writes=[ostb[sl]])
        def scdma(h, t=t, sl=sl):
            R = 1024 * 4
            dst = A2A_in.rearrange("(sh sl2 b) n -> b sh sl2 n", sh=8, sl2=2)[t*8:(t+1)*8, :, :, :]
            return h.dma_start(out=dst, in_=ost[sl][:, :])
        P.dma("act", scdma, reads=[ostb[sl]], wpart=[ainb], stream=f"{name}sc")
    def ccop(h):
        from concourse.aot_env import allow_ici_all_to_all
        with allow_ici_all_to_all():
            return h.collective_compute("AllToAll", ALU.bypass, replica_groups=[list(range(8))],
                                        ins=[A2A_in.opt()], outs=[A2A_out.opt()])
    P.op("pool", ccop, reads=[ainb], writes=[aoutb])
    P.barrier()
    P.release(m0)
    return aoutb


def _part8(b):
    return np.ascontiguousarray(b.reshape(8, 128).T.astype(np.float32))

def _rep(b):
    return np.ascontiguousarray(np.broadcast_to(np.asarray(b, np.float32), (128, len(b))))

def _T(w):
    return np.ascontiguousarray(np.asarray(w, np.float32).T)


def prep_all(inp):
    g = {}
    g["L0aWqT"] = _T(inp["L0_attn_Wq"]); g["L0aWkT"] = _T(inp["L0_attn_Wk"])
    g["L0aWvT"] = _T(inp["L0_attn_Wv"]); g["L0aWoT"] = _T(inp["L0_attn_Wo"])
    g["L0abq"] = _part8(inp["L0_attn_bq"]); g["L0abk"] = _part8(inp["L0_attn_bk"])
    g["L0abv"] = _rep(inp["L0_attn_bv"]); g["L0abo"] = _rep(inp["L0_attn_bo"])
    ipw = np.asarray(inp["L0_ffn_in_proj_w"], np.float32)
    ipb = np.asarray(inp["L0_ffn_in_proj_b"], np.float32)
    g["L0fWqT"] = _T(ipw[0:1024]); g["L0fWkT"] = _T(ipw[1024:2048]); g["L0fWvT"] = _T(ipw[2048:3072])
    g["L0fbq"] = _part8(ipb[0:1024]); g["L0fbk"] = _part8(ipb[1024:2048]); g["L0fbv"] = _rep(ipb[2048:3072])
    g["L0fWoT"] = _T(inp["L0_ffn_out_w"]); g["L0fbo"] = _rep(inp["L0_ffn_out_b"])
    g["L0fg1"] = _rep(inp["L0_ffn_ln1_g"]); g["L0fb1"] = _rep(inp["L0_ffn_ln1_b"])
    g["L0fg2"] = _rep(inp["L0_ffn_ln2_g"]); g["L0fb2"] = _rep(inp["L0_ffn_ln2_b"])
    g["L0fgwT"] = _T(inp["L0_ffn_gate_w"])
    g["L0fgb"] = _rep(inp["L0_ffn_gate_b"])
    w1 = np.asarray(inp["L0_ffn_w1"], np.float32)
    w2 = np.asarray(inp["L0_ffn_w2"], np.float32)
    g["L0fW1T"] = np.ascontiguousarray(w1.transpose(0, 2, 1).reshape(8192, 4096))
    g["L0fW2T"] = np.ascontiguousarray(w2.transpose(0, 2, 1).reshape(32768, 1024))
    b1 = np.asarray(inp["L0_ffn_b1"], np.float32)
    g["L0fB1"] = np.ascontiguousarray(b1.reshape(8, 32, 128).transpose(0, 2, 1).reshape(1024, 32))
    g["L0fB2"] = np.ascontiguousarray(np.asarray(inp["L0_ffn_b2"], np.float32))
    MQ = np.zeros((16, 128), np.float32); MK = np.zeros((16, 128), np.float32)
    MQ[0] = 1.0; MK[0] = -1000.0
    for j in range(8):
        MQ[1+j] = (np.arange(128)//16 == j).astype(np.float32)
        MK[1+j] = 1000.0*(np.arange(128)//16 == j)
    g["L0fMQ"] = MQ; g["L0fMK"] = MK
    wqkv = np.asarray(inp["L1_attn_w_qkv"], np.float32)
    g["L1aWqT"] = np.ascontiguousarray(wqkv[:, 0:1024])
    g["L1aWkT"] = np.ascontiguousarray(wqkv[:, 1024:2048])
    g["L1aWvT"] = np.ascontiguousarray(wqkv[:, 2048:3072])
    tp = prep_tree(np.asarray(inp["L1_ffn_Wq"], np.float32), np.asarray(inp["L1_ffn_routers"], np.float32),
                   np.asarray(inp["L1_ffn_leaf_values"], np.float32), np.asarray(inp["L1_ffn_Wout_w"], np.float32))
    g["L1fWqT"] = tp["WqT"]; g["L1fRoutT"] = tp["RoutT"]; g["L1fLV"] = tp["LV"]; g["L1fWoutT"] = tp["WoutT"]
    g["L1fbo"] = _rep(inp["L1_ffn_Wout_b"])
    g["L2aWq"] = np.ascontiguousarray(np.asarray(inp["L2_attn_wq"], np.float32))
    g["L2aWk"] = np.ascontiguousarray(np.asarray(inp["L2_attn_wk"], np.float32))
    g["L2aWv"] = np.ascontiguousarray(np.asarray(inp["L2_attn_wv"], np.float32))
    g["L2aWo"] = np.ascontiguousarray(np.asarray(inp["L2_attn_wo"], np.float32))
    g["L2abo"] = _rep(inp["L2_attn_bo"])
    proj = np.asarray(inp["L2_attn_proj"], np.float32)
    pp = np.zeros((128, 256), np.float32); pp[0:64] = proj.T; pp[64] = 1.0; pp[65] = 1.0
    ppB = np.zeros((128, 256), np.float32); ppB[64:128] = proj.T
    g["L2aPP"] = pp; g["L2aPPB"] = ppB
    g["L2fWg"] = np.ascontiguousarray(np.asarray(inp["L2_ffn_Wg"], np.float32))
    g["L2fW1"] = np.ascontiguousarray(np.asarray(inp["L2_ffn_W1"], np.float32).reshape(8192, 2048))
    g["L2fW2"] = np.ascontiguousarray(np.asarray(inp["L2_ffn_W2"], np.float32).reshape(8192, 2048))
    g["L2fW3"] = np.ascontiguousarray(np.asarray(inp["L2_ffn_W3"], np.float32).reshape(16384, 1024))
    wq3 = np.asarray(inp["L3_attn_W_qkv"], np.float32)
    bq3 = np.asarray(inp["L3_attn_b_qkv"], np.float32)
    g["L3aWqT"] = np.ascontiguousarray(wq3[:, 0:1024]); g["L3aWkT"] = np.ascontiguousarray(wq3[:, 1024:2048])
    g["L3aWvT"] = np.ascontiguousarray(wq3[:, 2048:3072])
    g["L3abq"] = _part8(bq3[0:1024]); g["L3abk"] = _part8(bq3[1024:2048]); g["L3abv"] = _rep(bq3[2048:3072])
    g["L3aWoT"] = np.ascontiguousarray(np.asarray(inp["L3_attn_W_out"], np.float32))
    g["L3abo"] = _rep(inp["L3_attn_b_out"])
    g["L3fWqT"] = _T(inp["L3_ffn_wq"]); g["L3fWkT"] = _T(inp["L3_ffn_wk"])
    g["L3fWvT"] = _T(inp["L3_ffn_wv"]); g["L3fWoT"] = _T(inp["L3_ffn_wo"])
    g["L3fanw"] = _rep(inp["L3_ffn_attn_norm_w"]); g["L3fmnw"] = _rep(inp["L3_ffn_moe_norm_w"])
    g["L3fgwT"] = _T(inp["L3_ffn_gate_w"])
    w13 = np.asarray(inp["L3_ffn_w1"], np.float32)
    w23 = np.asarray(inp["L3_ffn_w2"], np.float32)
    g["L3fW1T"] = np.ascontiguousarray(w13.transpose(0, 2, 1).reshape(8192, 4096))
    g["L3fW2T"] = np.ascontiguousarray(w23.transpose(0, 2, 1).reshape(32768, 1024))
    b13 = np.asarray(inp["L3_ffn_b1"], np.float32)
    g["L3fB1"] = np.ascontiguousarray(b13.reshape(8, 32, 128).transpose(0, 2, 1).reshape(1024, 32))
    g["L3fB2"] = np.ascontiguousarray(np.asarray(inp["L3_ffn_b2"], np.float32))
    hd = 64
    inv_freq = 1.0 / (10000.0 ** (np.arange(0, hd, 2) / hd))
    freqs = np.arange(2048)[:, None] * inv_freq[None, :]
    emb = np.concatenate([freqs, freqs], axis=1)
    g["L3fCos"] = np.ascontiguousarray(np.tile(np.cos(emb), (1, 16))).astype(BF16NP)
    g["L3fSin"] = np.ascontiguousarray(np.tile(np.sin(emb), (1, 16))).astype(BF16NP)
    return g


def dram(nc, name, shape, dtype, kind):
    return nc.dram_tensor(name, list(shape), dtype, kind=kind).ap()


def build_layer0(nc, ext_io="both"):
    kin = "ExternalInput"
    kout = "ExternalOutput"
    kint = "Internal"
    P = Prog(nc)
    C = Consts(nc)
    C.ident_sb(P)
    Xown = dram(nc, "Xown", [4096, 1024], F32, kin)
    Xpos = dram(nc, "Xpos", [4096, 1024], F32, kin)
    X1 = dram(nc, "X1", [4096, 1024], F32, kout)
    w = {}
    for nm, sh in [("L0aWqT",[1024,1024]),("L0aWkT",[1024,1024]),("L0aWvT",[1024,1024]),("L0aWoT",[1024,1024]),
                   ("L0abq",[128,8]),("L0abk",[128,8]),("L0abv",[128,1024]),("L0abo",[128,1024]),
                   ("L0fWqT",[1024,1024]),("L0fWkT",[1024,1024]),("L0fWvT",[1024,1024]),
                   ("L0fbq",[128,8]),("L0fbk",[128,8]),("L0fbv",[128,1024]),
                   ("L0fWoT",[1024,1024]),("L0fbo",[128,1024]),
                   ("L0fg1",[128,1024]),("L0fb1",[128,1024]),("L0fg2",[128,1024]),("L0fb2",[128,1024]),
                   ("L0fgwT",[1024,8]),("L0fgb",[128,8]),
                   ("L0fW1T",[8192,4096]),("L0fW2T",[32768,1024]),("L0fB1",[1024,32]),("L0fB2",[8,1024]),
                   ("L0fMQ",[16,128]),("L0fMK",[16,128])]:
        w[nm] = dram(nc, nm, sh, F32, kin)
    XNfm = dram(nc, "XNfm", [1024, 4096], BF16, kint)
    A0 = dram(nc, "A0", [4096, 1024], F32, kint)
    XNp_fm = dram(nc, "XNp_fm", [1024, 4096], BF16, kint)
    XNp_32 = dram(nc, "XNp_32", [4096, 1024], F32, kint)
    X1fm = dram(nc, "X1fm", [1024, 4096], BF16, kint)
    X1lo = dram(nc, "X1lo", [1024, 4096], BF16, kint)
    X132 = dram(nc, "X132", [4096, 1024], F32, kint)
    MOE0 = dram(nc, "MOE0", [4096, 1024], F32, kint)
    A2Ain = dram(nc, "A2Ain", [4096, 1024], F32, kint)
    A2Aout = dram(nc, "A2Aout", [4096, 1024], F32, kint)

    emit_ln(P, C, Xown, None, XNfm, 4096, name="ln0")
    emit_attn(P, C, {"XN_fm": XNfm, "OUT": A0, "WqT": w["L0aWqT"], "WkT": w["L0aWkT"], "WvT": w["L0aWvT"],
                     "WoT": w["L0aWoT"], "bq": w["L0abq"], "bk": w["L0abk"], "bv_rep": w["L0abv"], "bo_rep": w["L0abo"]},
              nheads=16, hd=64, causal=True, scale=0.125, name="l0a")
    emit_ln(P, C, Xpos, None, XNp_fm, 4096, name="lnp", XN_tm32=XNp_32)
    for half in range(2):
        hof = half * 2048
        io2 = {"XNp_fm": XNp_fm[:, hof:hof+2048], "XNp_32": XNp_32[hof:hof+2048, :],
               "WqT": w["L0fWqT"], "WkT": w["L0fWkT"], "WvT": w["L0fWvT"], "WoT": w["L0fWoT"],
               "bq": w["L0fbq"], "bk": w["L0fbk"], "bv_rep": w["L0fbv"], "bo_rep": w["L0fbo"],
               "g1_rep": w["L0fg1"], "b1_rep": w["L0fb1"], "MaskQ": w["L0fMQ"], "MaskK": w["L0fMK"],
               "X1_fm": X1fm[:, hof:hof+2048], "X1_lo_fm": X1lo[:, hof:hof+2048], "X1_32": X132[hof:hof+2048, :]}
        emit_l0f_mid(P, C, io2, ntok=2048, name=f"l0f{half}")
    emit_moe(P, C, {"X_fm": X1fm, "X_lo_fm": X1lo, "OUT": MOE0, "gate_wT": w["L0fgwT"], "gate_b_rep": w["L0fgb"],
                    "W1T": w["L0fW1T"], "W2T": w["L0fW2T"], "B1": w["L0fB1"], "B2": w["L0fB2"]},
             4096, func=AF.Relu, name="moe0")
    emit_l0f_fin(P, C, {"A32": MOE0, "B32": X132, "g_rep": w["L0fg2"], "b_rep": w["L0fb2"],
                        "A2A_in": A2Ain, "A2A_out": A2Aout}, name="l0fin")

    def src_x(t):
        return Xown[t*128:(t+1)*128, :]
    def src_a(t):
        return A0[t*128:(t+1)*128, :]
    def src_f(t):
        s_loc = t // 16
        pi = t % 16
        c = pi // 2
        b0 = (pi % 2) * 128
        r0 = c*512 + s_loc*256 + b0
        return A2Aout[r0:r0+128, :]
    emit_add(P, C, X1, [src_x, src_a, src_f], 4096, name="add0")
    P.finalize()
    return P, C


def build_layer1(nc):
    kin, kout, kint = "ExternalInput", "ExternalOutput", "Internal"
    P = Prog(nc); C = Consts(nc); C.ident_sb(P)
    X1 = dram(nc, "X1", [4096, 1024], F32, kin)
    X2 = dram(nc, "X2", [4096, 1024], F32, kout)
    w = {}
    for nm, sh in [("L1aWqT",[1024,1024]),("L1aWkT",[1024,1024]),("L1aWvT",[1024,1024]),
                   ("L1fWqT",[1024,1024]),("L1fRoutT",[1024,256]),("L1fLV",[256,1024]),
                   ("L1fWoutT",[1024,1024]),("L1fbo",[128,1024])]:
        w[nm] = dram(nc, nm, sh, F32, kin)
    XNfm = dram(nc, "XNfm", [1024, 4096], BF16, kint)
    A1 = dram(nc, "A1", [4096, 1024], F32, kint)
    F1 = dram(nc, "F1", [4096, 1024], F32, kint)
    emit_ln(P, C, X1, None, XNfm, 4096, name="ln1")
    emit_attn(P, C, {"XN_fm": XNfm, "OUT": A1, "WqT": w["L1aWqT"], "WkT": w["L1aWkT"], "WvT": w["L1aWvT"]},
              nheads=16, hd=64, causal=False, scale=1.0/32, name="l1a")
    emit_tree(P, C, {"XN_fm": XNfm, "OUT": F1, "WqT": w["L1fWqT"], "RoutT": w["L1fRoutT"], "LV": w["L1fLV"],
                     "WoutT": w["L1fWoutT"], "bo_rep": w["L1fbo"]}, 4096, name="l1f")
    emit_add(P, C, X2, [lambda t: X1[t*128:(t+1)*128, :], lambda t: A1[t*128:(t+1)*128, :], lambda t: F1[t*128:(t+1)*128, :]], 4096, name="add1")
    P.finalize()
    return P, C


def build_layer2(nc):
    kin, kout, kint = "ExternalInput", "ExternalOutput", "Internal"
    P = Prog(nc); C = Consts(nc); C.ident_sb(P)
    X2 = dram(nc, "X2", [4096, 1024], F32, kin)
    X3 = dram(nc, "X3", [4096, 1024], F32, kout)
    w = {}
    for nm, sh in [("L2aWq",[1024,1024]),("L2aWk",[1024,1024]),("L2aWv",[1024,1024]),("L2aWo",[1024,1024]),
                   ("L2abo",[128,1024]),("L2aPP",[128,256]),("L2aPPB",[128,256]),
                   ("L2fWg",[1024,8]),("L2fW1",[8192,2048]),("L2fW2",[8192,2048]),("L2fW3",[16384,1024])]:
        w[nm] = dram(nc, nm, sh, F32, kin)
    XNfm = dram(nc, "XNfm", [1024, 4096], BF16, kint)
    XNlo = dram(nc, "XNlo", [1024, 4096], BF16, kint)
    A2 = dram(nc, "A2", [4096, 1024], F32, kint)
    F2 = dram(nc, "F2", [4096, 1024], F32, kint)
    emit_ln(P, C, X2, None, XNfm, 4096, name="ln2", XN_lo_fm=XNlo)
    emit_perf(P, C, {"XN_fm": XNfm, "OUT": A2, "Wq": w["L2aWq"], "Wk": w["L2aWk"], "Wv": w["L2aWv"], "Wo": w["L2aWo"],
                     "bo_rep": w["L2abo"], "ProjPadT": w["L2aPP"], "ProjPadB": w["L2aPPB"]}, name="l2a")
    emit_glu(P, C, {"X_fm": XNfm, "X_lo_fm": XNlo, "OUT": F2, "Wg": w["L2fWg"],
                    "W1": w["L2fW1"], "W2": w["L2fW2"], "W3": w["L2fW3"]}, 4096, name="l2f")
    emit_add(P, C, X3, [lambda t: X2[t*128:(t+1)*128, :], lambda t: A2[t*128:(t+1)*128, :], lambda t: F2[t*128:(t+1)*128, :]], 4096, name="add2")
    P.finalize()
    return P, C


def build_layer3(nc):
    kin, kout, kint = "ExternalInput", "ExternalOutput", "Internal"
    P = Prog(nc); C = Consts(nc); C.ident_sb(P)
    X3 = dram(nc, "X3", [4096, 1024], F32, kin)
    X4 = dram(nc, "X4", [4096, 1024], F32, kout)
    w = {}
    for nm, sh in [("L3aWqT",[1024,1024]),("L3aWkT",[1024,1024]),("L3aWvT",[1024,1024]),("L3aWoT",[1024,1024]),
                   ("L3abq",[128,8]),("L3abk",[128,8]),("L3abv",[128,1024]),("L3abo",[128,1024]),
                   ("L3fWqT",[1024,1024]),("L3fWkT",[1024,1024]),("L3fWvT",[1024,1024]),("L3fWoT",[1024,1024]),
                   ("L3fanw",[128,1024]),("L3fmnw",[128,1024]),("L3fgwT",[1024,8]),
                   ("L3fW1T",[8192,4096]),("L3fW2T",[32768,1024]),("L3fB1",[1024,32]),("L3fB2",[8,1024])]:
        w[nm] = dram(nc, nm, sh, F32, kin)
    for nm in ["L3fCos", "L3fSin"]:
        w[nm] = dram(nc, nm, [2048, 1024], BF16, kin)
    XNfm = dram(nc, "XNfm", [1024, 4096], BF16, kint)
    XN32 = dram(nc, "XN32", [4096, 1024], F32, kint)
    A3 = dram(nc, "A3", [4096, 1024], F32, kint)
    H32 = dram(nc, "H32", [4096, 1024], F32, kint)
    TN2 = dram(nc, "TN2", [1024, 4096], BF16, kint)
    TN2L = dram(nc, "TN2L", [1024, 4096], BF16, kint)
    MOE3 = dram(nc, "MOE3", [4096, 1024], F32, kint)
    emit_ln(P, C, X3, None, XNfm, 4096, name="ln3", XN_tm32=XN32)
    emit_attn(P, C, {"XN_fm": XNfm, "OUT": A3, "WqT": w["L3aWqT"], "WkT": w["L3aWkT"], "WvT": w["L3aWvT"],
                     "WoT": w["L3aWoT"], "bq": w["L3abq"], "bk": w["L3abk"], "bv_rep": w["L3abv"], "bo_rep": w["L3abo"]},
              nheads=1, hd=1024, causal=False, scale=1.0/32, name="l3a")
    emit_l3f_attn(P, C, {"XN32": XN32, "H32": H32, "TN2_fm": TN2, "TN2_lo_fm": TN2L,
                         "WqT": w["L3fWqT"], "WkT": w["L3fWkT"], "WvT": w["L3fWvT"], "WoT": w["L3fWoT"],
                         "anw_rep": w["L3fanw"], "mnw_rep": w["L3fmnw"], "CosT": w["L3fCos"], "SinT": w["L3fSin"]},
                  name="l3f")
    emit_moe(P, C, {"X_fm": TN2, "X_lo_fm": TN2L, "OUT": MOE3, "gate_wT": w["L3fgwT"], "gate_b_rep": None,
                    "W1T": w["L3fW1T"], "W2T": w["L3fW2T"], "B1": w["L3fB1"], "B2": w["L3fB2"]},
             4096, func=AF.Gelu, name="moe3")
    emit_add(P, C, X4, [lambda t: X3[t*128:(t+1)*128, :], lambda t: A3[t*128:(t+1)*128, :],
                        lambda t: H32[t*128:(t+1)*128, :], lambda t: MOE3[t*128:(t+1)*128, :]], 4096, name="add3")
    P.finalize()
    return P, C

def build_full(nc):
    kin, kout, kint = "ExternalInput", "ExternalOutput", "Internal"
    P = Prog(nc)
    C = Consts(nc)
    C.ident_sb(P)
    Xown = dram(nc, "Xown", [4096, 1024], F32, kin)
    Xpos = dram(nc, "Xpos", [4096, 1024], F32, kin)
    X4 = dram(nc, "X4", [4096, 1024], F32, kout)
    w = {}
    WSHAPES = [
        ("L0aWqT",[1024,1024]),("L0aWkT",[1024,1024]),("L0aWvT",[1024,1024]),("L0aWoT",[1024,1024]),
        ("L0abq",[128,8]),("L0abk",[128,8]),("L0abv",[128,1024]),("L0abo",[128,1024]),
        ("L0fWqT",[1024,1024]),("L0fWkT",[1024,1024]),("L0fWvT",[1024,1024]),
        ("L0fbq",[128,8]),("L0fbk",[128,8]),("L0fbv",[128,1024]),
        ("L0fWoT",[1024,1024]),("L0fbo",[128,1024]),
        ("L0fg1",[128,1024]),("L0fb1",[128,1024]),("L0fg2",[128,1024]),("L0fb2",[128,1024]),
        ("L0fgwT",[1024,8]),("L0fgb",[128,8]),
        ("L0fW1T",[8192,4096]),("L0fW2T",[32768,1024]),("L0fB1",[1024,32]),("L0fB2",[8,1024]),
        ("L0fMQ",[16,128]),("L0fMK",[16,128]),
        ("L1aWqT",[1024,1024]),("L1aWkT",[1024,1024]),("L1aWvT",[1024,1024]),
        ("L1fWqT",[1024,1024]),("L1fRoutT",[1024,256]),("L1fLV",[256,1024]),
        ("L1fWoutT",[1024,1024]),("L1fbo",[128,1024]),
        ("L2aWq",[1024,1024]),("L2aWk",[1024,1024]),("L2aWv",[1024,1024]),("L2aWo",[1024,1024]),
        ("L2abo",[128,1024]),("L2aPP",[128,256]),("L2aPPB",[128,256]),
        ("L2fWg",[1024,8]),("L2fW1",[8192,2048]),("L2fW2",[8192,2048]),("L2fW3",[16384,1024]),
        ("L3aWqT",[1024,1024]),("L3aWkT",[1024,1024]),("L3aWvT",[1024,1024]),("L3aWoT",[1024,1024]),
        ("L3abq",[128,8]),("L3abk",[128,8]),("L3abv",[128,1024]),("L3abo",[128,1024]),
        ("L3fWqT",[1024,1024]),("L3fWkT",[1024,1024]),("L3fWvT",[1024,1024]),("L3fWoT",[1024,1024]),
        ("L3fanw",[128,1024]),("L3fmnw",[128,1024]),("L3fgwT",[1024,8]),
        ("L3fW1T",[8192,4096]),("L3fW2T",[32768,1024]),("L3fB1",[1024,32]),("L3fB2",[8,1024]),
    ]
    for nm, sh in WSHAPES:
        w[nm] = dram(nc, nm, sh, F32, kin)
    for nm in ["L3fCos", "L3fSin"]:
        w[nm] = dram(nc, nm, [2048, 1024], BF16, kin)
    X1 = dram(nc, "X1", [4096, 1024], F32, kint)
    X2 = dram(nc, "X2", [4096, 1024], F32, kint)
    X3 = dram(nc, "X3", [4096, 1024], F32, kint)
    XNfm = dram(nc, "XNfm", [1024, 4096], BF16, kint)
    XNlo = dram(nc, "XNlo", [1024, 4096], BF16, kint)
    XN32 = dram(nc, "XN32", [4096, 1024], F32, kint)
    Abr = dram(nc, "Abr", [4096, 1024], F32, kint)
    Fbr = dram(nc, "Fbr", [4096, 1024], F32, kint)
    XNp_fm = dram(nc, "XNp_fm", [1024, 4096], BF16, kint)
    XNp_32 = dram(nc, "XNp_32", [4096, 1024], F32, kint)
    X1fm = dram(nc, "X1fm", [1024, 4096], BF16, kint)
    X1lo = dram(nc, "X1lo", [1024, 4096], BF16, kint)
    X132 = dram(nc, "X132", [4096, 1024], F32, kint)
    MOE = dram(nc, "MOE", [4096, 1024], F32, kint)
    A2Ain = dram(nc, "A2Ain", [4096, 1024], F32, kint)
    A2Aout = dram(nc, "A2Aout", [4096, 1024], F32, kint)
    H32 = dram(nc, "H32", [4096, 1024], F32, kint)
    TN2 = dram(nc, "TN2", [1024, 4096], BF16, kint)
    TN2L = dram(nc, "TN2L", [1024, 4096], BF16, kint)

    emit_ln(P, C, Xown, None, XNfm, 4096, name="ln0")
    emit_attn(P, C, {"XN_fm": XNfm, "OUT": Abr, "WqT": w["L0aWqT"], "WkT": w["L0aWkT"], "WvT": w["L0aWvT"],
                     "WoT": w["L0aWoT"], "bq": w["L0abq"], "bk": w["L0abk"], "bv_rep": w["L0abv"], "bo_rep": w["L0abo"]},
              nheads=16, hd=64, causal=True, scale=0.125, name="l0a")
    emit_ln(P, C, Xpos, None, XNp_fm, 4096, name="lnp", XN_tm32=XNp_32)
    for half in range(2):
        hof = half * 2048
        io2 = {"XNp_fm": XNp_fm[:, hof:hof+2048], "XNp_32": XNp_32[hof:hof+2048, :],
               "WqT": w["L0fWqT"], "WkT": w["L0fWkT"], "WvT": w["L0fWvT"], "WoT": w["L0fWoT"],
               "bq": w["L0fbq"], "bk": w["L0fbk"], "bv_rep": w["L0fbv"], "bo_rep": w["L0fbo"],
               "g1_rep": w["L0fg1"], "b1_rep": w["L0fb1"], "MaskQ": w["L0fMQ"], "MaskK": w["L0fMK"],
               "X1_fm": X1fm[:, hof:hof+2048], "X1_lo_fm": X1lo[:, hof:hof+2048], "X1_32": X132[hof:hof+2048, :]}
        emit_l0f_mid(P, C, io2, ntok=2048, name=f"l0f{half}")
    emit_moe(P, C, {"X_fm": X1fm, "X_lo_fm": X1lo, "OUT": MOE, "gate_wT": w["L0fgwT"], "gate_b_rep": w["L0fgb"],
                    "W1T": w["L0fW1T"], "W2T": w["L0fW2T"], "B1": w["L0fB1"], "B2": w["L0fB2"]},
             4096, func=AF.Relu, name="moe0")
    def src_x(t):
        return Xown[t*128:(t+1)*128, :]
    def src_a(t):
        return Abr[t*128:(t+1)*128, :]
    def src_f(t):
        s_loc = t // 16
        pi = t % 16
        c = pi // 2
        b0 = (pi % 2) * 128
        r0 = c*512 + s_loc*256 + b0
        return A2Aout[r0:r0+128, :]
    m_add = emit_add(P, C, X2, [src_x, src_a], 4096, name="add0a", barrier=False, release=False)
    emit_l0f_fin(P, C, {"A32": MOE, "B32": X132, "g_rep": w["L0fg2"], "b_rep": w["L0fb2"],
                        "A2A_in": A2Ain, "A2A_out": A2Aout}, name="l0fin")
    P.release(m_add)
    emit_add(P, C, X1, [lambda t: X2[t*128:(t+1)*128, :], src_f], 4096, name="add0b")
    P.new_epoch()
    emit_ln(P, C, X1, None, XNfm, 4096, name="ln1")
    emit_attn(P, C, {"XN_fm": XNfm, "OUT": Abr, "WqT": w["L1aWqT"], "WkT": w["L1aWkT"], "WvT": w["L1aWvT"]},
              nheads=16, hd=64, causal=False, scale=1.0/32, name="l1a")
    emit_tree(P, C, {"XN_fm": XNfm, "OUT": Fbr, "WqT": w["L1fWqT"], "RoutT": w["L1fRoutT"], "LV": w["L1fLV"],
                     "WoutT": w["L1fWoutT"], "bo_rep": w["L1fbo"]}, 4096, name="l1f")
    emit_add(P, C, X2, [lambda t: X1[t*128:(t+1)*128, :], lambda t: Abr[t*128:(t+1)*128, :], lambda t: Fbr[t*128:(t+1)*128, :]], 4096, name="add1")
    P.new_epoch()
    emit_ln(P, C, X2, None, XNfm, 4096, name="ln2", XN_lo_fm=XNlo)
    emit_perf(P, C, {"XN_fm": XNfm, "OUT": Abr, "Wq": w["L2aWq"], "Wk": w["L2aWk"], "Wv": w["L2aWv"], "Wo": w["L2aWo"],
                     "bo_rep": w["L2abo"], "ProjPadT": w["L2aPP"], "ProjPadB": w["L2aPPB"]}, name="l2a")
    emit_glu(P, C, {"X_fm": XNfm, "X_lo_fm": XNlo, "OUT": Fbr, "Wg": w["L2fWg"],
                    "W1": w["L2fW1"], "W2": w["L2fW2"], "W3": w["L2fW3"]}, 4096, name="l2f")
    emit_add(P, C, X3, [lambda t: X2[t*128:(t+1)*128, :], lambda t: Abr[t*128:(t+1)*128, :], lambda t: Fbr[t*128:(t+1)*128, :]], 4096, name="add2")
    P.new_epoch()
    emit_ln(P, C, X3, None, XNfm, 4096, name="ln3", XN_tm32=XN32)
    emit_attn(P, C, {"XN_fm": XNfm, "OUT": Abr, "WqT": w["L3aWqT"], "WkT": w["L3aWkT"], "WvT": w["L3aWvT"],
                     "WoT": w["L3aWoT"], "bq": w["L3abq"], "bk": w["L3abk"], "bv_rep": w["L3abv"], "bo_rep": w["L3abo"]},
              nheads=1, hd=1024, causal=False, scale=1.0/32, name="l3a")
    emit_l3f_attn(P, C, {"XN32": XN32, "H32": H32, "TN2_fm": TN2, "TN2_lo_fm": TN2L,
                         "WqT": w["L3fWqT"], "WkT": w["L3fWkT"], "WvT": w["L3fWvT"], "WoT": w["L3fWoT"],
                         "anw_rep": w["L3fanw"], "mnw_rep": w["L3fmnw"], "CosT": w["L3fCos"], "SinT": w["L3fSin"]},
                  name="l3f")
    emit_moe(P, C, {"X_fm": TN2, "X_lo_fm": TN2L, "OUT": MOE, "gate_wT": w["L3fgwT"], "gate_b_rep": None,
                    "W1T": w["L3fW1T"], "W2T": w["L3fW2T"], "B1": w["L3fB1"], "B2": w["L3fB2"]},
             4096, func=AF.Gelu, name="moe3")
    emit_add(P, C, X4, [lambda t: X3[t*128:(t+1)*128, :], lambda t: Abr[t*128:(t+1)*128, :],
                        lambda t: H32[t*128:(t+1)*128, :], lambda t: MOE[t*128:(t+1)*128, :]], 4096, name="add3")
    P.finalize()
    return P, C, [nm for nm, _ in WSHAPES] + ["L3fCos", "L3fSin"]

import os as _os
_os.environ.setdefault("TRN_ALLOW_ICI_ALL_TO_ALL", "1")
from concourse.bass_utils import run_bass_kernel_spmd


def kernel(**inputs):
    patch_fast_compile()
    x = np.asarray(inputs["L0_attn_x"], np.float32)
    g = prep_all(inputs)
    xown = [np.ascontiguousarray(x[2*c:2*c+2].reshape(4096, 1024)) for c in range(8)]
    xpos = [np.ascontiguousarray(x[:, 256*c:256*(c+1), :].transpose(1, 0, 2).reshape(4096, 1024)) for c in range(8)]
    nc = bass.Bass("TRN2", target_bir_lowering=False)
    P, C, names = build_full(nc)
    maps = []
    for c in range(8):
        m = {nm: g[nm] for nm in names}
        m["Xown"] = xown[c]
        m["Xpos"] = xpos[c]
        m.update(C.arrs)
        maps.append(m)
    res = run_bass_kernel_spmd(nc, maps, core_ids=list(range(8)))
    out = np.empty((16, 2048, 1024), np.float32)
    for c in range(8):
        out[2*c:2*c+2] = res.results[c]["X4"].reshape(2, 2048, 1024)
    return out
